# version 02
# baseline (speedup 1.0000x reference)
import numpy as np
import ml_dtypes
from contextlib import ExitStack, contextmanager

import concourse.bass as bass
import concourse.mybir as mybir
from concourse.bass import IndirectOffsetOnAxis
from concourse.bass_utils import run_bass_kernel_spmd
from concourse.alu_op_type import AluOpType as OP

F32 = mybir.dt.float32
BF16 = mybir.dt.bfloat16
I32 = mybir.dt.int32
U32 = mybir.dt.uint32
AF = mybir.ActivationFunctionType
AX = mybir.AxisListType.X

B = 8
T = 2048
C = 1024
NH = 16
HD = 64
E = 64
CAP = 1024
IDIM = 512
NCH = C // 128
NT = T // 128
EPS_LN = 1e-5
EPS_RMS = 1e-6
BF = ml_dtypes.bfloat16



class _PrefNC:

    def __init__(self, nc, pref):
        object.__setattr__(self, "_nc", nc)
        object.__setattr__(self, "_pref", pref)

    def sbuf_tensor(self, name, *a, **k):
        return self._nc.sbuf_tensor(self._pref + name, *a, **k)

    def psum_tensor(self, name, *a, **k):
        return self._nc.psum_tensor(self._pref + name, *a, **k)

    def __getattr__(self, n):
        return getattr(self._nc, n)


@contextmanager
def _passthrough(x):
    yield x


class Sems:
    def __init__(self, stack, nc, names):
        self.h = {}
        self.n = {}
        for nm in names:
            self.h[nm] = stack.enter_context(nc.semaphore(nm))
            self.n[nm] = 0

    def inc(self, instr, nm, k=1):
        instr.then_inc(self.h[nm], k)
        self.n[nm] += k
        return instr

    def dma(self, instr, nm="ld"):
        return self.inc(instr, nm, 16)

    def pre(self, eng, nm):
        eng.wait_ge(self.h[nm], self.n[nm])

    def wait(self, eng, nm, val=None):
        eng.wait_ge(self.h[nm], self.n[nm] if val is None else val)


_CHAINED_OPS = {
    "tensor_tensor", "tensor_scalar", "tensor_copy", "tensor_add", "tensor_sub",
    "tensor_mul", "tensor_max", "tensor_relu", "tensor_scalar_add", "tensor_scalar_mul",
    "tensor_scalar_max", "tensor_scalar_min", "tensor_scalar_sub", "reduce_sum",
    "reduce_max", "tensor_reduce", "scalar_tensor_tensor", "max", "max_index",
    "select", "copy_predicated", "match_replace", "memset", "iota", "affine_select",
    "activation", "reciprocal",
}


class _Chain:

    def __init__(self, eng, S, nm):
        self._eng = eng
        self._S = S
        self._nm = nm

    def __getattr__(self, name):
        f = getattr(self._eng, name)
        if name in _CHAINED_OPS:
            def g(*a, **k):
                self._eng.wait_ge(self._S.h[self._nm], self._S.n[self._nm])
                r = f(*a, **k)
                self._S.inc(r, self._nm)
                return r
            return g
        return f


def n512a(lo, hi):
    out = []
    x = lo
    while x < hi:
        nxt = min(hi, (x // 512 + 1) * 512)
        out.append((x, nxt))
        x = nxt
    return out


def n512(lo, hi):
    out = []
    x = lo
    while x < hi:
        out.append((x, min(x + 512, hi)))
        x = min(x + 512, hi)
    return out


def emit_stage1(nc, blk, S, io, upto="all"):
    xT = io["xT"]
    xtok = io["xtok"]
    w_attn = io["w_attn"]
    w_attnT = io["w_attnT"]
    b_attn = io["b_attn"]
    w_proj = io["w_proj"]
    b_proj = io["b_proj"]
    bproj_col = io["bproj_col"]
    wk_cT = io["wk_cT"]
    wk_e = io["wk_e"]
    wv_cT = io["wv_cT"]
    wv_e = io["wv_e"]
    rms_w = io["rms_w"]
    rms_w_col = io["rms_w_col"]
    battn_col = io["battn_col"]
    gate_wT = io["gate_wT"]
    sw_gate = io["sw_gate"]
    sw_up = io["sw_up"]
    sw_down = io["sw_down"]
    partial = io["partial"]
    xm_out = io["xm_out"]
    meta = io["meta"]
    counts = io["counts"]
    st = ExitStack()
    with st:
        sbt = lambda *a: st.enter_context(nc.sbuf_tensor(*a))

        xnT = sbt("xnT", [128, NCH, T], BF16)
        vtok = sbt("vtok", [128, NT, NH * 65], BF16)
        attnTn = sbt("attnTn", [128, NCH, T], BF16)
        khat = sbt("khat", [128, NCH, NH * HD], BF16)
        ones_col = sbt("ones_col", [1, 128], F32)
        ones_r64 = sbt("ones_r64", [128, 128], F32)
        onesc128f = sbt("onesc128f", [128, 1], F32)
        onesc128b = sbt("onesc128b", [128, 1], BF16)
        ones16 = sbt("ones16", [16, 1], F32)
        sel2 = sbt("sel2", [2, 256], F32)
        hsel = sbt("hsel", [128, 2], F32)
        iden = sbt("iden", [128, 128], F32)
        iota64 = sbt("iota64", [128, E], F32)
        triU = sbt("triU", [128, 128], BF16)
        triU16 = sbt("triU16", [16, 16], F32)
        triD = sbt("triD", [128, 128], BF16)
        rmsw_col = sbt("rmsw_col", [128, NCH], F32)
        bq_col = sbt("bq_col", [128, NCH], F32)
        bkv = sbt("bkv", [128, 16], F32)
        bkh_col = sbt("bkh_col", [128, NCH], F32)
        m_cols = sbt("m_cols", [128, NT], F32)
        rstd_cols = sbt("rstd_cols", [128, NT], F32)
        g_cols = sbt("g_cols", [128, NT], F32)
        cg_cols = sbt("cg_cols", [128, NT], F32)
        meta_t = sbt("meta_t", [128, NT, 6], F32)
        small = sbt("small", [128, 64], F32)

        with _passthrough(blk) as blk:
            def _fin():
                @blk.gpsimd
                def _(g):
                    for nm in ("ld", "gp", "st", "lt0", "lt1", "lc0", "sx0", "sx1",
                               "sp0", "sp1", "ga0", "ga1"):
                        S.wait(g, nm)
                    for nm in ("pe", "dv", "ac", "gc", "dvx", "acx", "gcx"):
                        S.wait(g, nm)
                    g.nop()
            @blk.gpsimd
            def _(g):
                g = _Chain(g, S, "gcx")
                def ch(instr):
                    return instr
                ch(g.memset(ones_col[:, :], 1.0))
                ch(g.memset(ones_r64[:, :], 0.0))
                ch(g.memset(ones_r64[64:65, :], 1.0))
                ch(g.memset(onesc128f[:, :], 1.0))
                ch(g.memset(onesc128b[:, :], 1.0))
                ch(g.memset(ones16[:, :], 1.0))
                ch(g.memset(sel2[:, :], 1.0))
                ch(g.affine_select(sel2[:, :], sel2[:, :], [[1, 256]], OP.is_ge, 0.0,
                                   base=0, channel_multiplier=-128))
                ch(g.affine_select(sel2[:, :], sel2[:, :], [[-1, 256]], OP.is_ge, 0.0,
                                   base=127, channel_multiplier=128))
                ch(g.memset(hsel[:, :], 0.0))
                ch(g.memset(hsel[0:64, 0:1], 1.0))
                ch(g.memset(hsel[64:128, 1:2], 1.0))
                ch(g.memset(iden[:, :], 1.0))
                ch(g.affine_select(iden[:, :], iden[:, :], [[-1, 128]], OP.is_equal, 0.0,
                                   base=0, channel_multiplier=1))
                ch(g.iota(iota64[:, :], [[1, E]], base=0, channel_multiplier=0,
                          allow_small_or_imprecise_dtypes=True))
                ch(g.memset(triU[:, :], 1.0))
                ch(g.affine_select(triU[:, :], triU[:, :], [[1, 128]], OP.is_ge, 0.0,
                                   base=-1, channel_multiplier=-1))
                ch(g.memset(triU16[:, :], 1.0))
                ch(g.affine_select(triU16[:, :], triU16[:, :], [[1, 16]], OP.is_ge, 0.0,
                                   base=-1, channel_multiplier=-1))
                ch(g.memset(triD[:, :], 1.0))
                ch(g.affine_select(triD[:, :], triD[:, :], [[1, 128]], OP.is_ge, 0.0,
                                   base=0, channel_multiplier=-1))
                for h in range(NH):
                    ch(g.memset(vtok[:, :, h * 65 + 64:h * 65 + 65], 1.0))
                S.inc(g.drain(), "gc")

            @blk.sync
            def _(sy):
                S.pre(sy, "ld"); S.dma(sy.dma_start(out=rmsw_col[:, :], in_=rms_w_col))
                S.pre(sy, "ld"); S.dma(sy.dma_start(out=bq_col[:, :], in_=battn_col[:, 0:NCH]))
                S.pre(sy, "ld"); S.dma(sy.dma_start(out=bkv[:, :], in_=battn_col[:, NCH:3 * NCH]))
            ld0 = S.n["ld"]

            if upto == "consts":
                _fin()
                return nc
            with ExitStack() as ph:
                _e = ph.enter_context
                xt = _e(nc.sbuf_tensor("xt", [128, 2, T], F32))
                xtk = _e(nc.sbuf_tensor("xtk", [128, 2, C], F32))
                sq = _e(nc.sbuf_tensor("sq", [128, C], F32))
                ssq = _e(nc.sbuf_tensor("ssq", [128, NT], F32))
                xnf = _e(nc.sbuf_tensor("xnf", [128, 1, T], F32))
                xmbf = _e(nc.sbuf_tensor("xmbf", [128, 2, C], BF16))
                m_b = _e(nc.sbuf_tensor("m_b", [128, T], F32))
                rstd_b = _e(nc.sbuf_tensor("rstd_b", [128, T], F32))
                rmsw_b = _e(nc.sbuf_tensor("rmsw_b", [128, C], F32))
                gwT = _e(nc.sbuf_tensor("gwT", [128, NCH, E], F32))
                stile_b = _e(nc.sbuf_tensor("stile_b", [128, E], F32))
                ltile = _e(nc.sbuf_tensor("ltile", [128, E], F32))
                gacc = _e(nc.sbuf_tensor("gacc", [128, NT * E], F32))
                chosen = _e(nc.sbuf_tensor("chosen", [128, NT, E], BF16))
                dcol = _e(nc.sbuf_tensor("dcol", [128, NT], F32))
                mx8 = _e(nc.sbuf_tensor("mx8", [128, 8], F32))
                idx8 = _e(nc.sbuf_tensor("idx8", [128, 8], U32))
                oh = _e(nc.sbuf_tensor("oh", [128, 2, E], F32))
                w64 = _e(nc.sbuf_tensor("w64", [128, E], F32))
                cnt16 = _e(nc.sbuf_tensor("cnt16", [16, E], F32))
                toff16 = _e(nc.sbuf_tensor("toff16", [16, E], F32))
                psA = _e(nc.psum_tensor("psA1", [128, 2048], F32))
                psB = _e(nc.psum_tensor("psB1", [128, 1024], F32))
                psC = _e(nc.psum_tensor("psC1", [128, 512], F32))

                xtok_r = xtok.rearrange("(n p) c -> p n c", p=128)
                xT_r = xT.rearrange("(c p) t -> p c t", p=128)

                @blk.sync
                def _(sy):
                    S.pre(sy, "ld"); S.dma(sy.dma_start(out=gacc[0:1, 0:C], in_=rms_w))
                    S.pre(sy, "ld"); S.dma(sy.dma_start(out=gwT[:, :, :], in_=gate_wT.rearrange("(c p) e -> p c e", p=128)))
                ld0b = S.n["ld"]

                @blk.vector
                def _(v):
                    v = _Chain(v, S, "dvx")
                    v.wait_ge(S.h["ld"], ld0b)
                    S.wait(v, "gc")
                    for c_ in range(NCH):
                        v.tensor_scalar(gwT[:, c_, :], gwT[:, c_, :], rmsw_col[:, c_:c_ + 1], None, OP.mult)
                    S.inc(v.drain(), "dv")

                @blk.tensor
                def _(t):
                    S.wait(t, "dv")
                    for c_ in range(NCH):
                        t.matmul(psB[0:1, 0:E], onesc128f[:, :], gwT[:, c_, :],
                                 start=(c_ == 0), stop=(c_ == NCH - 1))
                    for (a, b) in n512(0, C):
                        t.matmul(psA[:, a:b], ones_col[:, :], gacc[0:1, a:b], start=True, stop=True)
                    S.inc(t.drain(), "pe")

                @blk.vector
                def _(v):
                    v = _Chain(v, S, "dvx")
                    S.wait(v, "pe")
                    v.tensor_copy(small[0:1, 0:E], psB[0:1, 0:E])
                    v.tensor_copy(rmsw_b[:, :], psA[:, 0:C])
                    S.inc(v.drain(), "dv")

                @blk.tensor
                def _(t):
                    S.wait(t, "dv")
                    t.matmul(psB[:, 0:E], ones_col[:, :], small[0:1, 0:E], start=True, stop=True)
                    S.inc(t.drain(), "pe")

                @blk.vector
                def _(v):
                    v = _Chain(v, S, "dvx")
                    S.wait(v, "pe")
                    v.tensor_copy(stile_b[:, :], psB[:, 0:E])
                    S.inc(v.drain(), "dv")

                ld_marks = []
                st_marks = []
                dv_marks = []
                for tt in range(NT):
                    @blk.sync
                    def _(sy, tt=tt):
                        if tt >= 2:
                            S.wait(sy, "dv", dv_marks[tt - 2])
                        S.dma(sy.dma_start(out=xtk[:, tt % 2, :], in_=xtok_r[:, tt, :]), f"lt{tt % 2}")
                        ld_marks.append(S.n[f"lt{tt % 2}"])

                    @blk.scalar
                    def _(sc, tt=tt):
                        sc = _Chain(sc, S, "acx")
                        sc.wait_ge(S.h[f"lt{tt % 2}"], ld_marks[tt])
                        S.wait(sc, "dv")
                        sc.activation(out=sq[:, :], in_=xtk[:, tt % 2, :], func=AF.Square,
                                      accum_out=ssq[:, tt:tt + 1])
                        S.inc(sc.drain(), "ac")

                    @blk.vector
                    def _(v, tt=tt):
                        v = _Chain(v, S, "dvx")
                        v.wait_ge(S.h[f"lt{tt % 2}"], ld_marks[tt])
                        v.reduce_sum(m_cols[:, tt:tt + 1], xtk[:, tt % 2, :], AX)
                        S.wait(v, "ac")
                        c0 = slice(tt, tt + 1)
                        v.tensor_scalar(m_cols[:, c0], m_cols[:, c0], 1.0 / C, None, OP.mult)
                        v.tensor_scalar(ssq[:, c0], ssq[:, c0], 1.0 / C, None, OP.mult)
                        v.tensor_tensor(g_cols[:, c0], m_cols[:, c0], m_cols[:, c0], OP.mult)
                        v.tensor_tensor(ssq[:, c0], ssq[:, c0], g_cols[:, c0], OP.subtract)
                        v.tensor_scalar(rstd_cols[:, c0], ssq[:, c0], EPS_LN, None, OP.add)
                        S.inc(v.drain(), "dv")

                    @blk.scalar
                    def _(sc, tt=tt):
                        sc = _Chain(sc, S, "acx")
                        S.wait(sc, "dv")
                        c0 = slice(tt, tt + 1)
                        sc.activation(out=cg_cols[:, c0], in_=rstd_cols[:, c0], func=AF.Ln)
                        sc.activation(out=rstd_cols[:, c0], in_=cg_cols[:, c0], func=AF.Exp, scale=-0.5)
                        sc.activation(out=g_cols[:, c0], in_=cg_cols[:, c0], func=AF.Exp, scale=-1.0)
                        S.inc(sc.drain(), "ac")

                    @blk.vector
                    def _(v, tt=tt):
                        v = _Chain(v, S, "dvx")
                        S.wait(v, "ac")
                        c0 = slice(tt, tt + 1)
                        v.tensor_tensor(g_cols[:, c0], g_cols[:, c0], ssq[:, c0], OP.mult)
                        v.tensor_scalar(g_cols[:, c0], g_cols[:, c0], EPS_RMS, None, OP.add)
                        S.inc(v.drain(), "dv")

                    @blk.scalar
                    def _(sc, tt=tt):
                        sc = _Chain(sc, S, "acx")
                        S.wait(sc, "dv")
                        c0 = slice(tt, tt + 1)
                        sc.activation(out=g_cols[:, c0], in_=g_cols[:, c0], func=AF.Ln)
                        sc.activation(out=g_cols[:, c0], in_=g_cols[:, c0], func=AF.Exp, scale=-0.5)
                        S.inc(sc.drain(), "ac")

                    @blk.vector
                    def _(v, tt=tt):
                        v = _Chain(v, S, "dvx")
                        S.wait(v, "ac")
                        c0 = slice(tt, tt + 1)
                        v.tensor_tensor(cg_cols[:, c0], rstd_cols[:, c0], g_cols[:, c0], OP.mult)
                        if tt >= 2:
                            v.wait_ge(S.h[f"sx{tt % 2}"], st_marks[tt - 2])
                        v.tensor_scalar(sq[:, :], xtk[:, tt % 2, :], m_cols[:, c0],
                                        cg_cols[:, c0], OP.subtract, OP.mult)
                        v.tensor_tensor(xmbf[:, tt % 2, :], sq[:, :], rmsw_b[:, :], OP.mult)
                        S.inc(v.drain(), "dv")
                        dv_marks.append(S.n["dv"])

                    @blk.sync
                    def _(sy, tt=tt):
                        S.wait(sy, "dv")
                        S.dma(sy.dma_start(out=xm_out.rearrange("(n p) c -> p n c", p=128)[:, tt, :],
                                           in_=xmbf[:, tt % 2, :]), f"sx{tt % 2}")
                        st_marks.append(S.n[f"sx{tt % 2}"])
                dv_stats = S.n["dv"]

                ph_rows = ExitStack()
                m_row = ph_rows.enter_context(nc.sbuf_tensor("m_row", [1, T], F32))
                for (colt, bt) in ((m_cols, m_b), (rstd_cols, rstd_b)):
                    @blk.tensor
                    def _(t, colt=colt):
                        if colt is m_cols:
                            t.wait_ge(S.h["dv"], dv_stats)
                        else:
                            S.wait(t, "dv")
                        for tt in range(NT):
                            t.transpose(psA[0:1, tt * 128:(tt + 1) * 128], colt[:, tt:tt + 1], iden[:, :])
                        S.inc(t.drain(), "pe")

                    @blk.vector
                    def _(v):
                        v = _Chain(v, S, "dvx")
                        S.wait(v, "pe")
                        v.tensor_copy(m_row[:, :], psA[0:1, 0:T])
                        S.inc(v.drain(), "dv")

                    @blk.tensor
                    def _(t):
                        S.wait(t, "dv")
                        for (a, b) in n512(0, T):
                            t.matmul(psA[:, a:b], ones_col[:, :], m_row[:, a:b], start=True, stop=True)
                        S.inc(t.drain(), "pe")

                    @blk.vector
                    def _(v, bt=bt):
                        v = _Chain(v, S, "dvx")
                        S.wait(v, "pe")
                        v.tensor_copy(bt[:, :], psA[:, :])
                        S.inc(v.drain(), "dv")
                dv_bcast = S.n["dv"]
                ph_rows.close()

                pe_marks = {}
                stx_marks = {}
                ldc_marks = []
                for c_ in range(NCH):
                    buf = 0

                    @blk.sync
                    def _(sy, c_=c_, buf=buf):
                        if c_ >= 1:
                            sy.wait_ge(S.h[f"sp{(c_ - 1) % 2}"], stx_marks[c_ - 1])
                        S.dma(sy.dma_start(out=xt[:, buf, :], in_=xT_r[:, c_, :]), "lc0")
                        ldc_marks.append(S.n["lc0"])

                    @blk.vector
                    def _(v, c_=c_, buf=buf):
                        v = _Chain(v, S, "dvx")
                        if c_ == 0:
                            v.wait_ge(S.h["dv"], dv_bcast)
                        v.wait_ge(S.h["lc0"], ldc_marks[c_])
                        if c_ >= 2:
                            v.wait_ge(S.h["pe"], pe_marks[c_ - 2])
                        v.tensor_tensor(xnf[:, buf, :], xt[:, buf, :], m_b[:, :], OP.subtract)
                        v.tensor_tensor(xnf[:, buf, :], xnf[:, buf, :], rstd_b[:, :], OP.mult)
                        v.tensor_copy(xnT[:, c_, :], xnf[:, buf, :])
                        v.tensor_tensor(xt[:, buf, :], xt[:, buf, :], xnf[:, buf, :], OP.add)
                        S.inc(v.drain(), "dv")

                    @blk.sync
                    def _(sy, c_=c_, buf=buf):
                        S.wait(sy, "dv")
                        S.dma(sy.dma_start(out=partial[c_ * 128:(c_ + 1) * 128, :], in_=xt[:, buf, :]), f"sp{c_ % 2}")
                        stx_marks[c_] = S.n[f"sp{c_ % 2}"]

                    @blk.tensor
                    def _(t, c_=c_, buf=buf):
                        S.wait(t, "dv")
                        if c_ >= 1:
                            S.wait(t, "dv")
                        for tt in range(NT):
                            t.matmul(psB[:, tt * E:(tt + 1) * E],
                                     xnf[:, buf, tt * 128:(tt + 1) * 128], gwT[:, c_, :],
                                     start=True, stop=True)
                        S.inc(t.drain(), "pe")
                        pe_marks[c_] = S.n["pe"]

                    @blk.vector
                    def _(v, c_=c_):
                        v = _Chain(v, S, "dvx")
                        S.wait(v, "pe")
                        if c_ == 0:
                            v.tensor_copy(gacc[:, :], psB[:, 0:NT * E])
                        else:
                            v.tensor_tensor(gacc[:, :], gacc[:, :], psB[:, 0:NT * E], OP.add)
                        S.inc(v.drain(), "dv")

                for tt in range(NT):
                    @blk.vector
                    def _(v, tt=tt):
                        v = _Chain(v, S, "dvx")
                        if tt == 0:
                            S.wait(v, "pe")
                        v.tensor_scalar(ltile[:, :], gacc[:, tt * E:(tt + 1) * E],
                                        g_cols[:, tt:tt + 1], None, OP.mult)
                        v.max(out=mx8[:, :], in_=ltile[:, :])
                        v.max_index(out=idx8[:, :], in_max=mx8[:, :], in_values=ltile[:, :])
                        v.tensor_copy(meta_t[:, tt, 0:2], idx8[:, 0:2])
                        v.tensor_tensor(dcol[:, tt:tt + 1], mx8[:, 0:1], mx8[:, 1:2], OP.subtract)
                        v.tensor_scalar(oh[:, 0, :], iota64[:, :], meta_t[:, tt, 0:1], None, OP.is_equal)
                        v.tensor_scalar(oh[:, 1, :], iota64[:, :], meta_t[:, tt, 1:2], None, OP.is_equal)
                        v.tensor_tensor(w64[:, :], oh[:, 0, :], oh[:, 1, :], OP.add)
                        v.tensor_copy(chosen[:, tt, :], w64[:, :])
                        S.inc(v.drain(), "dv")

                    @blk.tensor
                    def _(t, tt=tt):
                        S.wait(t, "dv")
                        t.matmul(psC[:, 0:E], triU[:, :], chosen[:, tt, :], start=True, stop=True)
                        S.inc(t.drain(), "pe")

                    @blk.vector
                    def _(v, tt=tt):
                        v = _Chain(v, S, "dvx")
                        S.wait(v, "pe")
                        for k in range(2):
                            v.tensor_tensor(w64[:, :], psC[:, 0:E], oh[:, k, :], OP.mult)
                            v.reduce_sum(meta_t[:, tt, 4 + k:5 + k], w64[:, :], AX)
                        S.inc(v.drain(), "dv")

                @blk.scalar
                def _(sc):
                    sc = _Chain(sc, S, "acx")
                    S.wait(sc, "dv")
                    sc.activation(out=meta_t[:, :, 2:3], in_=dcol[:, :].rearrange("p (n o) -> p n o", o=1),
                                  func=AF.Sigmoid)
                    sc.activation(out=meta_t[:, :, 3:4], in_=dcol[:, :].rearrange("p (n o) -> p n o", o=1),
                                  func=AF.Sigmoid, scale=-1.0)
                    S.inc(sc.drain(), "ac")

                @blk.tensor
                def _(t):
                    S.wait(t, "dv")
                    ch_flat = chosen.ap().rearrange("p n e -> p (n e)")
                    t.matmul(psC[0:1, 0:512], onesc128b[:, :], ch_flat[:, 0:512], start=True, stop=True)
                    t.matmul(psB[0:1, 0:512], onesc128b[:, :], ch_flat[:, 512:1024], start=True, stop=True)
                    S.inc(t.drain(), "pe")

                @blk.vector
                def _(v):
                    v = _Chain(v, S, "dvx")
                    S.wait(v, "pe")
                    v.tensor_copy(gacc[0:1, 0:512], psC[0:1, 0:512])
                    v.tensor_copy(gacc[0:1, 512:1024], psB[0:1, 0:512])
                    S.inc(v.drain(), "dv")

                @blk.gpsimd
                def _(g):
                    g = _Chain(g, S, "gcx")
                    S.wait(g, "dv")
                    S.pre(g, "gp")
                    S.dma(g.dma_start(out=cnt16[:, :], in_=gacc[0:1, 0:NT * E]), "gp")

                @blk.tensor
                def _(t):
                    S.wait(t, "gp")
                    t.matmul(psC[0:16, 0:E], triU16[:, :], cnt16[:, :], start=True, stop=True)
                    t.matmul(psB[0:1, 0:E], ones16[:, :], cnt16[:, :], start=True, stop=True)
                    S.inc(t.drain(), "pe")

                @blk.vector
                def _(v):
                    v = _Chain(v, S, "dvx")
                    S.wait(v, "pe")
                    v.tensor_copy(toff16[:, :], psC[0:16, 0:E])
                    v.tensor_copy(small[0:1, 0:E], psB[0:1, 0:E])
                    S.inc(v.drain(), "dv")

                @blk.sync
                def _(sy):
                    S.wait(sy, "dv")
                    S.pre(sy, "st")
                    S.dma(sy.dma_start(out=counts, in_=small[0:1, 0:E]), "st")

                for tt in range(NT):
                    @blk.gpsimd
                    def _(g, tt=tt):
                        g = _Chain(g, S, "gcx")
                        S.wait(g, "dv")
                        S.pre(g, "gp")
                        if tt == 0:
                            S.wait(g, "st")
                        S.dma(g.dma_start(out=small[0:1, 0:E], in_=toff16[tt:tt + 1, :]), "gp")

                    @blk.tensor
                    def _(t, tt=tt):
                        S.wait(t, "gp")
                        t.matmul(psB[:, 0:E], ones_col[:, :], small[0:1, 0:E],
                                 start=True, stop=True)
                        S.inc(t.drain(), "pe")

                    @blk.vector
                    def _(v, tt=tt):
                        v = _Chain(v, S, "dvx")
                        S.wait(v, "pe")
                        v.tensor_copy(w64[:, :], psB[:, 0:E])
                        for k in range(2):
                            v.tensor_scalar(oh[:, 0, :], iota64[:, :], meta_t[:, tt, k:k + 1], None, OP.is_equal)
                            v.tensor_tensor(oh[:, 0, :], oh[:, 0, :], w64[:, :], OP.mult)
                            v.reduce_sum(oh[:, 1, 0:1], oh[:, 0, :], AX)
                            v.tensor_tensor(meta_t[:, tt, 4 + k:5 + k], meta_t[:, tt, 4 + k:5 + k],
                                            oh[:, 1, 0:1], OP.add)
                        S.inc(v.drain(), "dv")

                @blk.sync
                def _(sy):
                    S.wait(sy, "dv")
                    S.wait(sy, "ac")
                    S.pre(sy, "st")
                    S.dma(sy.dma_start(out=meta.rearrange("(n p) c -> p n c", p=128), in_=meta_t[:, :, :]), "st")
                    S.wait(sy, "st")
                    sy.wait_ge(S.h["sx0"], S.n["sx0"])
                    sy.wait_ge(S.h["sx1"], S.n["sx1"])
                    sy.wait_ge(S.h["sp0"], S.n["sp0"])
                    sy.wait_ge(S.h["sp1"], S.n["sp1"])

            if upto == "A":
                _fin()
                return nc
            with ExitStack() as ph:
                _e = ph.enter_context
                w2t = _e(nc.sbuf_tensor("w2t", [64, 128], F32))
                w2d = _e(nc.sbuf_tensor("w2d", [128, 64], F32))
                wfold = _e(nc.sbuf_tensor("wfold", [128, NCH, C], F32))
                brow = _e(nc.sbuf_tensor("brow", [1, C], F32))
                vhat = _e(nc.sbuf_tensor("vhat", [128, NCH, NH * HD], BF16))
                bvh_b = _e(nc.sbuf_tensor("bvh_b", [128, C], F32))
                psA = _e(nc.psum_tensor("psA2", [128, 1024], F32))
                psB = _e(nc.psum_tensor("psB2", [128, 1024], F32))
                psC = _e(nc.psum_tensor("psC2", [128, 512], F32))

                for side in ("k", "v"):
                    @blk.sync
                    def _(sy, side=side):
                        sy.wait_ge(S.h["pe"], S.n["pe"])
                        sy.wait_ge(S.h["dv"], S.n["dv"])
                        sy.wait_ge(S.h["ac"], S.n["ac"])
                        S.pre(sy, "ld")
                        S.dma(sy.dma_start(out=w2t[0:32, 0:64], in_=(wk_cT if side == "k" else wv_cT)))
                        S.pre(sy, "ld")
                        S.dma(sy.dma_start(out=w2t[0:32, 64:128], in_=(wk_e if side == "k" else wv_e)))
                        lo = C if side == "k" else 2 * C
                        S.pre(sy, "ld")
                        S.dma(sy.dma_start(out=wfold[:, :, :],
                                           in_=w_attnT[lo:lo + C, :].rearrange("(c p) q -> p c q", p=128)))
                    ldw_ = S.n["ld"]

                    @blk.tensor
                    def _(t, ldw_=ldw_):
                        t.wait_ge(S.h["ld"], ldw_)
                        t.matmul(psC[0:64, 0:64], w2t[0:32, 0:64], w2t[0:32, 64:128], start=True, stop=True)
                        t.matmul(psB[64:128, 0:64], w2t[0:32, 0:64], w2t[0:32, 64:128], start=True, stop=True)
                        S.inc(t.drain(), "pe")

                    @blk.vector
                    def _(v, side=side):
                        v = _Chain(v, S, "dvx")
                        S.wait(v, "pe")
                        v.tensor_scalar(w2d[0:64, 0:64], psC[0:64, 0:64],
                                        0.125 if side == "k" else 1.0, None, OP.mult)
                        v.tensor_scalar(w2d[64:128, 0:64], psB[64:128, 0:64],
                                        0.125 if side == "k" else 1.0, None, OP.mult)
                        S.inc(v.drain(), "dv")

                    if upto == "fk1" and side == "k":
                        _fin()
                        return nc
                    for ct in range(NCH):
                        @blk.tensor
                        def _(t, ct=ct):
                            S.wait(t, "dv")
                            for h in [x for x in range(NH) if x % 2 == 0] + \
                                     [x for x in range(NH) if x % 2 == 1]:
                                t.matmul(psB[:, h * 64:(h + 1) * 64],
                                         wfold[(h % 2) * 64:(h % 2) * 64 + 64, h // 2, ct * 128:(ct + 1) * 128],
                                         w2d[(h % 2) * 64:(h % 2) * 64 + 64, 0:64], start=True, stop=True)
                            S.inc(t.drain(), "pe")

                        @blk.vector
                        def _(v, ct=ct, side=side):
                            v = _Chain(v, S, "dvx")
                            S.wait(v, "pe")
                            v.tensor_copy((khat if side == "k" else vhat)[:, ct, :], psB[:, 0:C])
                            S.inc(v.drain(), "dv")

                    if upto == "fk2" and side == "k":
                        _fin()
                        return nc
                    @blk.tensor
                    def _(t, side=side):
                        S.wait(t, "dv")
                        off = 0 if side == "k" else NCH
                        for h in [x for x in range(NH) if x % 2 == 0] + \
                                 [x for x in range(NH) if x % 2 == 1]:
                            t.matmul(psA[0:1, h * 64:(h + 1) * 64],
                                     bkv[(h % 2) * 64:(h % 2) * 64 + 64, off + h // 2:off + h // 2 + 1],
                                     w2d[(h % 2) * 64:(h % 2) * 64 + 64, 0:64], start=True, stop=True)
                        S.inc(t.drain(), "pe")

                    @blk.vector
                    def _(v):
                        v = _Chain(v, S, "dvx")
                        S.wait(v, "pe")
                        v.tensor_copy(brow[0:1, 0:C], psA[0:1, 0:C])
                        S.inc(v.drain(), "dv")

                    if upto == "fk3" and side == "k":
                        _fin()
                        return nc
                    if side == "k":
                        @blk.tensor
                        def _(t):
                            S.wait(t, "dv")
                            for c_ in range(NCH):
                                t.matmul(psC[:, c_:c_ + 1], brow[0:1, c_ * 128:(c_ + 1) * 128],
                                         ones_col[0:1, 0:1], start=True, stop=True)
                            S.inc(t.drain(), "pe")

                        @blk.vector
                        def _(v):
                            v = _Chain(v, S, "dvx")
                            S.wait(v, "pe")
                            v.tensor_copy(bkh_col[:, :], psC[:, 0:NCH])
                            S.inc(v.drain(), "dv")
                    else:
                        @blk.tensor
                        def _(t):
                            S.wait(t, "dv")
                            for (a, b) in n512(0, C):
                                t.matmul(psA[:, a:b], ones_col[:, :], brow[0:1, a:b], start=True, stop=True)
                            S.inc(t.drain(), "pe")

                        @blk.vector
                        def _(v):
                            v = _Chain(v, S, "dvx")
                            S.wait(v, "pe")
                            v.tensor_copy(bvh_b[:, :], psA[:, 0:C])
                            S.inc(v.drain(), "dv")

                if upto == "folds0":
                    _fin()
                    return nc
                with nc.psum_tensor("psV", [128, 1024], F32) as psV:
                    for tt in range(NT):
                        @blk.tensor
                        def _(t, tt=tt):
                            S.wait(t, "dv")
                            for cc in range(NCH):
                                for (a, b) in n512(0, C):
                                    t.matmul(psV[:, a:b], xnT[:, cc, tt * 128:(tt + 1) * 128],
                                             vhat[:, cc, a:b], start=(cc == 0), stop=(cc == NCH - 1))
                            S.inc(t.drain(), "pe")

                        @blk.vector
                        def _(v, tt=tt):
                            v = _Chain(v, S, "dvx")
                            S.wait(v, "pe")
                            v.tensor_tensor(
                                vtok[:, tt, :].rearrange("p (h x) -> p h x", h=NH)[:, :, 0:64],
                                psV[:, 0:C].rearrange("p (h x) -> p h x", h=NH),
                                bvh_b[:, 0:C].rearrange("p (h x) -> p h x", h=NH),
                                OP.add)
                            S.inc(v.drain(), "dv")


            if upto == "folds":
                _fin()
                return nc
            with nc.sbuf_tensor("wq_sb", [128, NCH, C], BF16) as wq_sb:
                @blk.gpsimd
                def _(g):
                    g = _Chain(g, S, "gcx")
                    S.wait(g, "pe")
                    S.wait(g, "dv")
                    S.pre(g, "gp")
                    S.dma(g.dma_start(out=wq_sb[:, :, :],
                                      in_=w_attn[:, 0:C].rearrange("(c p) f -> p c f", p=128)), "gp")
                gp_wq = S.n["gp"]

                for hp in range(NH // 2):
                    with ExitStack() as ph:
                        _e = ph.enter_context
                        qp = _e(nc.sbuf_tensor(f"qp{hp}", [128, T], BF16))
                        kp = _e(nc.sbuf_tensor(f"kp{hp}", [128, T], BF16))
                        sqn = _e(nc.sbuf_tensor(f"sqn{hp}", [128, T], F32))
                        nrm = _e(nc.sbuf_tensor(f"nrm{hp}", [2, 8], F32))
                        mneg2 = _e(nc.sbuf_tensor(f"mneg2{hp}", [128, 2], F32))

                        with ExitStack() as ph2:
                            psQ = ph2.enter_context(nc.psum_tensor(f"psQ{hp}", [128, 2048], F32))
                            psN = ph2.enter_context(nc.psum_tensor(f"psN{hp}", [2, 2048], F32))
                            for (which, wtile, outt) in (("q", wq_sb, qp), ("k", khat, kp)):
                                @blk.tensor
                                def _(t, which=which, wtile=wtile, hp=hp):
                                    if which == "q":
                                        t.wait_ge(S.h["gp"], gp_wq)
                                    S.wait(t, "dv")
                                    for cc in range(NCH):
                                        for (a, b) in n512(0, T):
                                            t.matmul(psQ[:, a:b], wtile[:, cc, hp * 128:(hp + 1) * 128],
                                                     xnT[:, cc, a:b], start=(cc == 0), stop=(cc == NCH - 1))
                                    S.inc(t.drain(), "pe")

                                @blk.vector
                                def _(v, which=which, outt=outt, hp=hp):
                                    v = _Chain(v, S, "dvx")
                                    S.wait(v, "pe")
                                    col = bq_col[:, hp:hp + 1] if which == "q" else bkh_col[:, hp:hp + 1]
                                    v.tensor_scalar(outt[:, :], psQ[:, :], col, None, OP.add)
                                    S.inc(v.drain(), "dv")

                                @blk.scalar
                                def _(sc, outt=outt):
                                    sc = _Chain(sc, S, "acx")
                                    S.wait(sc, "dv")
                                    sc.activation(out=sqn[:, :], in_=outt[:, :], func=AF.Square)
                                    S.inc(sc.drain(), "ac")

                                @blk.tensor
                                def _(t, which=which):
                                    S.wait(t, "ac")
                                    S.wait(t, "dv")
                                    for (a, b) in n512(0, T):
                                        t.matmul(psN[0:2, a:b], hsel[:, :], sqn[:, a:b], start=True, stop=True)
                                    S.inc(t.drain(), "pe")

                                @blk.vector
                                def _(v, which=which):
                                    v = _Chain(v, S, "dvx")
                                    S.wait(v, "pe")
                                    idx = 0 if which == "q" else 1
                                    v.reduce_max(nrm[0:2, idx:idx + 1], psN[0:2, 0:T], AX)
                                    S.inc(v.drain(), "dv")

                        @blk.scalar
                        def _(sc):
                            sc = _Chain(sc, S, "acx")
                            S.wait(sc, "dv")
                            sc.activation(out=nrm[0:2, 2:4], in_=nrm[0:2, 0:2], func=AF.Sqrt)
                            S.inc(sc.drain(), "ac")

                        @blk.vector
                        def _(v):
                            v = _Chain(v, S, "dvx")
                            S.wait(v, "ac")
                            v.tensor_tensor(nrm[0:2, 4:5], nrm[0:2, 2:3], nrm[0:2, 3:4], OP.mult)
                            v.tensor_scalar(nrm[0:2, 5:6], nrm[0:2, 4:5], -1.0, None, OP.mult)
                            S.inc(v.drain(), "dv")

                        with nc.psum_tensor(f"psM{hp}", [128, 512], F32) as psM:
                            @blk.tensor
                            def _(t):
                                S.wait(t, "dv")
                                t.matmul(psM[:, 0:1], sel2[:, 0:128], nrm[0:2, 5:6], start=True, stop=True)
                                t.matmul(psM[:, 1:2], sel2[:, 128:256], nrm[0:2, 5:6], start=True, stop=True)
                                S.inc(t.drain(), "pe")

                            @blk.vector
                            def _(v):
                                v = _Chain(v, S, "dvx")
                                S.wait(v, "pe")
                                v.tensor_copy(mneg2[:, :], psM[:, 0:2])
                                S.inc(v.drain(), "dv")

                        with ExitStack() as ph3:
                            _e3 = ph3.enter_context
                            pt = _e3(nc.sbuf_tensor(f"pt{hp}", [128, 2, 1024], BF16))
                            attn_un = _e3(nc.sbuf_tensor(f"attn_un{hp}", [64, T], F32))
                            lex = _e3(nc.sbuf_tensor(f"lex{hp}", [128, T], F32))
                            psS0 = _e3(nc.psum_tensor(f"psS0{hp}", [128, 1024], F32))
                            psS1 = _e3(nc.psum_tensor(f"psS1{hp}", [128, 1024], F32))
                            psAV = _e3(nc.psum_tensor(f"psAV{hp}", [65, 2048], F32))
                            psS = (psS0, psS1)
                            for lh in range(2):
                                h = hp * 2 + lh
                                r0 = lh * 64
                                sl = 0
                                for kc in range(NT):
                                    qlo = kc * 128
                                    pos = qlo
                                    while pos < T:
                                        qhi = min(pos + 1024, T)
                                        par = sl % 2

                                        @blk.tensor
                                        def _(t, kc=kc, pos=pos, qhi=qhi, par=par, r0=r0, sl=sl):
                                            S.wait(t, "dv")
                                            S.wait(t, "ac")
                                            for (a, b) in n512(pos, qhi):
                                                t.matmul(psS[par][:, a - pos:b - pos],
                                                         kp[r0:r0 + 64, kc * 128:(kc + 1) * 128],
                                                         qp[r0:r0 + 64, a:b], start=True, stop=True)
                                            S.inc(t.drain(), "pe")

                                        @blk.scalar
                                        def _(sc, kc=kc, pos=pos, qhi=qhi, par=par, lh=lh):
                                            sc = _Chain(sc, S, "acx")
                                            S.wait(sc, "pe")
                                            sc.activation(out=pt[:, par, 0:qhi - pos],
                                                          in_=psS[par][:, 0:qhi - pos], func=AF.Exp,
                                                          bias=mneg2[:, lh:lh + 1])
                                            S.inc(sc.drain(), "ac")

                                        if pos == qlo:
                                            @blk.vector
                                            def _(v, par=par):
                                                v = _Chain(v, S, "dvx")
                                                S.wait(v, "ac")
                                                v.tensor_tensor(pt[:, par, 0:128], pt[:, par, 0:128],
                                                                triD[:, :], OP.mult)
                                                S.inc(v.drain(), "dv")

                                        @blk.tensor
                                        def _(t, kc=kc, pos=pos, qhi=qhi, par=par, h=h):
                                            if pos == kc * 128:
                                                S.wait(t, "dv")
                                            S.wait(t, "ac")
                                            for (a, b) in n512a(pos, qhi):
                                                t.matmul(psAV[:, a:b], vtok[:, kc, h * 65:h * 65 + 65],
                                                         pt[:, par, a - pos:b - pos],
                                                         start=(kc == 0), stop=(kc == NT - 1),
                                                         skip_group_check=True)
                                            S.inc(t.drain(), "pe")
                                        pos = qhi
                                        sl += 1

                                @blk.scalar
                                def _(sc):
                                    sc = _Chain(sc, S, "acx")
                                    S.wait(sc, "pe")
                                    sc.activation(out=lex[64:65, :], in_=psAV[64:65, :], func=AF.Ln)
                                    sc.activation(out=lex[64:65, :], in_=lex[64:65, :], func=AF.Exp, scale=-1.0)
                                    S.inc(sc.drain(), "ac")

                                @blk.vector
                                def _(v):
                                    v = _Chain(v, S, "dvx")
                                    S.wait(v, "ac")
                                    v.tensor_copy(attn_un[:, :], psAV[0:64, :])
                                    S.inc(v.drain(), "dv")

                                @blk.tensor
                                def _(t):
                                    S.wait(t, "dv")
                                    for j in range(2):
                                        for (a, b) in n512(j * 1024, (j + 1) * 1024):
                                            t.matmul(psS[j][:, a - j * 1024:b - j * 1024],
                                                     ones_r64[64:65, :], lex[64:65, a:b],
                                                     start=True, stop=True)
                                    S.inc(t.drain(), "pe")

                                @blk.vector
                                def _(v, h=h):
                                    v = _Chain(v, S, "dvx")
                                    S.wait(v, "pe")
                                    for j in range(2):
                                        v.tensor_tensor(attnTn[(h % 2) * 64:(h % 2) * 64 + 64, h // 2,
                                                                j * 1024:(j + 1) * 1024],
                                                        attn_un[:, j * 1024:(j + 1) * 1024],
                                                        psS[j][0:64, 0:1024], OP.mult)
                                    S.inc(v.drain(), "dv")

            if upto == "attn":
                _fin()
                return nc
            with ExitStack() as ph:
                _e = ph.enter_context
                wpj = _e(nc.sbuf_tensor("wpj", [128, NCH, C], BF16))
                bpj_col = _e(nc.sbuf_tensor("bpj_col", [128, NCH], F32))
                pjf = _e(nc.sbuf_tensor("pjf", [128, 2, T], F32))
                psP = _e(nc.psum_tensor("psP", [128, 2048], F32))
                @blk.gpsimd
                def _(g):
                    g = _Chain(g, S, "gcx")
                    S.wait(g, "pe")
                    S.wait(g, "dv")
                    S.wait(g, "ac")
                    S.pre(g, "gp")
                    S.dma(g.dma_start(out=wpj[:, :, :],
                                      in_=w_proj.rearrange("(c p) f -> p c f", p=128)), "gp")

                @blk.sync
                def _(sy):
                    sy.wait_ge(S.h["pe"], S.n["pe"])
                    sy.wait_ge(S.h["dv"], S.n["dv"])
                    S.pre(sy, "ld")
                    S.dma(sy.dma_start(out=bpj_col[:, :], in_=bproj_col))
                mark_pj = (S.n["gp"], S.n["ld"])

                gp_prev = [None, None]
                for ct in range(NCH):
                    @blk.tensor
                    def _(t, ct=ct):
                        if ct == 0:
                            t.wait_ge(S.h["gp"], mark_pj[0])
                        S.wait(t, "dv")
                        if ct >= 2:
                            t.wait_ge(S.h[f"ga{ct % 2}"], gp_prev[ct % 2])
                        for cc in range(NCH):
                            for (a, b) in n512(0, T):
                                t.matmul(psP[:, a:b], wpj[:, cc, ct * 128:(ct + 1) * 128],
                                         attnTn[:, cc, a:b], start=(cc == 0), stop=(cc == NCH - 1))
                        S.inc(t.drain(), "pe")

                    @blk.vector
                    def _(v, ct=ct):
                        v = _Chain(v, S, "dvx")
                        S.wait(v, "pe")
                        v.wait_ge(S.h["ld"], mark_pj[1])
                        v.tensor_scalar(pjf[:, ct % 2, :], psP[:, :], bpj_col[:, ct:ct + 1], None, OP.add)
                        S.inc(v.drain(), "dv")

                    @blk.gpsimd
                    def _(g, ct=ct):
                        g = _Chain(g, S, "gcx")
                        S.wait(g, "dv")
                        if ct == 0:
                            g.wait_ge(S.h["sp0"], S.n["sp0"])
                            g.wait_ge(S.h["sp1"], S.n["sp1"])
                        S.dma(g.dma_start(out=partial[ct * 128:(ct + 1) * 128, :], in_=pjf[:, ct % 2, :],
                                          accum_op=OP.add), f"ga{ct % 2}")
                        gp_prev[ct % 2] = S.n[f"ga{ct % 2}"]

            if upto == "proj":
                _fin()
                return nc
            with ExitStack() as ph:
                _e = ph.enter_context
                swg_sb = _e(nc.sbuf_tensor("swg_sb", [128, NCH, IDIM], BF16))
                swu_sb = _e(nc.sbuf_tensor("swu_sb", [128, NCH, IDIM], BF16))
                swd_sb = _e(nc.sbuf_tensor("swd_sb", [128, 4, C], BF16))
                gb = _e(nc.sbuf_tensor("gb", [128, T], F32))
                g_row = _e(nc.sbuf_tensor("g_row", [1, T], F32))
                hactT = _e(nc.sbuf_tensor("hactT", [128, 4, T], BF16))
                pgf = _e(nc.sbuf_tensor("pgf", [128, T], F32))
                sgf = _e(nc.sbuf_tensor("sgf", [128, T], F32))
                dnf = _e(nc.sbuf_tensor("dnf", [128, 2, T], F32))
                psG = _e(nc.psum_tensor("psG", [128, 2048], F32))
                psU = _e(nc.psum_tensor("psU", [128, 2048], F32))

                @blk.gpsimd
                def _(g):
                    g = _Chain(g, S, "gcx")
                    S.wait(g, "pe")
                    S.wait(g, "dv")
                    S.wait(g, "ga0")
                    S.wait(g, "ga1")
                    S.pre(g, "gp")
                    S.dma(g.dma_start(out=swg_sb[:, :, :],
                                      in_=sw_gate.rearrange("(c p) i -> p c i", p=128)), "gp")
                    S.pre(g, "gp")
                    S.dma(g.dma_start(out=swu_sb[:, :, :],
                                      in_=sw_up.rearrange("(c p) i -> p c i", p=128)), "gp")
                    S.pre(g, "gp")
                    S.dma(g.dma_start(out=swd_sb[:, :, :],
                                      in_=sw_down.rearrange("(c p) i -> p c i", p=128)), "gp")
                gp_sw = S.n["gp"]

                @blk.vector
                def _(v):
                    v = _Chain(v, S, "dvx")
                    v.wait_ge(S.h["gp"], gp_sw)
                    for c_ in range(NCH):
                        v.tensor_scalar(swg_sb[:, c_, :], swg_sb[:, c_, :], rmsw_col[:, c_:c_ + 1], None, OP.mult)
                        v.tensor_scalar(swu_sb[:, c_, :], swu_sb[:, c_, :], rmsw_col[:, c_:c_ + 1], None, OP.mult)
                    S.inc(v.drain(), "dv")

                @blk.tensor
                def _(t):
                    S.wait(t, "dv")
                    for tt in range(NT):
                        t.transpose(psG[0:1, tt * 128:(tt + 1) * 128], g_cols[:, tt:tt + 1], iden[:, :])
                    S.inc(t.drain(), "pe")

                @blk.vector
                def _(v):
                    v = _Chain(v, S, "dvx")
                    S.wait(v, "pe")
                    v.tensor_copy(g_row[:, :], psG[0:1, 0:T])
                    S.inc(v.drain(), "dv")

                @blk.tensor
                def _(t):
                    S.wait(t, "dv")
                    for (a, b) in n512(0, T):
                        t.matmul(psG[:, a:b], ones_col[:, :], g_row[:, a:b], start=True, stop=True)
                    S.inc(t.drain(), "pe")

                @blk.vector
                def _(v):
                    v = _Chain(v, S, "dvx")
                    S.wait(v, "pe")
                    v.tensor_copy(gb[:, :], psG[:, :])
                    S.inc(v.drain(), "dv")
                dv_gb = S.n["dv"]

                for it in range(4):
                    @blk.tensor
                    def _(t, it=it):
                        t.wait_ge(S.h["dv"], dv_gb)
                        if it >= 1:
                            S.wait(t, "dv")
                        for cc in range(NCH):
                            for (a, b) in n512(0, T):
                                t.matmul(psG[:, a:b], swg_sb[:, cc, it * 128:(it + 1) * 128],
                                         xnT[:, cc, a:b], start=(cc == 0), stop=(cc == NCH - 1))
                        for cc in range(NCH):
                            for (a, b) in n512(0, T):
                                t.matmul(psU[:, a:b], swu_sb[:, cc, it * 128:(it + 1) * 128],
                                         xnT[:, cc, a:b], start=(cc == 0), stop=(cc == NCH - 1))
                        S.inc(t.drain(), "pe")

                    @blk.vector
                    def _(v, it=it):
                        v = _Chain(v, S, "dvx")
                        S.wait(v, "pe")
                        v.tensor_tensor(pgf[:, :], psG[:, :], gb[:, :], OP.mult)
                        S.inc(v.drain(), "dv")

                    @blk.scalar
                    def _(sc, it=it):
                        sc = _Chain(sc, S, "acx")
                        S.wait(sc, "dv")
                        sc.activation(out=sgf[:, :], in_=pgf[:, :], func=AF.Silu)
                        S.inc(sc.drain(), "ac")

                    @blk.vector
                    def _(v, it=it):
                        v = _Chain(v, S, "dvx")
                        S.wait(v, "ac")
                        v.tensor_tensor(pgf[:, :], psU[:, :], gb[:, :], OP.mult)
                        v.tensor_tensor(hactT[:, it, :], sgf[:, :], pgf[:, :], OP.mult)
                        S.inc(v.drain(), "dv")

                gp_dn = [None, None]
                for hc in range(NCH):
                    @blk.tensor
                    def _(t, hc=hc):
                        S.wait(t, "dv")
                        if hc >= 2:
                            t.wait_ge(S.h[f"ga{hc % 2}"], gp_dn[hc % 2])
                        for ic in range(4):
                            for (a, b) in n512(0, T):
                                t.matmul(psG[:, a:b], swd_sb[:, ic, hc * 128:(hc + 1) * 128],
                                         hactT[:, ic, a:b], start=(ic == 0), stop=(ic == 3))
                        S.inc(t.drain(), "pe")

                    @blk.vector
                    def _(v, hc=hc):
                        v = _Chain(v, S, "dvx")
                        S.wait(v, "pe")
                        v.tensor_copy(dnf[:, hc % 2, :], psG[:, :])
                        S.inc(v.drain(), "dv")

                    @blk.gpsimd
                    def _(g, hc=hc):
                        g = _Chain(g, S, "gcx")
                        S.wait(g, "dv")
                        if hc == 0:
                            g.wait_ge(S.h["ga0"], S.n["ga0"])
                            g.wait_ge(S.h["ga1"], S.n["ga1"])
                        S.dma(g.dma_start(out=partial[hc * 128:(hc + 1) * 128, :], in_=dnf[:, hc % 2, :],
                                          accum_op=OP.add), f"ga{hc % 2}")
                        gp_dn[hc % 2] = S.n[f"ga{hc % 2}"]

            @blk.gpsimd
            def _(g):
                g = _Chain(g, S, "gcx")
                for nm in ("gp", "st", "sx0", "sx1", "sp0", "sp1", "ga0", "ga1"):
                    S.wait(g, nm)
                g.nop()



SEM_NAMES = ["ld", "pe", "dv", "ac", "gp", "gc", "st",
             "lt0", "lt1", "lc0", "sx0", "sx1", "sp0", "sp1", "ga0", "ga1",
             "sct0", "sct1", "tr0", "tr1", "wa0", "wa1", "yb0", "yb1",
             "g0", "g1", "lp0", "lp1", "so0", "so1", "tx0", "tx1",
             "cc", "dvx", "acx", "gcx"]


def _stage1_io(nc):
    io = {}
    io["xT"] = nc.dram_tensor("xT", [C, T], F32, kind="ExternalInput").ap()
    io["xtok"] = nc.dram_tensor("xtok", [T, C], F32, kind="ExternalInput").ap()
    io["w_attn"] = nc.dram_tensor("w_attn", [C, 3 * C], F32, kind="ExternalInput").ap()
    io["w_attnT"] = nc.dram_tensor("w_attnT", [3 * C, C], F32, kind="ExternalInput").ap()
    io["b_attn"] = nc.dram_tensor("b_attn", [1, 3 * C], F32, kind="ExternalInput").ap()
    io["w_proj"] = nc.dram_tensor("w_proj", [C, C], F32, kind="ExternalInput").ap()
    io["b_proj"] = nc.dram_tensor("b_proj", [1, C], F32, kind="ExternalInput").ap()
    io["bproj_col"] = nc.dram_tensor("bproj_col", [128, NCH], F32, kind="ExternalInput").ap()
    io["wk_cT"] = nc.dram_tensor("wk_cT", [32, 64], F32, kind="ExternalInput").ap()
    io["wk_e"] = nc.dram_tensor("wk_e", [32, 64], F32, kind="ExternalInput").ap()
    io["wv_cT"] = nc.dram_tensor("wv_cT", [32, 64], F32, kind="ExternalInput").ap()
    io["wv_e"] = nc.dram_tensor("wv_e", [32, 64], F32, kind="ExternalInput").ap()
    io["rms_w"] = nc.dram_tensor("rms_w", [1, C], F32, kind="ExternalInput").ap()
    io["rms_w_col"] = nc.dram_tensor("rms_w_col", [128, NCH], F32, kind="ExternalInput").ap()
    io["battn_col"] = nc.dram_tensor("battn_col", [128, 24], F32, kind="ExternalInput").ap()
    io["gate_wT"] = nc.dram_tensor("gate_wT", [C, E], F32, kind="ExternalInput").ap()
    io["sw_gate"] = nc.dram_tensor("sw_gate", [C, IDIM], F32, kind="ExternalInput").ap()
    io["sw_up"] = nc.dram_tensor("sw_up", [C, IDIM], F32, kind="ExternalInput").ap()
    io["sw_down"] = nc.dram_tensor("sw_down", [IDIM, C], F32, kind="ExternalInput").ap()
    io["partial"] = nc.dram_tensor("partial", [C, T], F32, kind="ExternalOutput").ap()
    io["xm_out"] = nc.dram_tensor("xm_out", [T, C], BF16, kind="ExternalOutput").ap()
    io["meta"] = nc.dram_tensor("meta", [T, 6], F32, kind="ExternalOutput").ap()
    io["counts"] = nc.dram_tensor("counts", [1, E], F32, kind="ExternalOutput").ap()
    return io


def build_stage1(upto="all"):
    nc = bass.Bass("TRN2", target_bir_lowering=False, debug=False)
    io = _stage1_io(nc)
    st0 = ExitStack()
    with st0:
        S = Sems(st0, nc, SEM_NAMES)
        with nc.Block() as blk:
            emit_stage1(_PrefNC(nc, "s1_"), blk, S, io, upto)
    return nc


def _prep_stage1_inputs(inputs):
    f32 = np.float32
    com = {
        "w_attn": np.ascontiguousarray(inputs["L0_attn_c_attn_w"], f32),
        "w_attnT": np.ascontiguousarray(inputs["L0_attn_c_attn_w"].T, f32),
        "b_attn": np.ascontiguousarray(inputs["L0_attn_c_attn_b"].reshape(1, -1), f32),
        "w_proj": np.ascontiguousarray(inputs["L0_attn_c_proj_w"], f32),
        "b_proj": np.ascontiguousarray(inputs["L0_attn_c_proj_b"].reshape(1, -1), f32),
        "wk_cT": np.ascontiguousarray(inputs["L0_attn_wk_c"].T, f32),
        "wk_e": np.ascontiguousarray(inputs["L0_attn_wk_e"], f32),
        "wv_cT": np.ascontiguousarray(inputs["L0_attn_wv_c"].T, f32),
        "wv_e": np.ascontiguousarray(inputs["L0_attn_wv_e"], f32),
        "rms_w": np.ascontiguousarray(inputs["L0_ffn_rms_w"].reshape(1, -1), f32),
        "rms_w_col": np.ascontiguousarray(inputs["L0_ffn_rms_w"].reshape(8, 128).T, f32),
        "battn_col": np.ascontiguousarray(inputs["L0_attn_c_attn_b"].reshape(24, 128).T, f32),
        "bproj_col": np.ascontiguousarray(inputs["L0_attn_c_proj_b"].reshape(8, 128).T, f32),
        "gate_wT": np.ascontiguousarray(inputs["L0_ffn_gate_w"].T, f32),
        "sw_gate": np.ascontiguousarray(inputs["L0_ffn_sw_gate"], f32),
        "sw_up": np.ascontiguousarray(inputs["L0_ffn_sw_up"], f32),
        "sw_down": np.ascontiguousarray(inputs["L0_ffn_sw_down"], f32),
    }
    x = np.asarray(inputs["L0_attn_hidden_states"], f32)
    maps = []
    for b in range(B):
        m = dict(com)
        m["xT"] = np.ascontiguousarray(x[b].T)
        m["xtok"] = np.ascontiguousarray(x[b])
        maps.append(m)
    return maps


def run_stage1(inputs, upto="all", **kw):
    nc = build_stage1(upto)
    maps = _prep_stage1_inputs(inputs)
    res = run_bass_kernel_spmd(nc, maps, core_ids=list(range(B)), **kw)
    return res



def emit_stage2(nc, blk, S, io):
    NTOK = B * T
    NTL = NTOK // 128
    xm_all = io["xm_all"]
    meta_all = io["meta_all"]
    counts_all = io["counts_all"]
    w13 = io["w13"]
    w2w = io["w2w"]
    ebase = io["ebase"]
    yb = io["yb"]
    disp = io["disp"]
    st = ExitStack()
    with st:
        sbt = lambda *a: st.enter_context(nc.sbuf_tensor(*a))

        meta_sb = sbt("meta_sb", [128, NTL, 6], F32)
        cnt8 = sbt("cnt8", [8, E], F32)
        offs = sbt("offs", [8, E], F32)
        offb = sbt("offb", [128, 8, E], F32)
        iota64 = sbt("iota64", [128, E], F32)
        triU8 = sbt("triU8", [8, 8], F32)
        ones_col = sbt("ones_col", [1, 128], F32)
        ebase_c = sbt("ebase_c", [128, 1], F32)
        slots = sbt("slots", [128, NTL, 2], I32)
        oh = sbt("oh", [128, E], F32)
        colA = sbt("colA", [128, 8], F32)
        xmt = sbt("xmt", [128, 2, C], BF16)
        bufT = sbt("bufT", [128, 2, NCH, CAP], BF16)
        w13_sb = sbt("w13_sb", [128, 2, NCH, 2 * IDIM], BF16)
        w2_sb = sbt("w2_sb", [128, 2, 4, C], BF16)
        hactT = sbt("hactT", [128, 4, CAP], BF16)
        sgf = sbt("sgf", [128, CAP], F32)
        ybf = sbt("ybf", [128, 2, C], BF16)

        with _passthrough(blk) as blk:
            def _fin2():
                @blk.gpsimd
                def _(g):
                    for nm in ("ld", "gp", "st", "sct0", "sct1", "lt0", "lt1", "tr0", "tr1",
                               "wa0", "wa1", "yb0", "yb1"):
                        S.wait(g, nm)
                    for nm in ("pe", "dv", "ac", "dvx", "acx", "gcx"):
                        S.wait(g, nm)
                    g.nop()

            @blk.gpsimd
            def _(g):
                g = _Chain(g, S, "gcx")
                g.memset(ones_col[:, :], 1.0)
                g.memset(triU8[:, :], 1.0)
                g.affine_select(triU8[:, :], triU8[:, :], [[1, 8]], OP.is_ge, 0.0,
                                base=-1, channel_multiplier=-1)
                g.iota(iota64[:, :], [[1, E]], base=0, channel_multiplier=0,
                       allow_small_or_imprecise_dtypes=True)
                S.inc(g.drain(), "gc" if False else "gcx")

            @blk.sync
            def _(sy):
                S.pre(sy, "ld")
                S.dma(sy.dma_start(out=meta_sb[:, :, :], in_=meta_all.rearrange("(n p) c -> p n c", p=128)))
                S.pre(sy, "ld")
                S.dma(sy.dma_start(out=cnt8[:, :], in_=counts_all))
                S.pre(sy, "ld")
                S.dma(sy.dma_start(out=ebase_c[:, :], in_=ebase))
            ld_meta = S.n["ld"]

            with nc.psum_tensor("ps2a", [128, 512], F32) as psA:
                @blk.tensor
                def _(t):
                    t.wait_ge(S.h["ld"], ld_meta)
                    S.wait(t, "gcx")
                    t.matmul(psA[0:8, 0:E], triU8[:, :], cnt8[:, :], start=True, stop=True)
                    S.inc(t.drain(), "pe")

                @blk.vector
                def _(v):
                    v = _Chain(v, S, "dvx")
                    S.wait(v, "pe")
                    v.tensor_copy(offs[:, :], psA[0:8, 0:E])
                    S.inc(v.drain(), "dv")

                for c_ in range(8):
                    @blk.gpsimd
                    def _(g, c_=c_):
                        g = _Chain(g, S, "gcx")
                        S.wait(g, "dv")
                        S.pre(g, "gp")
                        S.dma(g.dma_start(out=colA[0:1, 0:E] if False else oh[0:1, :], in_=offs[c_:c_ + 1, :]), "gp")

                    @blk.tensor
                    def _(t, c_=c_):
                        S.wait(t, "gp")
                        S.wait(t, "dv")
                        t.matmul(psA[:, 0:E], ones_col[:, :], oh[0:1, :], start=True, stop=True)
                        S.inc(t.drain(), "pe")

                    @blk.vector
                    def _(v, c_=c_):
                        v = _Chain(v, S, "dvx")
                        S.wait(v, "pe")
                        v.tensor_copy(offb[:, c_, :], psA[:, 0:E])
                        S.inc(v.drain(), "dv")

            @blk.vector
            def _(v):
                v = _Chain(v, S, "dvx")
                S.wait(v, "dv")
                for n in range(NTL):
                    cg = n // 16
                    for k in range(2):
                        v.tensor_scalar(oh[:, :], iota64[:, :], meta_sb[:, n, k:k + 1], None, OP.is_equal)
                        v.tensor_tensor(oh[:, :], oh[:, :], offb[:, cg, :], OP.mult)
                        v.reduce_sum(colA[:, 0:1], oh[:, :], AX)
                        v.tensor_tensor(colA[:, 1:2], colA[:, 0:1], meta_sb[:, n, 4 + k:5 + k], OP.add)
                        v.tensor_scalar(colA[:, 2:3], colA[:, 1:2], 1024.0, 20000.0, OP.is_ge, OP.mult)
                        v.tensor_scalar(colA[:, 3:4], meta_sb[:, n, k:k + 1], ebase_c[:, 0:1], None, OP.subtract)
                        v.tensor_scalar(colA[:, 4:5], colA[:, 3:4], 0.0, None, OP.is_ge)
                        v.tensor_scalar(colA[:, 5:6], colA[:, 3:4], 7.0, None, OP.is_le)
                        v.tensor_tensor(colA[:, 4:5], colA[:, 4:5], colA[:, 5:6], OP.mult)
                        v.tensor_scalar(colA[:, 4:5], colA[:, 4:5], -100000.0, 100000.0, OP.mult, OP.add)
                        v.tensor_scalar(colA[:, 6:7], colA[:, 3:4], 1024.0, None, OP.mult)
                        v.tensor_tensor(colA[:, 6:7], colA[:, 6:7], colA[:, 1:2], OP.add)
                        v.tensor_tensor(colA[:, 6:7], colA[:, 6:7], colA[:, 2:3], OP.add)
                        v.tensor_tensor(colA[:, 6:7], colA[:, 6:7], colA[:, 4:5], OP.add)
                        v.tensor_copy(slots[:, n, k:k + 1], colA[:, 6:7])
                S.inc(v.drain(), "dv")
            dv_slots = S.n["dv"]

            zt = st.enter_context(nc.sbuf_tensor("zt", [128, C], BF16))

            @blk.gpsimd
            def _(g):
                g = _Chain(g, S, "gcx")
                g.memset(zt[:, :], 0.0)
                S.inc(g.drain(), "gc2" if False else "gcx")

            @blk.sync
            def _(sy):
                S.wait(sy, "gcx")
                S.dma(sy.dma_start(out=disp.rearrange("(j p) c -> p j c", p=128),
                                   in_=zt[:, :].rearrange("p (j c) -> p j c", j=1).to_broadcast([128, 64, C])
                                   if hasattr(zt[:, :], "to_broadcast") else zt[:, :]), "st")

            bc_reg = [None]

            @blk.gpsimd
            def _(g):
                r = g.alloc_register()
                g.reg_mov(r, 8 * CAP - 1)
                bc_reg[0] = r

            sct_marks = []
            for n in range(NTL):
                @blk.sync
                def _(sy, n=n):
                    if n >= 2:
                        sy.wait_ge(S.h[f"sct{n % 2}"], sct_marks[n - 2])
                    S.dma(sy.dma_start(out=xmt[:, n % 2, :],
                                       in_=xm_all.rearrange("(n p) c -> p n c", p=128)[:, n, :]),
                          f"lt{n % 2}")

                @blk.gpsimd
                def _(g, n=n):
                    g.wait_ge(S.h[f"lt{n % 2}"], S.n[f"lt{n % 2}"])
                    if n == 0:
                        g.wait_ge(S.h["dv"], dv_slots)
                        S.wait(g, "st")
                    for k in range(2):
                        S.dma(g.indirect_dma_start(
                            out=disp, out_offset=IndirectOffsetOnAxis(slots[:, n, k:k + 1], axis=0),
                            in_=xmt[:, n % 2, :], in_offset=None,
                            bounds_check=bc_reg[0], oob_is_err=False), f"sct{n % 2}")
                    sct_marks.append(S.n[f"sct{n % 2}"])

            disp_r = disp.rearrange("(e r) c -> e r c", e=8)
            for e in range(8):
                par = e % 2

                @blk.sync
                def _(sy, e=e, par=par):
                    sy.wait_ge(S.h["sct0"], S.n["sct0"])
                    sy.wait_ge(S.h["sct1"], S.n["sct1"])
                    if e >= 2:
                        S.wait(sy, "pe")
                    for cc in range(NCH):
                        S.pre(sy, f"tr{par}")
                        S.dma(sy.dma_start(out=bufT[:, par, cc, :],
                                           in_=disp_r[e, :, cc * 128:(cc + 1) * 128],
                                           transpose=True), f"tr{par}")

                @blk.gpsimd
                def _(g, e=e, par=par):
                    if e >= 2:
                        S.wait(g, "pe")
                    S.pre(g, f"wa{par}")
                    S.dma(g.dma_start(out=w13_sb[:, par, :, :],
                                      in_=w13[e, :, :].rearrange("(c p) f -> p c f", p=128)), f"wa{par}")
                    S.pre(g, f"wa{par}")
                    S.dma(g.dma_start(out=w2_sb[:, par, :, :],
                                      in_=w2w[e, :, :].rearrange("(c p) f -> p c f", p=128)), f"wa{par}")

                with nc.psum_tensor(f"psG{e}", [128, 1024], F32) as psG, \
                     nc.psum_tensor(f"psU{e}", [128, 1024], F32) as psU, \
                     nc.psum_tensor(f"psY{e}", [128, 1024], F32) as psY:
                    for it in range(4):
                        @blk.tensor
                        def _(t, e=e, par=par, it=it):
                            t.wait_ge(S.h[f"tr{par}"], S.n[f"tr{par}"])
                            t.wait_ge(S.h[f"wa{par}"], S.n[f"wa{par}"])
                            S.wait(t, "dv")
                            S.wait(t, "ac")
                            for cc in range(NCH):
                                for (a, b) in n512(0, CAP):
                                    t.matmul(psG[:, a:b], w13_sb[:, par, cc, it * 128:(it + 1) * 128],
                                             bufT[:, par, cc, a:b], start=(cc == 0), stop=(cc == NCH - 1))
                            for cc in range(NCH):
                                for (a, b) in n512(0, CAP):
                                    t.matmul(psU[:, a:b], w13_sb[:, par, cc, IDIM + it * 128:IDIM + (it + 1) * 128],
                                             bufT[:, par, cc, a:b], start=(cc == 0), stop=(cc == NCH - 1))
                            S.inc(t.drain(), "pe")

                        @blk.scalar
                        def _(sc, it=it):
                            sc = _Chain(sc, S, "acx")
                            S.wait(sc, "pe")
                            sc.activation(out=sgf[:, :], in_=psG[:, :], func=AF.Silu)
                            S.inc(sc.drain(), "ac")

                        @blk.vector
                        def _(v, it=it):
                            v = _Chain(v, S, "dvx")
                            S.wait(v, "ac")
                            v.tensor_tensor(hactT[:, it, :], sgf[:, :], psU[:, :], OP.mult)
                            S.inc(v.drain(), "dv")

                    for stt in range(8):
                        @blk.tensor
                        def _(t, e=e, par=par, stt=stt):
                            S.wait(t, "dv")
                            if stt >= 2:
                                S.wait(t, "dv")
                            for ic in range(4):
                                for (a, b) in n512(0, C):
                                    t.matmul(psY[:, a:b], hactT[:, ic, stt * 128:(stt + 1) * 128],
                                             w2_sb[:, par, ic, a:b], start=(ic == 0), stop=(ic == 3))
                            S.inc(t.drain(), "pe")

                        @blk.vector
                        def _(v, e=e, stt=stt):
                            v = _Chain(v, S, "dvx")
                            S.wait(v, "pe")
                            v.wait_ge(S.h[f"yb{stt % 2}"], S.n[f"yb{stt % 2}"])
                            v.tensor_copy(ybf[:, stt % 2, :], psY[:, :])
                            S.inc(v.drain(), "dv")

                        @blk.sync
                        def _(sy, e=e, stt=stt):
                            S.wait(sy, "dv")
                            S.dma(sy.dma_start(out=yb[e * CAP + stt * 128:e * CAP + (stt + 1) * 128, :],
                                               in_=ybf[:, stt % 2, :]), f"yb{stt % 2}")

            _fin2()



def _stage2_io(nc):
    io = {}
    io["xm_all"] = nc.dram_tensor("xm_all", [B * T, C], BF16, kind="ExternalInput").ap()
    io["meta_all"] = nc.dram_tensor("meta_all", [B * T, 6], F32, kind="ExternalInput").ap()
    io["counts_all"] = nc.dram_tensor("counts_all", [8, E], F32, kind="ExternalInput").ap()
    io["w13"] = nc.dram_tensor("w13", [8, C, 2 * IDIM], F32, kind="ExternalInput").ap()
    io["w2w"] = nc.dram_tensor("w2w", [8, IDIM, C], F32, kind="ExternalInput").ap()
    io["ebase"] = nc.dram_tensor("ebase", [128, 1], F32, kind="ExternalInput").ap()
    io["yb"] = nc.dram_tensor("yb", [8 * CAP, C], BF16, kind="ExternalOutput").ap()
    io["disp"] = nc.dram_tensor("disp", [8 * CAP, C], BF16, kind="Internal").ap()
    return io


def build_stage2():
    nc = bass.Bass("TRN2", target_bir_lowering=False, debug=False)
    io = _stage2_io(nc)
    st0 = ExitStack()
    with st0:
        S = Sems(st0, nc, SEM_NAMES)
        with nc.Block() as blk:
            emit_stage2(_PrefNC(nc, "s2_"), blk, S, io)
    return nc


def _prep_stage2_inputs(inputs, s1res):
    f32 = np.float32
    xm_all = np.concatenate([r["xm_out"] for r in s1res], axis=0)
    meta_all = np.concatenate([r["meta"] for r in s1res], axis=0).astype(f32)
    counts_all = np.concatenate([r["counts"] for r in s1res], axis=0).astype(f32)
    w13 = np.ascontiguousarray(inputs["L0_ffn_w13"], f32)
    w2w = np.ascontiguousarray(inputs["L0_ffn_w2"], f32)
    maps = []
    for i in range(8):
        maps.append({
            "xm_all": xm_all, "meta_all": meta_all, "counts_all": counts_all,
            "w13": np.ascontiguousarray(w13[8 * i:8 * i + 8]),
            "w2w": np.ascontiguousarray(w2w[8 * i:8 * i + 8]),
            "ebase": np.full((128, 1), 8.0 * i, f32),
        })
    return maps


def run_stage2(inputs, s1res, **kw):
    nc = build_stage2()
    maps = _prep_stage2_inputs(inputs, s1res)
    return run_bass_kernel_spmd(nc, maps, core_ids=list(range(8)), **kw)



def emit_stage3(nc, blk, S, io):
    partialT = io["partialT"]
    yb_all = io["yb_all"]
    meta_in = io["meta_in"]
    counts_all = io["counts_all"]
    bsel = io["bsel"]
    wqT = io["wqT"]
    wkT = io["wkT"]
    wv = io["wv"]
    wo = io["wo"]
    bo = io["bo"]
    wf2 = io["wf2"]
    bfq_col = io["bfq_col"]
    bf_row = io["bf_row"]
    g1r_R = io["g1r_R"]
    g2r_R = io["g2r_R"]
    g1r_I = io["g1r_I"]
    g2r_I = io["g2r_I"]
    bri_row = io["bri_row"]
    out = io["out"]
    a2R = io["a2R"]
    a2I = io["a2I"]
    st = ExitStack()
    with st:
        sbt = lambda *a: st.enter_context(nc.sbuf_tensor(*a))

        x1 = sbt("x1", [128, NT, C], F32)
        xn2T = sbt("xn2T", [128, NCH, T], BF16)
        ones_col = sbt("ones_col", [1, 128], F32)
        ones_r64 = sbt("ones_r64", [128, 128], F32)
        iden = sbt("iden", [128, 128], F32)
        iota64 = sbt("iota64", [128, E], F32)
        triU8 = sbt("triU8", [8, 8], F32)
        offb = sbt("offb", [128, E], F32)
        m2c = sbt("m2c", [128, NT], F32)
        r2c = sbt("r2c", [128, NT], F32)
        meta_sb = sbt("meta_sb", [128, NT, 6], F32)
        slots = sbt("slots", [128, NT, 2], I32)
        colA = sbt("colA", [128, 8], F32)
        oh = sbt("oh", [128, E], F32)
        kv_sb = sbt("kv_sb", [128, NH, 65], BF16)
        bo_b = sbt("bo_b", [128, C], F32)
        bf_b = sbt("bf_b", [128, C], F32)
        bri_b = sbt("bri_b", [128, C], F32)
        lex = sbt("lex", [128, T], F32)
        bfq_sb = sbt("bfq_sb", [128, 8], F32)

        with _passthrough(blk) as blk:
            def _fin3():
                @blk.gpsimd
                def _(g):
                    for nm in ("ld", "gp", "st", "g0", "g1", "lp0", "lp1",
                               "so0", "so1", "tx0", "tx1"):
                        S.wait(g, nm)
                    for nm in ("pe", "dv", "ac", "dvx", "acx", "gcx"):
                        S.wait(g, nm)
                    g.nop()

            @blk.gpsimd
            def _(g):
                g = _Chain(g, S, "gcx")
                g.memset(ones_col[:, :], 1.0)
                g.memset(ones_r64[:, :], 0.0)
                g.memset(ones_r64[64:65, :], 1.0)
                g.memset(iden[:, :], 1.0)
                g.affine_select(iden[:, :], iden[:, :], [[-1, 128]], OP.is_equal, 0.0,
                                base=0, channel_multiplier=1)
                g.iota(iota64[:, :], [[1, E]], base=0, channel_multiplier=0,
                       allow_small_or_imprecise_dtypes=True)
                g.memset(triU8[:, :], 1.0)
                g.affine_select(triU8[:, :], triU8[:, :], [[1, 8]], OP.is_ge, 0.0,
                                base=-1, channel_multiplier=-1)
                S.inc(g.drain(), "gcx")

            @blk.sync
            def _(sy):
                for (dst, srcap) in ((meta_sb[:, :, :], meta_in.rearrange("(n p) c -> p n c", p=128)),
                                     (colA[0:8, 0:1] , bsel), (oh[0:8, :], counts_all),
                                     (bo_b[0:1, :], bo), (bf_b[0:1, :], bf_row),
                                     (bri_b[0:1, :], bri_row),
                                     (bfq_sb[:, :], bfq_col)):
                    S.pre(sy, "ld")
                    S.dma(sy.dma_start(out=dst, in_=srcap), "ld")
            ld0 = S.n["ld"]

            with nc.psum_tensor("ps3o", [128, 1024], F32) as psO:
                @blk.tensor
                def _(t):
                    t.wait_ge(S.h["ld"], ld0)
                    S.wait(t, "gcx")
                    t.matmul(psO[0:8, 0:E], triU8[:, :], oh[0:8, :], start=True, stop=True)
                    S.inc(t.drain(), "pe")

                @blk.vector
                def _(v):
                    v = _Chain(v, S, "dvx")
                    S.wait(v, "pe")
                    v.tensor_copy(oh[0:8, :], psO[0:8, 0:E])
                    S.inc(v.drain(), "dv")

                @blk.tensor
                def _(t):
                    S.wait(t, "dv")
                    t.matmul(psO[0:1, 0:E], colA[0:8, 0:1], oh[0:8, :], start=True, stop=True)
                    S.inc(t.drain(), "pe")

                @blk.vector
                def _(v):
                    v = _Chain(v, S, "dvx")
                    S.wait(v, "pe")
                    v.tensor_copy(oh[0:1, :], psO[0:1, 0:E])
                    S.inc(v.drain(), "dv")

                @blk.tensor
                def _(t):
                    S.wait(t, "dv")
                    t.matmul(psO[:, 0:E], ones_col[:, :], oh[0:1, :], start=True, stop=True)
                    for (a, b) in n512(0, C):
                        t.matmul(psO[:, 512:1024] if False else psO[:, a:b], ones_col[:, :], bo_b[0:1, a:b],
                                 start=True, stop=True) if False else None
                    S.inc(t.drain(), "pe")

                @blk.vector
                def _(v):
                    v = _Chain(v, S, "dvx")
                    S.wait(v, "pe")
                    v.tensor_copy(offb[:, :], psO[:, 0:E])
                    S.inc(v.drain(), "dv")

                for (rowt, bt) in ((bo_b, bo_b), (bf_b, bf_b), (bri_b, bri_b)):
                    @blk.tensor
                    def _(t, rowt=rowt):
                        S.wait(t, "dv")
                        for (a, b) in n512(0, C):
                            t.matmul(psO[:, a:b], ones_col[:, :], rowt[0:1, a:b], start=True, stop=True)
                        S.inc(t.drain(), "pe")

                    @blk.vector
                    def _(v, bt=bt):
                        v = _Chain(v, S, "dvx")
                        S.wait(v, "pe")
                        v.tensor_copy(bt[:, :], psO[:, 0:C])
                        S.inc(v.drain(), "dv")

            @blk.vector
            def _(v):
                v = _Chain(v, S, "dvx")
                S.wait(v, "dv")
                for n in range(NT):
                    for k in range(2):
                        v.tensor_scalar(oh[:, :], iota64[:, :], meta_sb[:, n, k:k + 1], None, OP.is_equal)
                        v.tensor_tensor(oh[:, :], oh[:, :], offb[:, :], OP.mult)
                        v.reduce_sum(colA[:, 0:1], oh[:, :], AX)
                        v.tensor_tensor(colA[:, 1:2], colA[:, 0:1], meta_sb[:, n, 4 + k:5 + k], OP.add)
                        v.tensor_scalar(colA[:, 2:3], colA[:, 1:2], 1024.0, 70000.0, OP.is_ge, OP.mult)
                        v.tensor_scalar(colA[:, 3:4], meta_sb[:, n, k:k + 1], 1024.0, None, OP.mult)
                        v.tensor_tensor(colA[:, 3:4], colA[:, 3:4], colA[:, 1:2], OP.add)
                        v.tensor_tensor(colA[:, 3:4], colA[:, 3:4], colA[:, 2:3], OP.add)
                        v.tensor_copy(slots[:, n, k:k + 1], colA[:, 3:4])
                S.inc(v.drain(), "dv")
            dv_slots3 = S.n["dv"]

            with ExitStack() as ph:
                _e = ph.enter_context
                ptt = _e(nc.sbuf_tensor("ptt", [128, 2, T], F32))
                yg = _e(nc.sbuf_tensor("yg", [128, 2, 2, C], BF16))
                psT3 = _e(nc.psum_tensor("ps3t", [128, 2048], F32))
                bc3 = [None]

                @blk.gpsimd
                def _(g):
                    r = g.alloc_register()
                    g.reg_mov(r, E * CAP - 1)
                    bc3[0] = r

                tmarks = {}
                for cc in range(NCH):
                    par = cc % 2

                    @blk.sync
                    def _(sy, cc=cc, par=par):
                        if cc >= 2:
                            S.wait(sy, "pe", tmarks[cc - 2])
                        S.pre(sy, f"lp{par}")
                        S.dma(sy.dma_start(out=ptt[:, par, :],
                                           in_=partialT[cc * 128:(cc + 1) * 128, :]), f"lp{par}")

                    @blk.tensor
                    def _(t, cc=cc, par=par):
                        t.wait_ge(S.h[f"lp{par}"], S.n[f"lp{par}"])
                        if cc == 0:
                            S.wait(t, "dv")
                        if cc >= 1:
                            S.wait(t, "dv")
                        for n in range(NT):
                            t.transpose(psT3[:, n * 128:(n + 1) * 128],
                                        ptt[:, par, n * 128:(n + 1) * 128], iden[:, :])
                        S.inc(t.drain(), "pe")
                        tmarks[cc] = S.n["pe"]

                    @blk.vector
                    def _(v, cc=cc):
                        v = _Chain(v, S, "dvx")
                        S.wait(v, "pe")
                        for n in range(NT):
                            v.tensor_copy(x1[:, n, cc * 128:(cc + 1) * 128],
                                          psT3[:, n * 128:(n + 1) * 128])
                        S.inc(v.drain(), "dv")

                gmarks = []
                for n in range(NT):
                    par = n % 2

                    @blk.gpsimd
                    def _(g, n=n, par=par):
                        g = _Chain(g, S, "gcx")
                        if n == 0:
                            g.wait_ge(S.h["dv"], dv_slots3)
                            S.wait(g, "dv")
                        if n >= 2:
                            S.wait(g, "dv", gmarks[n - 2])
                        g.memset(yg[:, par, 0, :], 0.0)
                        g.memset(yg[:, par, 1, :], 0.0)
                        gd = g.drain()
                        S.inc(gd, "gcx")
                        g.wait_ge(S.h["gcx"], S.n["gcx"])
                        for k in range(2):
                            S.dma(g.indirect_dma_start(
                                out=yg[:, par, k, :], out_offset=None,
                                in_=yb_all,
                                in_offset=IndirectOffsetOnAxis(slots[:, n, k:k + 1], axis=0),
                                bounds_check=bc3[0], oob_is_err=False), f"g{par}")

                    @blk.vector
                    def _(v, n=n, par=par):
                        v = _Chain(v, S, "dvx")
                        v.wait_ge(S.h[f"g{par}"], S.n[f"g{par}"])
                        v.tensor_scalar(lex[:, 0:C], yg[:, par, 0, :], meta_sb[:, n, 2:3], None, OP.mult)
                        v.scalar_tensor_tensor(lex[:, C:2 * C] if False else lex[:, 0:C],
                                               yg[:, par, 1, :], meta_sb[:, n, 3:4],
                                               lex[:, 0:C], OP.mult, OP.add)
                        v.tensor_tensor(x1[:, n, :], x1[:, n, :], lex[:, 0:C], OP.add)
                        S.inc(v.drain(), "dv")
                        gmarks.append(S.n["dv"])

            with ExitStack() as ph:
                _e = ph.enter_context
                sq3 = _e(nc.sbuf_tensor("sq3", [128, C], F32))
                ssq3 = _e(nc.sbuf_tensor("ssq3", [128, NT], F32))
                xnt3 = _e(nc.sbuf_tensor("xnt3", [128, 2, C], BF16))

                for n in range(NT):
                    par = n % 2

                    @blk.scalar
                    def _(sc, n=n):
                        sc = _Chain(sc, S, "acx")
                        if n == 0:
                            S.wait(sc, "dv")
                        sc.activation(out=sq3[:, :], in_=x1[:, n, :], func=AF.Square,
                                      accum_out=ssq3[:, n:n + 1])
                        S.inc(sc.drain(), "ac")

                    @blk.vector
                    def _(v, n=n, par=par):
                        v = _Chain(v, S, "dvx")
                        S.wait(v, "ac")
                        if n >= 2:
                            v.wait_ge(S.h[f"tx{par}"], S.n[f"tx{par}"])
                        c0 = slice(n, n + 1)
                        v.reduce_sum(m2c[:, c0], x1[:, n, :], AX)
                        v.tensor_scalar(m2c[:, c0], m2c[:, c0], 1.0 / C, None, OP.mult)
                        v.tensor_scalar(ssq3[:, c0], ssq3[:, c0], 1.0 / C, None, OP.mult)
                        v.tensor_tensor(r2c[:, c0], m2c[:, c0], m2c[:, c0], OP.mult)
                        v.tensor_tensor(ssq3[:, c0], ssq3[:, c0], r2c[:, c0], OP.subtract)
                        v.tensor_scalar(r2c[:, c0], ssq3[:, c0], EPS_LN, None, OP.add)
                        S.inc(v.drain(), "dv")

                    @blk.scalar
                    def _(sc, n=n):
                        sc = _Chain(sc, S, "acx")
                        S.wait(sc, "dv")
                        c0 = slice(n, n + 1)
                        sc.activation(out=r2c[:, c0], in_=r2c[:, c0], func=AF.Ln)
                        sc.activation(out=r2c[:, c0], in_=r2c[:, c0], func=AF.Exp, scale=-0.5)
                        S.inc(sc.drain(), "ac")

                    @blk.vector
                    def _(v, n=n, par=par):
                        v = _Chain(v, S, "dvx")
                        S.wait(v, "ac")
                        c0 = slice(n, n + 1)
                        v.tensor_scalar(xnt3[:, par, :], x1[:, n, :], m2c[:, c0], r2c[:, c0],
                                        OP.subtract, OP.mult)
                        S.inc(v.drain(), "dv")

                    @blk.sync
                    def _(sy, n=n, par=par):
                        S.wait(sy, "dv")
                        for cc in range(NCH):
                            S.pre(sy, f"tx{par}")
                            S.dma(sy.dma_start(out=xn2T[:, cc, n * 128:(n + 1) * 128],
                                               in_=xnt3[:, par, cc * 128:(cc + 1) * 128],
                                               transpose=True), f"tx{par}")

            def fold_wf(wsrc, wdst, wfd, psF):
                @blk.sync
                def _(sy):
                    sy.wait_ge(S.h["pe"], S.n["pe"])
                    sy.wait_ge(S.h["dv"], S.n["dv"])
                    S.pre(sy, "ld")
                    S.dma(sy.dma_start(out=wtf[:, :, :],
                                       in_=wsrc.rearrange("(c p) q -> p c q", p=128)), "ld")
                ldw = S.n["ld"]

                for ct in range(NCH):
                    @blk.tensor
                    def _(t, ct=ct, ldw=ldw):
                        t.wait_ge(S.h["ld"], ldw)
                        S.wait(t, "dv")
                        for h in [x for x in range(NH) if x % 2 == 0] + \
                                 [x for x in range(NH) if x % 2 == 1]:
                            t.matmul(psF[:, h * 64:(h + 1) * 64],
                                     wtf[(h % 2) * 64:(h % 2) * 64 + 64, h // 2, ct * 128:(ct + 1) * 128],
                                     wfd[(h % 2) * 64:(h % 2) * 64 + 64, 0:64], start=True, stop=True)
                        S.inc(t.drain(), "pe")

                    @blk.vector
                    def _(v, ct=ct):
                        v = _Chain(v, S, "dvx")
                        S.wait(v, "pe")
                        v.tensor_copy(wdst[:, ct, :], psF[:, 0:C])
                        S.inc(v.drain(), "dv")

            wfd = st.enter_context(nc.sbuf_tensor("wfd", [128, 64], F32))
            wkf_sb = None

            @blk.sync
            def _(sy):
                S.pre(sy, "ld")
                S.dma(sy.dma_start(out=wfd[:, :], in_=wf2), "ld")

            with ExitStack() as phK:
                _eK = phK.enter_context
                kf = _eK(nc.sbuf_tensor("kf", [128, NT, C], BF16))
                psK = _eK(nc.psum_tensor("ps3k", [128, 1024], F32))

                with ExitStack() as phW:
                    wtf = phW.enter_context(nc.sbuf_tensor("wtf", [128, NCH, C], F32))
                    wkf_sb = phW.enter_context(nc.sbuf_tensor("wkf_sb", [128, NCH, C], BF16))
                    fold_wf(wkT, wkf_sb, wfd, psK)

                    for n in range(NT):
                        @blk.tensor
                        def _(t, n=n):
                            S.wait(t, "dv")
                            for cc in range(NCH):
                                for (a, b) in n512(0, C):
                                    t.matmul(psK[:, a:b], xn2T[:, cc, n * 128:(n + 1) * 128],
                                             wkf_sb[:, cc, a:b], start=(cc == 0), stop=(cc == NCH - 1))
                            S.inc(t.drain(), "pe")

                        @blk.vector
                        def _(v, n=n):
                            v = _Chain(v, S, "dvx")
                            S.wait(v, "pe")
                            if n == 0:
                                v.wait_ge(S.h["tx0"], S.n["tx0"])
                                v.wait_ge(S.h["tx1"], S.n["tx1"])
                            v.tensor_tensor(lex[:, 0:C], psK[:, 0:C], bf_b[:, :], OP.add)
                            v.tensor_scalar(kf[:, n, :], lex[:, 0:C], 0.0, None, OP.max)
                            S.inc(v.drain(), "dv")

                with ExitStack() as phV:
                    _eV = phV.enter_context
                    v65 = _eV(nc.sbuf_tensor("v65", [128, NT, NH * 65], BF16))

                    with ExitStack() as phWv:
                        wv_sb = phWv.enter_context(nc.sbuf_tensor("wv_sb", [128, NCH, C], BF16))

                        @blk.gpsimd
                        def _(g):
                            g = _Chain(g, S, "gcx")
                            S.wait(g, "pe")
                            S.wait(g, "dv")
                            for h in range(NH):
                                g.memset(v65[:, :, h * 65 + 64:h * 65 + 65], 1.0)
                            S.pre(g, "gp")
                            S.dma(g.dma_start(out=wv_sb[:, :, :],
                                              in_=wv.rearrange("(c p) q -> p c q", p=128)), "gp")
                        gp_wv = S.n["gp"]

                        for n in range(NT):
                            @blk.tensor
                            def _(t, n=n):
                                S.wait(t, "dv")
                                if n == 0:
                                    t.wait_ge(S.h["gp"], gp_wv)
                                    S.wait(t, "gcx")
                                for cc in range(NCH):
                                    for (a, b) in n512(0, C):
                                        t.matmul(psK[:, a:b], xn2T[:, cc, n * 128:(n + 1) * 128],
                                                 wv_sb[:, cc, a:b], start=(cc == 0), stop=(cc == NCH - 1))
                                S.inc(t.drain(), "pe")

                            @blk.vector
                            def _(v, n=n):
                                v = _Chain(v, S, "dvx")
                                S.wait(v, "pe")
                                v.tensor_copy(
                                    v65[:, n, :].rearrange("p (h x) -> p h x", h=NH)[:, :, 0:64],
                                    psK[:, 0:C].rearrange("p (h x) -> p h x", h=NH))
                                S.inc(v.drain(), "dv")

                    with nc.psum_tensor("ps3kv", [128, 2048], F32) as psKV:
                        @blk.tensor
                        def _(t):
                            S.wait(t, "dv")
                            for h in range(NH):
                                for n in range(NT):
                                    t.matmul(psKV[0:64, h * 128:h * 128 + 65],
                                             kf[:, n, h * 64:(h + 1) * 64],
                                             v65[:, n, h * 65:h * 65 + 65],
                                             start=(n == 0), stop=(n == NT - 1))
                            for h in range(NH):
                                for n in range(NT):
                                    t.matmul(psKV[64:128, h * 128:h * 128 + 65],
                                             kf[:, n, h * 64:(h + 1) * 64],
                                             v65[:, n, h * 65:h * 65 + 65],
                                             start=(n == 0), stop=(n == NT - 1))
                            S.inc(t.drain(), "pe")

                        @blk.vector
                        def _(v):
                            v = _Chain(v, S, "dvx")
                            S.wait(v, "pe")
                            v.tensor_copy(kv_sb[:, :, 0:65],
                                          psKV[:, 0:2048].rearrange("p (h x) -> p h x", h=NH)[:, :, 0:65])
                            S.inc(v.drain(), "dv")

            with ExitStack() as ph2:
                _e2 = ph2.enter_context
                psQ3 = _e2(nc.psum_tensor("ps3q", [128, 1024], F32))
                psL = _e2(nc.psum_tensor("ps3l", [65, 2048], F32))

                qfT = _e2(nc.sbuf_tensor("qfT", [128, NCH, T], BF16))
                with ExitStack() as phW:
                    wqf_sb = phW.enter_context(nc.sbuf_tensor("wqf_sb", [128, NCH, C], BF16))
                    with ExitStack() as phW2:
                        wtf = phW2.enter_context(nc.sbuf_tensor("wtf2", [128, NCH, C], F32))
                        fold_wf(wqT, wqf_sb, wfd, psQ3)

                    for fc in range(NCH):
                        for half in range(2):
                            @blk.tensor
                            def _(t, fc=fc, half=half):
                                S.wait(t, "dv")
                                S.wait(t, "ac")
                                for cc in range(NCH):
                                    for (a, b) in n512(0, 1024):
                                        t.matmul(psQ3[:, a:b], wqf_sb[:, cc, fc * 128:(fc + 1) * 128],
                                                 xn2T[:, cc, half * 1024 + a:half * 1024 + b],
                                                 start=(cc == 0), stop=(cc == NCH - 1))
                                S.inc(t.drain(), "pe")

                            @blk.scalar
                            def _(sc, fc=fc, half=half):
                                sc = _Chain(sc, S, "acx")
                                S.wait(sc, "pe")
                                sc.activation(out=qfT[:, fc, half * 1024:(half + 1) * 1024],
                                              in_=psQ3[:, :], func=AF.Relu,
                                              bias=bfq_sb[:, fc:fc + 1])
                                S.inc(sc.drain(), "ac")

                attn1nT = _e2(nc.sbuf_tensor("attn1nT", [128, NCH, T], BF16))
                aun = lex[0:64, :]
                for h in range(NH):
                    r0 = (h % 2) * 64

                    @blk.tensor
                    def _(t, h=h, r0=r0):
                        S.wait(t, "ac")
                        S.wait(t, "dv")
                        for (a, b) in n512(0, T):
                            t.matmul(psL[:, a:b], kv_sb[r0:r0 + 64, h, 0:65],
                                     qfT[r0:r0 + 64, h // 2, a:b], start=True, stop=True)
                        S.inc(t.drain(), "pe")

                    @blk.vector
                    def _(v, h=h):
                        v = _Chain(v, S, "dvx")
                        S.wait(v, "pe")
                        v.tensor_scalar(lex[64:65, :], psL[64:65, :], 1e-6, None, OP.max)
                        v.tensor_copy(aun[:, :], psL[0:64, :])
                        S.inc(v.drain(), "dv")

                    @blk.scalar
                    def _(sc, h=h):
                        sc = _Chain(sc, S, "acx")
                        S.wait(sc, "dv")
                        sc.activation(out=lex[64:65, :], in_=lex[64:65, :], func=AF.Ln)
                        sc.activation(out=lex[64:65, :], in_=lex[64:65, :], func=AF.Exp, scale=-1.0)
                        S.inc(sc.drain(), "ac")

                    for half in range(2):
                        @blk.tensor
                        def _(t, h=h, half=half):
                            S.wait(t, "ac")
                            S.wait(t, "dv")
                            for (a, b) in n512(half * 1024, (half + 1) * 1024):
                                t.matmul(psQ3[:, a - half * 1024:b - half * 1024],
                                         ones_r64[64:65, :], lex[64:65, a:b], start=True, stop=True)
                            S.inc(t.drain(), "pe")

                        @blk.vector
                        def _(v, h=h, half=half):
                            v = _Chain(v, S, "dvx")
                            S.wait(v, "pe")
                            v.tensor_tensor(
                                attn1nT[(h % 2) * 64:(h % 2) * 64 + 64, h // 2,
                                        half * 1024:(half + 1) * 1024],
                                aun[:, half * 1024:(half + 1) * 1024],
                                psQ3[0:64, 0:1024], OP.mult)
                            S.inc(v.drain(), "dv")

                with nc.sbuf_tensor("wo_sb", [128, NCH, C], BF16) as wo_sb:
                    @blk.gpsimd
                    def _(g):
                        S.wait(g, "pe")
                        S.wait(g, "dv")
                        S.pre(g, "gp")
                        S.dma(g.dma_start(out=wo_sb[:, :, :],
                                          in_=wo.rearrange("(c p) q -> p c q", p=128)), "gp")
                    gp_wo = S.n["gp"]

                    for n in range(NT):
                        @blk.tensor
                        def _(t, n=n):
                            t.wait_ge(S.h["gp"], gp_wo)
                            S.wait(t, "dv")
                            for cc in range(NCH):
                                for (a, b) in n512(0, C):
                                    t.matmul(psQ3[:, a:b], attn1nT[:, cc, n * 128:(n + 1) * 128],
                                             wo_sb[:, cc, a:b], start=(cc == 0), stop=(cc == NCH - 1))
                            S.inc(t.drain(), "pe")

                        @blk.vector
                        def _(v, n=n):
                            v = _Chain(v, S, "dvx")
                            S.wait(v, "pe")
                            v.tensor_tensor(lex[:, 0:C], psQ3[:, 0:C], bo_b[:, :], OP.add)
                            v.tensor_tensor(x1[:, n, :], x1[:, n, :], lex[:, 0:C], OP.add)
                            S.inc(v.drain(), "dv")

            with ExitStack() as ph:
                _e = ph.enter_context
                g1t = _e(nc.sbuf_tensor("g1t", [4, 1024], F32))
                g2t = _e(nc.sbuf_tensor("g2t", [4, 256], F32))
                wsb = _e(nc.sbuf_tensor("wsb", [128, 256], F32))
                wrT = _e(nc.sbuf_tensor("wrT", [128, 4, IDIM], BF16))
                wiT = _e(nc.sbuf_tensor("wiT", [128, 4, IDIM], BF16))
                winT = _e(nc.sbuf_tensor("winT", [128, 4, IDIM], BF16))
                psW = _e(nc.psum_tensor("ps3w", [128, 512], F32))
                psR = _e(nc.psum_tensor("ps3r", [128, 512], F32))
                psI = _e(nc.psum_tensor("ps3i", [128, 512], F32))

                for (g1s, g2s, a2d, wdst) in ((g1r_R, g2r_R, a2R, wrT), (g1r_I, g2r_I, a2I, wiT)):
                    @blk.sync
                    def _(sy, g1s=g1s, g2s=g2s):
                        sy.wait_ge(S.h["pe"], S.n["pe"])
                        sy.wait_ge(S.h["dv"], S.n["dv"])
                        S.pre(sy, "ld")
                        S.dma(sy.dma_start(out=g1t[:, :], in_=g1s), "ld")
                        S.pre(sy, "ld")
                        S.dma(sy.dma_start(out=g2t[:, :], in_=g2s), "ld")
                    ldg = S.n["ld"]

                    for bt in range(8):
                        @blk.tensor
                        def _(t, bt=bt, ldg=ldg):
                            t.wait_ge(S.h["ld"], ldg)
                            S.wait(t, "st")
                            t.matmul(psW[:, 0:256], g1t[:, bt * 128:(bt + 1) * 128], g2t[:, :],
                                     start=True, stop=True)
                            S.inc(t.drain(), "pe")

                        @blk.vector
                        def _(v, bt=bt):
                            v = _Chain(v, S, "dvx")
                            S.wait(v, "pe")
                            v.tensor_copy(wsb[:, :], psW[:, 0:256])
                            S.inc(v.drain(), "dv")

                        @blk.sync
                        def _(sy, bt=bt, a2d=a2d):
                            S.wait(sy, "dv")
                            for bb in range(4):
                                bglob = bt * 4 + bb
                                S.pre(sy, "st")
                                S.dma(sy.dma_start(
                                    out=a2d.rearrange("(d g) (b e) -> d g b e", g=16, b=32)[:, :, bglob, :],
                                    in_=wsb[bb * 32:(bb + 1) * 32, :]), "st")

                @blk.gpsimd
                def _(g):
                    S.wait(g, "st")
                    S.pre(g, "gp")
                    S.dma(g.dma_start(out=wrT[:, :, :], in_=a2R.rearrange("(c p) q -> p c q", p=128)), "gp")
                    S.pre(g, "gp")
                    S.dma(g.dma_start(out=wiT[:, :, :], in_=a2I.rearrange("(c p) q -> p c q", p=128)), "gp")
                gp_w = S.n["gp"]

                @blk.vector
                def _(v):
                    v = _Chain(v, S, "dvx")
                    v.wait_ge(S.h["gp"], gp_w)
                    for ic in range(4):
                        v.tensor_scalar(winT[:, ic, :], wiT[:, ic, :], -1.0, None, OP.mult)
                    S.inc(v.drain(), "dv")
                dv_w = S.n["dv"]

                somarks = {}
                for n in range(NT):
                    par = n % 2

                    @blk.tensor
                    def _(t, n=n):
                        t.wait_ge(S.h["dv"], dv_w)
                        S.wait(t, "dv")
                        for ic in range(4):
                            t.matmul(psR[:, 0:512], xn2T[:, ic, n * 128:(n + 1) * 128],
                                     wrT[:, ic, :], start=(ic == 0), stop=False)
                        for ic in range(4):
                            t.matmul(psR[:, 0:512], xn2T[:, 4 + ic, n * 128:(n + 1) * 128],
                                     winT[:, ic, :], start=False, stop=(ic == 3))
                        for ic in range(4):
                            t.matmul(psI[:, 0:512], xn2T[:, ic, n * 128:(n + 1) * 128],
                                     wiT[:, ic, :], start=(ic == 0), stop=False)
                        for ic in range(4):
                            t.matmul(psI[:, 0:512], xn2T[:, 4 + ic, n * 128:(n + 1) * 128],
                                     wrT[:, ic, :], start=False, stop=(ic == 3))
                        S.inc(t.drain(), "pe")

                    @blk.vector
                    def _(v, n=n, par=par):
                        v = _Chain(v, S, "dvx")
                        S.wait(v, "pe")
                        if n >= 2:
                            v.wait_ge(S.h[f"so{par}"], somarks[n - 2])
                        v.tensor_tensor(lex[:, 0:512], psR[:, 0:512], bri_b[:, 0:512], OP.add)
                        v.tensor_tensor(x1[:, n, 0:512], x1[:, n, 0:512], lex[:, 0:512], OP.add)
                        v.tensor_tensor(lex[:, 512:1024], psI[:, 0:512], bri_b[:, 512:1024], OP.add)
                        v.tensor_tensor(x1[:, n, 512:1024], x1[:, n, 512:1024], lex[:, 512:1024], OP.add)
                        S.inc(v.drain(), "dv")

                    @blk.sync
                    def _(sy, n=n, par=par):
                        S.wait(sy, "dv")
                        S.dma(sy.dma_start(out=out.rearrange("(n p) c -> p n c", p=128)[:, n, :],
                                           in_=x1[:, n, :]), f"so{par}")
                        somarks[n] = S.n[f"so{par}"]

            _fin3()



def _stage3_io(nc):
    io = {}
    io["partialT"] = nc.dram_tensor("partialT", [C, T], F32, kind="ExternalInput").ap()
    io["yb_all"] = nc.dram_tensor("yb_all", [E * CAP, C], BF16, kind="ExternalInput").ap()
    io["meta_in"] = nc.dram_tensor("meta_in", [T, 6], F32, kind="ExternalInput").ap()
    io["counts_all"] = nc.dram_tensor("counts_all", [8, E], F32, kind="ExternalInput").ap()
    io["bsel"] = nc.dram_tensor("bsel", [8, 1], F32, kind="ExternalInput").ap()
    io["wqT"] = nc.dram_tensor("wqT", [C, C], F32, kind="ExternalInput").ap()
    io["wkT"] = nc.dram_tensor("wkT", [C, C], F32, kind="ExternalInput").ap()
    io["wv"] = nc.dram_tensor("wv", [C, C], F32, kind="ExternalInput").ap()
    io["wo"] = nc.dram_tensor("wo", [C, C], F32, kind="ExternalInput").ap()
    io["bo"] = nc.dram_tensor("bo", [1, C], F32, kind="ExternalInput").ap()
    io["wf2"] = nc.dram_tensor("wf2", [128, 64], F32, kind="ExternalInput").ap()
    io["bfq_col"] = nc.dram_tensor("bfq_col", [128, 8], F32, kind="ExternalInput").ap()
    io["bf_row"] = nc.dram_tensor("bf_row", [1, C], F32, kind="ExternalInput").ap()
    io["g1r_R"] = nc.dram_tensor("g1r_R", [4, 1024], F32, kind="ExternalInput").ap()
    io["g2r_R"] = nc.dram_tensor("g2r_R", [4, 256], F32, kind="ExternalInput").ap()
    io["g1r_I"] = nc.dram_tensor("g1r_I", [4, 1024], F32, kind="ExternalInput").ap()
    io["g2r_I"] = nc.dram_tensor("g2r_I", [4, 256], F32, kind="ExternalInput").ap()
    io["bri_row"] = nc.dram_tensor("bri_row", [1, C], F32, kind="ExternalInput").ap()
    io["out"] = nc.dram_tensor("out", [T, C], F32, kind="ExternalOutput").ap()
    io["a2R"] = nc.dram_tensor("a2R", [IDIM, IDIM], F32, kind="Internal").ap()
    io["a2I"] = nc.dram_tensor("a2I", [IDIM, IDIM], F32, kind="Internal").ap()
    return io


def build_stage3():
    nc = bass.Bass("TRN2", target_bir_lowering=False, debug=False)
    io = _stage3_io(nc)
    st0 = ExitStack()
    with st0:
        S = Sems(st0, nc, SEM_NAMES)
        with nc.Block() as blk:
            emit_stage3(_PrefNC(nc, "s3_"), blk, S, io)
    return nc


def _prep_stage3_inputs(inputs, s1res, s2res):
    f32 = np.float32
    yb_all = np.concatenate([r["yb"] for r in s2res], axis=0)
    counts_all = np.concatenate([r["counts"] for r in s1res], axis=0).astype(f32)
    Wf = np.asarray(inputs["L1_attn_Wf"], f32)
    bf = np.asarray(inputs["L1_attn_bf"], f32)
    G1R = np.asarray(inputs["L1_ffn_G1_R"], f32)[0]
    G2R = np.asarray(inputs["L1_ffn_G2_R"], f32)
    G1I = np.asarray(inputs["L1_ffn_G1_I"], f32)[0]
    G2I = np.asarray(inputs["L1_ffn_G2_I"], f32)
    com = {
        "yb_all": yb_all,
        "counts_all": counts_all,
        "wqT": np.ascontiguousarray(np.asarray(inputs["L1_attn_Wq"], f32).T),
        "wkT": np.ascontiguousarray(np.asarray(inputs["L1_attn_Wk"], f32).T),
        "wv": np.ascontiguousarray(inputs["L1_attn_Wv"], f32),
        "wo": np.ascontiguousarray(inputs["L1_attn_Wo"], f32),
        "bo": np.ascontiguousarray(np.asarray(inputs["L1_attn_bo"], f32).reshape(1, -1)),
        "wf2": np.ascontiguousarray(np.concatenate([Wf, Wf], axis=0)),
        "bfq_col": np.ascontiguousarray(np.tile(bf, 16).reshape(8, 128).T),
        "bf_row": np.ascontiguousarray(np.tile(bf, 16).reshape(1, -1)),
        "g1r_R": np.ascontiguousarray(G1R.transpose(1, 0, 2).reshape(4, 1024)),
        "g2r_R": np.ascontiguousarray(G2R[:, :, 0, :].transpose(0, 2, 1).reshape(4, 256)),
        "g1r_I": np.ascontiguousarray(G1I.transpose(1, 0, 2).reshape(4, 1024)),
        "g2r_I": np.ascontiguousarray(G2I[:, :, 0, :].transpose(0, 2, 1).reshape(4, 256)),
        "bri_row": np.ascontiguousarray(np.concatenate(
            [np.asarray(inputs["L1_ffn_bias_real"], f32),
             np.asarray(inputs["L1_ffn_bias_imag"], f32)]).reshape(1, -1)),
    }
    maps = []
    for b in range(8):
        m = dict(com)
        m["partialT"] = np.ascontiguousarray(s1res[b]["partial"])
        m["meta_in"] = np.ascontiguousarray(s1res[b]["meta"].astype(f32))
        sel = np.zeros((8, 1), f32)
        sel[b, 0] = 1.0
        m["bsel"] = sel
        maps.append(m)
    return maps


def run_stage3(inputs, s1res, s2res, **kw):
    nc = build_stage3()
    maps = _prep_stage3_inputs(inputs, s1res, s2res)
    return run_bass_kernel_spmd(nc, maps, core_ids=list(range(8)), **kw)



def build_fused():
    nc = bass.Bass("TRN2", target_bir_lowering=False, debug=False)
    E_IN = lambda n, s, d: nc.dram_tensor(n, s, d, kind="ExternalInput").ap()
    INT = lambda n, s, d: nc.dram_tensor(n, s, d, kind="Internal").ap()

    io1 = {
        "xT": E_IN("xT", [C, T], F32), "xtok": E_IN("xtok", [T, C], F32),
        "w_attn": E_IN("w_attn", [C, 3 * C], F32), "w_attnT": E_IN("w_attnT", [3 * C, C], F32),
        "b_attn": E_IN("b_attn", [1, 3 * C], F32), "w_proj": E_IN("w_proj", [C, C], F32),
        "b_proj": E_IN("b_proj", [1, C], F32), "bproj_col": E_IN("bproj_col", [128, NCH], F32),
        "wk_cT": E_IN("wk_cT", [32, 64], F32), "wk_e": E_IN("wk_e", [32, 64], F32),
        "wv_cT": E_IN("wv_cT", [32, 64], F32), "wv_e": E_IN("wv_e", [32, 64], F32),
        "rms_w": E_IN("rms_w", [1, C], F32), "rms_w_col": E_IN("rms_w_col", [128, NCH], F32),
        "battn_col": E_IN("battn_col", [128, 24], F32), "gate_wT": E_IN("gate_wT", [C, E], F32),
        "sw_gate": E_IN("sw_gate", [C, IDIM], F32), "sw_up": E_IN("sw_up", [C, IDIM], F32),
        "sw_down": E_IN("sw_down", [IDIM, C], F32),
    }
    partial_i = INT("fz_partial", [C, T], F32)
    xm_loc = INT("fz_xm_loc", [T, C], BF16)
    meta_loc = INT("fz_meta_loc", [T, 6], F32)
    counts_loc = INT("fz_counts_loc", [1, E], F32)
    io1["partial"] = partial_i
    io1["xm_out"] = xm_loc
    io1["meta"] = meta_loc
    io1["counts"] = counts_loc

    xm_all = INT("fz_xm_all", [B * T, C], BF16)
    meta_all = INT("fz_meta_all", [B * T, 6], F32)
    counts_all8 = INT("fz_counts_all", [8, E], F32)
    yb_loc = INT("fz_yb_loc", [8 * CAP, C], BF16)
    yb_all = INT("fz_yb_all", [E * CAP, C], BF16)

    io2 = {
        "xm_all": xm_all,
        "meta_all": meta_all,
        "counts_all": counts_all8,
        "w13": E_IN("w13", [8, C, 2 * IDIM], F32),
        "w2w": E_IN("w2w", [8, IDIM, C], F32),
        "ebase": E_IN("ebase", [128, 1], F32),
        "yb": yb_loc,
        "disp": INT("fz_disp", [8 * CAP, C], BF16),
    }
    io3 = {
        "partialT": partial_i,
        "yb_all": yb_all,
        "meta_in": meta_loc,
        "counts_all": counts_all8,
        "bsel": E_IN("bsel", [8, 1], F32),
        "wqT": E_IN("wqT", [C, C], F32), "wkT": E_IN("wkT", [C, C], F32),
        "wv": E_IN("wv", [C, C], F32), "wo": E_IN("wo", [C, C], F32),
        "bo": E_IN("bo", [1, C], F32), "wf2": E_IN("wf2", [128, 64], F32),
        "bfq_col": E_IN("bfq_col", [128, 8], F32), "bf_row": E_IN("bf_row", [1, C], F32),
        "g1r_R": E_IN("g1r_R", [4, 1024], F32), "g2r_R": E_IN("g2r_R", [4, 256], F32),
        "g1r_I": E_IN("g1r_I", [4, 1024], F32), "g2r_I": E_IN("g2r_I", [4, 256], F32),
        "bri_row": E_IN("bri_row", [1, C], F32),
        "out": nc.dram_tensor("out", [T, C], F32, kind="ExternalOutput").ap(),
        "a2R": INT("fz_a2R", [IDIM, IDIM], F32),
        "a2I": INT("fz_a2I", [IDIM, IDIM], F32),
    }

    st0 = ExitStack()
    with st0:
        S = Sems(st0, nc, SEM_NAMES)
        with nc.Block() as blk:
            def _gate(val, extra=()):
                for sec in (blk.sync, blk.gpsimd, blk.tensor, blk.vector, blk.scalar):
                    @sec
                    def _(e):
                        e.wait_ge(S.h["cc"], val)
                        for (nm, v_) in extra:
                            e.wait_ge(S.h[nm], v_)

            emit_stage1(_PrefNC(nc, "s1_"), blk, S, io1)

            @blk.gpsimd
            def _(g):
                for nm in ("sx0", "sx1", "st", "sp0", "sp1", "ga0", "ga1"):
                    S.wait(g, nm)
                S.inc(g.collective_compute(
                    "AllGather", OP.bypass, replica_groups=[list(range(8))],
                    ins=[xm_loc], outs=[xm_all]), "cc", 1)
                S.inc(g.collective_compute(
                    "AllGather", OP.bypass, replica_groups=[list(range(8))],
                    ins=[meta_loc], outs=[meta_all]), "cc", 1)
                S.inc(g.collective_compute(
                    "AllGather", OP.bypass, replica_groups=[list(range(8))],
                    ins=[counts_loc], outs=[counts_all8]), "cc", 1)

            _gate(3)
            emit_stage2(_PrefNC(nc, "s2_"), blk, S, io2)

            @blk.gpsimd
            def _(g):
                for nm in ("yb0", "yb1"):
                    S.wait(g, nm)
                S.inc(g.collective_compute(
                    "AllGather", OP.bypass, replica_groups=[list(range(8))],
                    ins=[yb_loc], outs=[yb_all]), "cc", 1)

            _gate(4)
            emit_stage3(_PrefNC(nc, "s3_"), blk, S, io3)
    return nc


def _prep_fused_inputs(inputs):
    f32 = np.float32
    m1 = _prep_stage1_inputs(inputs)
    w13 = np.ascontiguousarray(inputs["L0_ffn_w13"], f32)
    w2w = np.ascontiguousarray(inputs["L0_ffn_w2"], f32)
    dummy1 = [{"partial": np.zeros((C, T), f32), "meta": np.zeros((T, 6), f32),
               "counts": np.zeros((1, E), f32), "xm_out": None} for _ in range(8)]
    dummy2 = [{"yb": np.zeros((1, 1), BF)} for _ in range(8)]
    m3 = _prep_stage3_inputs(inputs, dummy1, dummy2)
    drop3 = {"yb_all", "counts_all", "partialT", "meta_in"}
    maps = []
    for i in range(8):
        m = dict(m1[i])
        m["w13"] = np.ascontiguousarray(w13[8 * i:8 * i + 8])
        m["w2w"] = np.ascontiguousarray(w2w[8 * i:8 * i + 8])
        m["ebase"] = np.full((128, 1), 8.0 * i, f32)
        for k, v in m3[i].items():
            if k not in drop3:
                m[k] = v
        maps.append(m)
    return maps


def run_fused(inputs, **kw):
    nc = build_fused()
    maps = _prep_fused_inputs(inputs)
    return run_bass_kernel_spmd(nc, maps, core_ids=list(range(8)), **kw)


def kernel(**inputs):
    inputs = {k: np.asarray(v) for k, v in inputs.items()}
    res = run_fused(inputs).results
    return np.stack([r["out"] for r in res], axis=0).astype(np.float32)



# version 03
# speedup vs baseline: 1.0567x; 1.0567x over previous
import numpy as np
import ml_dtypes
from contextlib import ExitStack, contextmanager

import concourse.bass as bass
import concourse.mybir as mybir
from concourse.bass import IndirectOffsetOnAxis
from concourse.bass_utils import run_bass_kernel_spmd
from concourse.alu_op_type import AluOpType as OP

F32 = mybir.dt.float32
BF16 = mybir.dt.bfloat16
I32 = mybir.dt.int32
U32 = mybir.dt.uint32
AF = mybir.ActivationFunctionType
AX = mybir.AxisListType.X

B = 8
T = 2048
C = 1024
NH = 16
HD = 64
E = 64
CAP = 1024
IDIM = 512
NCH = C // 128
NT = T // 128
EPS_LN = 1e-5
EPS_RMS = 1e-6
BF = ml_dtypes.bfloat16



class _PrefNC:

    def __init__(self, nc, pref):
        object.__setattr__(self, "_nc", nc)
        object.__setattr__(self, "_pref", pref)

    def sbuf_tensor(self, name, *a, **k):
        return self._nc.sbuf_tensor(self._pref + name, *a, **k)

    def psum_tensor(self, name, *a, **k):
        return self._nc.psum_tensor(self._pref + name, *a, **k)

    def __getattr__(self, n):
        return getattr(self._nc, n)


@contextmanager
def _passthrough(x):
    yield x


class Sems:
    def __init__(self, stack, nc, names):
        self.h = {}
        self.n = {}
        for nm in names:
            self.h[nm] = stack.enter_context(nc.semaphore(nm))
            self.n[nm] = 0

    def inc(self, instr, nm, k=1):
        instr.then_inc(self.h[nm], k)
        self.n[nm] += k
        return instr

    def dma(self, instr, nm="ld"):
        return self.inc(instr, nm, 16)

    def pre(self, eng, nm):
        eng.wait_ge(self.h[nm], self.n[nm])

    def wait(self, eng, nm, val=None):
        eng.wait_ge(self.h[nm], self.n[nm] if val is None else val)


_CHAINED_OPS = {
    "tensor_tensor", "tensor_scalar", "tensor_copy", "tensor_add", "tensor_sub",
    "tensor_mul", "tensor_max", "tensor_relu", "tensor_scalar_add", "tensor_scalar_mul",
    "tensor_scalar_max", "tensor_scalar_min", "tensor_scalar_sub", "reduce_sum",
    "reduce_max", "tensor_reduce", "scalar_tensor_tensor", "max", "max_index",
    "select", "copy_predicated", "match_replace", "memset", "iota", "affine_select",
    "activation", "reciprocal",
}



CHAIN_OFF = False


class _TTrack:

    def __init__(self, t):
        object.__setattr__(self, "_t", t)
        object.__setattr__(self, "last", None)

    def matmul(self, *a, **k):
        r = self._t.matmul(*a, **k)
        object.__setattr__(self, "last", r)
        return r

    def transpose(self, *a, **k):
        r = self._t.transpose(*a, **k)
        object.__setattr__(self, "last", r)
        return r

    def last_inst(self):
        return self.last if self.last is not None else self._t.nop()

    def __getattr__(self, n):
        return getattr(self._t, n)


class _Chain:

    def __init__(self, eng, S, nm):
        self._eng = eng
        self._S = S
        self._nm = nm

    def __getattr__(self, name):
        f = getattr(self._eng, name)
        if name in _CHAINED_OPS and not CHAIN_OFF:
            def g(*a, **k):
                self._eng.wait_ge(self._S.h[self._nm], self._S.n[self._nm])
                r = f(*a, **k)
                self._S.inc(r, self._nm)
                return r
            return g
        return f


def n512a(lo, hi):
    out = []
    x = lo
    while x < hi:
        nxt = min(hi, (x // 512 + 1) * 512)
        out.append((x, nxt))
        x = nxt
    return out


def n512(lo, hi):
    out = []
    x = lo
    while x < hi:
        out.append((x, min(x + 512, hi)))
        x = min(x + 512, hi)
    return out


def emit_stage1(nc, blk, S, io, upto="all", hook_after_A=None):
    xT = io["xT"]
    xtok = io["xtok"]
    w_attn = io["w_attn"]
    w_attnT = io["w_attnT"]
    b_attn = io["b_attn"]
    w_proj = io["w_proj"]
    b_proj = io["b_proj"]
    bproj_col = io["bproj_col"]
    wk_cT = io["wk_cT"]
    wk_e = io["wk_e"]
    wv_cT = io["wv_cT"]
    wv_e = io["wv_e"]
    rms_w = io["rms_w"]
    rms_w_col = io["rms_w_col"]
    battn_col = io["battn_col"]
    gate_wT = io["gate_wT"]
    sw_gate = io["sw_gate"]
    sw_up = io["sw_up"]
    sw_down = io["sw_down"]
    partial = io["partial"]
    xm_out = io["xm_out"]
    meta = io["meta"]
    counts = io["counts"]
    st = ExitStack()
    with st:
        sbt = lambda *a: st.enter_context(nc.sbuf_tensor(*a))

        xnT = sbt("xnT", [128, NCH, T], BF16)
        vtok = sbt("vtok", [128, NT, NH * 65], BF16)
        attnTn = sbt("attnTn", [128, NCH, T], BF16)
        khat = sbt("khat", [128, NCH, NH * HD], BF16)
        ones_col = sbt("ones_col", [1, 128], F32)
        ones_r64 = sbt("ones_r64", [128, 128], F32)
        onesc128f = sbt("onesc128f", [128, 1], F32)
        onesc128b = sbt("onesc128b", [128, 1], BF16)
        ones16 = sbt("ones16", [16, 1], F32)
        sel2 = sbt("sel2", [2, 256], F32)
        hsel = sbt("hsel", [128, 2], F32)
        iden = sbt("iden", [128, 128], F32)
        iota64 = sbt("iota64", [128, E], F32)
        triU = sbt("triU", [128, 128], BF16)
        triU16 = sbt("triU16", [16, 16], F32)
        triD = sbt("triD", [128, 128], BF16)
        rmsw_col = sbt("rmsw_col", [128, NCH], F32)
        bq_col = sbt("bq_col", [128, NCH], F32)
        bkv = sbt("bkv", [128, 16], F32)
        bkh_col = sbt("bkh_col", [128, NCH], F32)
        m_cols = sbt("m_cols", [128, NT], F32)
        rstd_cols = sbt("rstd_cols", [128, NT], F32)
        g_cols = sbt("g_cols", [128, NT], F32)
        cg_cols = sbt("cg_cols", [128, NT], F32)
        meta_t = sbt("meta_t", [128, NT, 6], F32)
        small = sbt("small", [128, 64], F32)

        with _passthrough(blk) as blk:
            def _fin():
                @blk.gpsimd
                def _(g):
                    for nm in ("ld", "gp", "st", "lt0", "lt1", "lc0", "sx0", "sx1",
                               "sp0", "sp1", "ga0", "ga1"):
                        S.wait(g, nm)
                    for nm in ("pe", "dv", "ac", "gc", "dvx", "acx", "gcx"):
                        S.wait(g, nm)
                    g.nop()
            @blk.gpsimd
            def _(g):
                g = _Chain(g, S, "gcx")
                def ch(instr):
                    return instr
                ch(g.memset(ones_col[:, :], 1.0))
                ch(g.memset(ones_r64[:, :], 0.0))
                ch(g.memset(ones_r64[64:65, :], 1.0))
                ch(g.memset(onesc128f[:, :], 1.0))
                ch(g.memset(onesc128b[:, :], 1.0))
                ch(g.memset(ones16[:, :], 1.0))
                ch(g.memset(sel2[:, :], 1.0))
                ch(g.affine_select(sel2[:, :], sel2[:, :], [[1, 256]], OP.is_ge, 0.0,
                                   base=0, channel_multiplier=-128))
                ch(g.affine_select(sel2[:, :], sel2[:, :], [[-1, 256]], OP.is_ge, 0.0,
                                   base=127, channel_multiplier=128))
                ch(g.memset(hsel[:, :], 0.0))
                ch(g.memset(hsel[0:64, 0:1], 1.0))
                ch(g.memset(hsel[64:128, 1:2], 1.0))
                ch(g.memset(iden[:, :], 1.0))
                ch(g.affine_select(iden[:, :], iden[:, :], [[-1, 128]], OP.is_equal, 0.0,
                                   base=0, channel_multiplier=1))
                ch(g.iota(iota64[:, :], [[1, E]], base=0, channel_multiplier=0,
                          allow_small_or_imprecise_dtypes=True))
                ch(g.memset(triU[:, :], 1.0))
                ch(g.affine_select(triU[:, :], triU[:, :], [[1, 128]], OP.is_ge, 0.0,
                                   base=-1, channel_multiplier=-1))
                ch(g.memset(triU16[:, :], 1.0))
                ch(g.affine_select(triU16[:, :], triU16[:, :], [[1, 16]], OP.is_ge, 0.0,
                                   base=-1, channel_multiplier=-1))
                ch(g.memset(triD[:, :], 1.0))
                ch(g.affine_select(triD[:, :], triD[:, :], [[1, 128]], OP.is_ge, 0.0,
                                   base=0, channel_multiplier=-1))
                for h in range(NH):
                    ch(g.memset(vtok[:, :, h * 65 + 64:h * 65 + 65], 1.0))
                S.inc(g.drain(), "gc")

            @blk.sync
            def _(sy):
                S.pre(sy, "ld"); S.dma(sy.dma_start(out=rmsw_col[:, :], in_=rms_w_col))
                S.pre(sy, "ld"); S.dma(sy.dma_start(out=bq_col[:, :], in_=battn_col[:, 0:NCH]))
                S.pre(sy, "ld"); S.dma(sy.dma_start(out=bkv[:, :], in_=battn_col[:, NCH:3 * NCH]))
            ld0 = S.n["ld"]

            if upto == "consts":
                _fin()
                return nc
            with ExitStack() as ph:
                _e = ph.enter_context
                xt = _e(nc.sbuf_tensor("xt", [128, 2, T], F32))
                xtk = _e(nc.sbuf_tensor("xtk", [128, 2, C], F32))
                sq = _e(nc.sbuf_tensor("sq", [128, C], F32))
                ssq = _e(nc.sbuf_tensor("ssq", [128, NT], F32))
                xnf = _e(nc.sbuf_tensor("xnf", [128, 1, T], F32))
                xmbf = _e(nc.sbuf_tensor("xmbf", [128, 2, C], BF16))
                m_b = _e(nc.sbuf_tensor("m_b", [128, T], F32))
                rstd_b = _e(nc.sbuf_tensor("rstd_b", [128, T], F32))
                rmsw_b = _e(nc.sbuf_tensor("rmsw_b", [128, C], F32))
                gwT = _e(nc.sbuf_tensor("gwT", [128, NCH, E], F32))
                stile_b = _e(nc.sbuf_tensor("stile_b", [128, E], F32))
                ltile = _e(nc.sbuf_tensor("ltile", [128, E], F32))
                gacc = _e(nc.sbuf_tensor("gacc", [128, NT * E], F32))
                chosen = _e(nc.sbuf_tensor("chosen", [128, NT, E], BF16))
                dcol = _e(nc.sbuf_tensor("dcol", [128, NT], F32))
                mx8 = _e(nc.sbuf_tensor("mx8", [128, 8], F32))
                idx8 = _e(nc.sbuf_tensor("idx8", [128, 8], U32))
                oh = _e(nc.sbuf_tensor("oh", [128, 2, E], F32))
                w64 = _e(nc.sbuf_tensor("w64", [128, E], F32))
                cnt16 = _e(nc.sbuf_tensor("cnt16", [16, E], F32))
                toff16 = _e(nc.sbuf_tensor("toff16", [16, E], F32))
                psA = _e(nc.psum_tensor("psA1", [128, 2048], F32))
                psB = _e(nc.psum_tensor("psB1", [128, 1024], F32))
                psC = _e(nc.psum_tensor("psC1", [128, 512], F32))

                xtok_r = xtok.rearrange("(n p) c -> p n c", p=128)
                xT_r = xT.rearrange("(c p) t -> p c t", p=128)

                @blk.sync
                def _(sy):
                    S.pre(sy, "ld"); S.dma(sy.dma_start(out=gacc[0:1, 0:C], in_=rms_w))
                    S.pre(sy, "ld"); S.dma(sy.dma_start(out=gwT[:, :, :], in_=gate_wT.rearrange("(c p) e -> p c e", p=128)))
                ld0b = S.n["ld"]

                @blk.vector
                def _(v):
                    v = _Chain(v, S, "dvx")
                    v.wait_ge(S.h["ld"], ld0b)
                    S.wait(v, "gc")
                    for c_ in range(NCH):
                        v.tensor_scalar(gwT[:, c_, :], gwT[:, c_, :], rmsw_col[:, c_:c_ + 1], None, OP.mult)
                    S.inc(v.drain(), "dv")

                @blk.tensor
                def _(t):
                    t = _TTrack(t)
                    S.wait(t, "dv")
                    for c_ in range(NCH):
                        t.matmul(psB[0:1, 0:E], onesc128f[:, :], gwT[:, c_, :],
                                 start=(c_ == 0), stop=(c_ == NCH - 1))
                    for (a, b) in n512(0, C):
                        t.matmul(psA[:, a:b], ones_col[:, :], gacc[0:1, a:b], start=True, stop=True)
                    S.inc(t.last_inst(), "pe")

                @blk.vector
                def _(v):
                    v = _Chain(v, S, "dvx")
                    S.wait(v, "pe")
                    v.tensor_copy(small[0:1, 0:E], psB[0:1, 0:E])
                    v.tensor_copy(rmsw_b[:, :], psA[:, 0:C])
                    S.inc(v.drain(), "dv")

                @blk.tensor
                def _(t):
                    t = _TTrack(t)
                    S.wait(t, "dv")
                    t.matmul(psB[:, 0:E], ones_col[:, :], small[0:1, 0:E], start=True, stop=True)
                    S.inc(t.last_inst(), "pe")

                @blk.vector
                def _(v):
                    v = _Chain(v, S, "dvx")
                    S.wait(v, "pe")
                    v.tensor_copy(stile_b[:, :], psB[:, 0:E])
                    S.inc(v.drain(), "dv")

                ld_marks = []
                st_marks = []
                dv_marks = []
                for tt in range(NT):
                    @blk.sync
                    def _(sy, tt=tt):
                        if tt >= 2:
                            S.wait(sy, "dv", dv_marks[tt - 2])
                        S.dma(sy.dma_start(out=xtk[:, tt % 2, :], in_=xtok_r[:, tt, :]), f"lt{tt % 2}")
                        ld_marks.append(S.n[f"lt{tt % 2}"])

                    @blk.scalar
                    def _(sc, tt=tt):
                        sc = _Chain(sc, S, "acx")
                        sc.wait_ge(S.h[f"lt{tt % 2}"], ld_marks[tt])
                        S.wait(sc, "dv")
                        sc.activation(out=sq[:, :], in_=xtk[:, tt % 2, :], func=AF.Square,
                                      accum_out=ssq[:, tt:tt + 1])
                        S.inc(sc.drain(), "ac")

                    @blk.vector
                    def _(v, tt=tt):
                        v = _Chain(v, S, "dvx")
                        v.wait_ge(S.h[f"lt{tt % 2}"], ld_marks[tt])
                        v.reduce_sum(m_cols[:, tt:tt + 1], xtk[:, tt % 2, :], AX)
                        S.wait(v, "ac")
                        c0 = slice(tt, tt + 1)
                        v.tensor_scalar(m_cols[:, c0], m_cols[:, c0], 1.0 / C, None, OP.mult)
                        v.tensor_scalar(ssq[:, c0], ssq[:, c0], 1.0 / C, None, OP.mult)
                        v.tensor_tensor(g_cols[:, c0], m_cols[:, c0], m_cols[:, c0], OP.mult)
                        v.tensor_tensor(ssq[:, c0], ssq[:, c0], g_cols[:, c0], OP.subtract)
                        v.tensor_scalar(rstd_cols[:, c0], ssq[:, c0], EPS_LN, None, OP.add)
                        S.inc(v.drain(), "dv")

                    @blk.scalar
                    def _(sc, tt=tt):
                        sc = _Chain(sc, S, "acx")
                        S.wait(sc, "dv")
                        c0 = slice(tt, tt + 1)
                        sc.activation(out=cg_cols[:, c0], in_=rstd_cols[:, c0], func=AF.Ln)
                        sc.activation(out=rstd_cols[:, c0], in_=cg_cols[:, c0], func=AF.Exp, scale=-0.5)
                        sc.activation(out=g_cols[:, c0], in_=cg_cols[:, c0], func=AF.Exp, scale=-1.0)
                        S.inc(sc.drain(), "ac")

                    @blk.vector
                    def _(v, tt=tt):
                        v = _Chain(v, S, "dvx")
                        S.wait(v, "ac")
                        c0 = slice(tt, tt + 1)
                        v.tensor_tensor(g_cols[:, c0], g_cols[:, c0], ssq[:, c0], OP.mult)
                        v.tensor_scalar(g_cols[:, c0], g_cols[:, c0], EPS_RMS, None, OP.add)
                        S.inc(v.drain(), "dv")

                    @blk.scalar
                    def _(sc, tt=tt):
                        sc = _Chain(sc, S, "acx")
                        S.wait(sc, "dv")
                        c0 = slice(tt, tt + 1)
                        sc.activation(out=g_cols[:, c0], in_=g_cols[:, c0], func=AF.Ln)
                        sc.activation(out=g_cols[:, c0], in_=g_cols[:, c0], func=AF.Exp, scale=-0.5)
                        S.inc(sc.drain(), "ac")

                    @blk.vector
                    def _(v, tt=tt):
                        v = _Chain(v, S, "dvx")
                        S.wait(v, "ac")
                        c0 = slice(tt, tt + 1)
                        v.tensor_tensor(cg_cols[:, c0], rstd_cols[:, c0], g_cols[:, c0], OP.mult)
                        if tt >= 2:
                            v.wait_ge(S.h[f"sx{tt % 2}"], st_marks[tt - 2])
                        v.tensor_scalar(sq[:, :], xtk[:, tt % 2, :], m_cols[:, c0],
                                        cg_cols[:, c0], OP.subtract, OP.mult)
                        v.tensor_tensor(xmbf[:, tt % 2, :], sq[:, :], rmsw_b[:, :], OP.mult)
                        S.inc(v.drain(), "dv")
                        dv_marks.append(S.n["dv"])

                    @blk.sync
                    def _(sy, tt=tt):
                        S.wait(sy, "dv")
                        S.dma(sy.dma_start(out=xm_out.rearrange("(n p) c -> p n c", p=128)[:, tt, :],
                                           in_=xmbf[:, tt % 2, :]), f"sx{tt % 2}")
                        st_marks.append(S.n[f"sx{tt % 2}"])
                dv_stats = S.n["dv"]

                ph_rows = ExitStack()
                m_row = ph_rows.enter_context(nc.sbuf_tensor("m_row", [1, T], F32))
                for (colt, bt) in ((m_cols, m_b), (rstd_cols, rstd_b)):
                    @blk.tensor
                    def _(t, colt=colt):
                        t = _TTrack(t)
                        if colt is m_cols:
                            t.wait_ge(S.h["dv"], dv_stats)
                        else:
                            S.wait(t, "dv")
                        for tt in range(NT):
                            t.transpose(psA[0:1, tt * 128:(tt + 1) * 128], colt[:, tt:tt + 1], iden[:, :])
                        S.inc(t.last_inst(), "pe")

                    @blk.vector
                    def _(v):
                        v = _Chain(v, S, "dvx")
                        S.wait(v, "pe")
                        v.tensor_copy(m_row[:, :], psA[0:1, 0:T])
                        S.inc(v.drain(), "dv")

                    @blk.tensor
                    def _(t):
                        t = _TTrack(t)
                        S.wait(t, "dv")
                        for (a, b) in n512(0, T):
                            t.matmul(psA[:, a:b], ones_col[:, :], m_row[:, a:b], start=True, stop=True)
                        S.inc(t.last_inst(), "pe")

                    @blk.vector
                    def _(v, bt=bt):
                        v = _Chain(v, S, "dvx")
                        S.wait(v, "pe")
                        v.tensor_copy(bt[:, :], psA[:, :])
                        S.inc(v.drain(), "dv")
                dv_bcast = S.n["dv"]
                ph_rows.close()

                pe_marks = {}
                stx_marks = {}
                ldc_marks = []
                for c_ in range(NCH):
                    buf = 0

                    @blk.sync
                    def _(sy, c_=c_, buf=buf):
                        if c_ >= 1:
                            sy.wait_ge(S.h[f"sp{(c_ - 1) % 2}"], stx_marks[c_ - 1])
                        S.dma(sy.dma_start(out=xt[:, buf, :], in_=xT_r[:, c_, :]), "lc0")
                        ldc_marks.append(S.n["lc0"])

                    @blk.vector
                    def _(v, c_=c_, buf=buf):
                        v = _Chain(v, S, "dvx")
                        if c_ == 0:
                            v.wait_ge(S.h["dv"], dv_bcast)
                        v.wait_ge(S.h["lc0"], ldc_marks[c_])
                        if c_ >= 2:
                            v.wait_ge(S.h["pe"], pe_marks[c_ - 2])
                        v.tensor_tensor(xnf[:, buf, :], xt[:, buf, :], m_b[:, :], OP.subtract)
                        v.tensor_tensor(xnf[:, buf, :], xnf[:, buf, :], rstd_b[:, :], OP.mult)
                        v.tensor_copy(xnT[:, c_, :], xnf[:, buf, :])
                        v.tensor_tensor(xt[:, buf, :], xt[:, buf, :], xnf[:, buf, :], OP.add)
                        S.inc(v.drain(), "dv")

                    @blk.sync
                    def _(sy, c_=c_, buf=buf):
                        S.wait(sy, "dv")
                        S.dma(sy.dma_start(out=partial[c_ * 128:(c_ + 1) * 128, :], in_=xt[:, buf, :]), f"sp{c_ % 2}")
                        stx_marks[c_] = S.n[f"sp{c_ % 2}"]

                    @blk.tensor
                    def _(t, c_=c_, buf=buf):
                        t = _TTrack(t)
                        S.wait(t, "dv")
                        if c_ >= 1:
                            S.wait(t, "dv")
                        for tt in range(NT):
                            t.matmul(psB[:, tt * E:(tt + 1) * E],
                                     xnf[:, buf, tt * 128:(tt + 1) * 128], gwT[:, c_, :],
                                     start=True, stop=True)
                        S.inc(t.last_inst(), "pe")
                        pe_marks[c_] = S.n["pe"]

                    @blk.vector
                    def _(v, c_=c_):
                        v = _Chain(v, S, "dvx")
                        S.wait(v, "pe")
                        if c_ == 0:
                            v.tensor_copy(gacc[:, :], psB[:, 0:NT * E])
                        else:
                            v.tensor_tensor(gacc[:, :], gacc[:, :], psB[:, 0:NT * E], OP.add)
                        S.inc(v.drain(), "dv")

                for tt in range(NT):
                    @blk.vector
                    def _(v, tt=tt):
                        v = _Chain(v, S, "dvx")
                        if tt == 0:
                            S.wait(v, "pe")
                        v.tensor_scalar(ltile[:, :], gacc[:, tt * E:(tt + 1) * E],
                                        g_cols[:, tt:tt + 1], None, OP.mult)
                        v.max(out=mx8[:, :], in_=ltile[:, :])
                        v.max_index(out=idx8[:, :], in_max=mx8[:, :], in_values=ltile[:, :])
                        v.tensor_copy(meta_t[:, tt, 0:2], idx8[:, 0:2])
                        v.tensor_tensor(dcol[:, tt:tt + 1], mx8[:, 0:1], mx8[:, 1:2], OP.subtract)
                        v.tensor_scalar(oh[:, 0, :], iota64[:, :], meta_t[:, tt, 0:1], None, OP.is_equal)
                        v.tensor_scalar(oh[:, 1, :], iota64[:, :], meta_t[:, tt, 1:2], None, OP.is_equal)
                        v.tensor_tensor(w64[:, :], oh[:, 0, :], oh[:, 1, :], OP.add)
                        v.tensor_copy(chosen[:, tt, :], w64[:, :])
                        S.inc(v.drain(), "dv")

                    @blk.tensor
                    def _(t, tt=tt):
                        t = _TTrack(t)
                        S.wait(t, "dv")
                        t.matmul(psC[:, 0:E], triU[:, :], chosen[:, tt, :], start=True, stop=True)
                        S.inc(t.last_inst(), "pe")

                    @blk.vector
                    def _(v, tt=tt):
                        v = _Chain(v, S, "dvx")
                        S.wait(v, "pe")
                        for k in range(2):
                            v.tensor_tensor(w64[:, :], psC[:, 0:E], oh[:, k, :], OP.mult)
                            v.reduce_sum(meta_t[:, tt, 4 + k:5 + k], w64[:, :], AX)
                        S.inc(v.drain(), "dv")

                @blk.scalar
                def _(sc):
                    sc = _Chain(sc, S, "acx")
                    S.wait(sc, "dv")
                    sc.activation(out=meta_t[:, :, 2:3], in_=dcol[:, :].rearrange("p (n o) -> p n o", o=1),
                                  func=AF.Sigmoid)
                    sc.activation(out=meta_t[:, :, 3:4], in_=dcol[:, :].rearrange("p (n o) -> p n o", o=1),
                                  func=AF.Sigmoid, scale=-1.0)
                    S.inc(sc.drain(), "ac")

                @blk.tensor
                def _(t):
                    t = _TTrack(t)
                    S.wait(t, "dv")
                    ch_flat = chosen.ap().rearrange("p n e -> p (n e)")
                    t.matmul(psC[0:1, 0:512], onesc128b[:, :], ch_flat[:, 0:512], start=True, stop=True)
                    t.matmul(psB[0:1, 0:512], onesc128b[:, :], ch_flat[:, 512:1024], start=True, stop=True)
                    S.inc(t.last_inst(), "pe")

                @blk.vector
                def _(v):
                    v = _Chain(v, S, "dvx")
                    S.wait(v, "pe")
                    v.tensor_copy(gacc[0:1, 0:512], psC[0:1, 0:512])
                    v.tensor_copy(gacc[0:1, 512:1024], psB[0:1, 0:512])
                    S.inc(v.drain(), "dv")

                @blk.gpsimd
                def _(g):
                    g = _Chain(g, S, "gcx")
                    S.wait(g, "dv")
                    S.pre(g, "gp")
                    S.dma(g.dma_start(out=cnt16[:, :], in_=gacc[0:1, 0:NT * E]), "gp")

                @blk.tensor
                def _(t):
                    t = _TTrack(t)
                    S.wait(t, "gp")
                    t.matmul(psC[0:16, 0:E], triU16[:, :], cnt16[:, :], start=True, stop=True)
                    t.matmul(psB[0:1, 0:E], ones16[:, :], cnt16[:, :], start=True, stop=True)
                    S.inc(t.last_inst(), "pe")

                @blk.vector
                def _(v):
                    v = _Chain(v, S, "dvx")
                    S.wait(v, "pe")
                    v.tensor_copy(toff16[:, :], psC[0:16, 0:E])
                    v.tensor_copy(small[0:1, 0:E], psB[0:1, 0:E])
                    S.inc(v.drain(), "dv")

                @blk.sync
                def _(sy):
                    S.wait(sy, "dv")
                    S.pre(sy, "st")
                    S.dma(sy.dma_start(out=counts, in_=small[0:1, 0:E]), "st")

                for tt in range(NT):
                    @blk.gpsimd
                    def _(g, tt=tt):
                        g = _Chain(g, S, "gcx")
                        S.wait(g, "dv")
                        S.pre(g, "gp")
                        if tt == 0:
                            S.wait(g, "st")
                        S.dma(g.dma_start(out=small[0:1, 0:E], in_=toff16[tt:tt + 1, :]), "gp")

                    @blk.tensor
                    def _(t, tt=tt):
                        t = _TTrack(t)
                        S.wait(t, "gp")
                        t.matmul(psB[:, 0:E], ones_col[:, :], small[0:1, 0:E],
                                 start=True, stop=True)
                        S.inc(t.last_inst(), "pe")

                    @blk.vector
                    def _(v, tt=tt):
                        v = _Chain(v, S, "dvx")
                        S.wait(v, "pe")
                        v.tensor_copy(w64[:, :], psB[:, 0:E])
                        for k in range(2):
                            v.tensor_scalar(oh[:, 0, :], iota64[:, :], meta_t[:, tt, k:k + 1], None, OP.is_equal)
                            v.tensor_tensor(oh[:, 0, :], oh[:, 0, :], w64[:, :], OP.mult)
                            v.reduce_sum(oh[:, 1, 0:1], oh[:, 0, :], AX)
                            v.tensor_tensor(meta_t[:, tt, 4 + k:5 + k], meta_t[:, tt, 4 + k:5 + k],
                                            oh[:, 1, 0:1], OP.add)
                        S.inc(v.drain(), "dv")

                @blk.sync
                def _(sy):
                    S.wait(sy, "dv")
                    S.wait(sy, "ac")
                    S.pre(sy, "st")
                    S.dma(sy.dma_start(out=meta.rearrange("(n p) c -> p n c", p=128), in_=meta_t[:, :, :]), "st")
                    S.wait(sy, "st")
                    sy.wait_ge(S.h["sx0"], S.n["sx0"])
                    sy.wait_ge(S.h["sx1"], S.n["sx1"])
                    sy.wait_ge(S.h["sp0"], S.n["sp0"])
                    sy.wait_ge(S.h["sp1"], S.n["sp1"])

            if upto == "A":
                _fin()
                return nc
            if hook_after_A is not None:
                hook_after_A()
            with ExitStack() as ph:
                _e = ph.enter_context
                w2t = _e(nc.sbuf_tensor("w2t", [64, 128], F32))
                w2d = _e(nc.sbuf_tensor("w2d", [128, 64], F32))
                wfold = _e(nc.sbuf_tensor("wfold", [128, NCH, C], F32))
                brow = _e(nc.sbuf_tensor("brow", [1, C], F32))
                vhat = _e(nc.sbuf_tensor("vhat", [128, NCH, NH * HD], BF16))
                bvh_b = _e(nc.sbuf_tensor("bvh_b", [128, C], F32))
                psA = _e(nc.psum_tensor("psA2", [128, 1024], F32))
                psB = _e(nc.psum_tensor("psB2", [128, 1024], F32))
                psC = _e(nc.psum_tensor("psC2", [128, 512], F32))

                for side in ("k", "v"):
                    @blk.sync
                    def _(sy, side=side):
                        sy.wait_ge(S.h["pe"], S.n["pe"])
                        sy.wait_ge(S.h["dv"], S.n["dv"])
                        sy.wait_ge(S.h["ac"], S.n["ac"])
                        S.pre(sy, "ld")
                        S.dma(sy.dma_start(out=w2t[0:32, 0:64], in_=(wk_cT if side == "k" else wv_cT)))
                        S.pre(sy, "ld")
                        S.dma(sy.dma_start(out=w2t[0:32, 64:128], in_=(wk_e if side == "k" else wv_e)))
                        lo = C if side == "k" else 2 * C
                        S.pre(sy, "ld")
                        S.dma(sy.dma_start(out=wfold[:, :, :],
                                           in_=w_attnT[lo:lo + C, :].rearrange("(c p) q -> p c q", p=128)))
                    ldw_ = S.n["ld"]

                    @blk.tensor
                    def _(t, ldw_=ldw_):
                        t = _TTrack(t)
                        t.wait_ge(S.h["ld"], ldw_)
                        t.matmul(psC[0:64, 0:64], w2t[0:32, 0:64], w2t[0:32, 64:128], start=True, stop=True)
                        t.matmul(psB[64:128, 0:64], w2t[0:32, 0:64], w2t[0:32, 64:128], start=True, stop=True)
                        S.inc(t.last_inst(), "pe")

                    @blk.vector
                    def _(v, side=side):
                        v = _Chain(v, S, "dvx")
                        S.wait(v, "pe")
                        v.tensor_scalar(w2d[0:64, 0:64], psC[0:64, 0:64],
                                        0.125 if side == "k" else 1.0, None, OP.mult)
                        v.tensor_scalar(w2d[64:128, 0:64], psB[64:128, 0:64],
                                        0.125 if side == "k" else 1.0, None, OP.mult)
                        S.inc(v.drain(), "dv")

                    if upto == "fk1" and side == "k":
                        _fin()
                        return nc
                    for ct in range(NCH):
                        @blk.tensor
                        def _(t, ct=ct):
                            t = _TTrack(t)
                            S.wait(t, "dv")
                            for h in [x for x in range(NH) if x % 2 == 0] + \
                                     [x for x in range(NH) if x % 2 == 1]:
                                t.matmul(psB[:, h * 64:(h + 1) * 64],
                                         wfold[(h % 2) * 64:(h % 2) * 64 + 64, h // 2, ct * 128:(ct + 1) * 128],
                                         w2d[(h % 2) * 64:(h % 2) * 64 + 64, 0:64], start=True, stop=True)
                            S.inc(t.last_inst(), "pe")

                        @blk.vector
                        def _(v, ct=ct, side=side):
                            v = _Chain(v, S, "dvx")
                            S.wait(v, "pe")
                            v.tensor_copy((khat if side == "k" else vhat)[:, ct, :], psB[:, 0:C])
                            S.inc(v.drain(), "dv")

                    if upto == "fk2" and side == "k":
                        _fin()
                        return nc
                    @blk.tensor
                    def _(t, side=side):
                        t = _TTrack(t)
                        S.wait(t, "dv")
                        off = 0 if side == "k" else NCH
                        for h in [x for x in range(NH) if x % 2 == 0] + \
                                 [x for x in range(NH) if x % 2 == 1]:
                            t.matmul(psA[0:1, h * 64:(h + 1) * 64],
                                     bkv[(h % 2) * 64:(h % 2) * 64 + 64, off + h // 2:off + h // 2 + 1],
                                     w2d[(h % 2) * 64:(h % 2) * 64 + 64, 0:64], start=True, stop=True)
                        S.inc(t.last_inst(), "pe")

                    @blk.vector
                    def _(v):
                        v = _Chain(v, S, "dvx")
                        S.wait(v, "pe")
                        v.tensor_copy(brow[0:1, 0:C], psA[0:1, 0:C])
                        S.inc(v.drain(), "dv")

                    if upto == "fk3" and side == "k":
                        _fin()
                        return nc
                    if side == "k":
                        @blk.tensor
                        def _(t):
                            t = _TTrack(t)
                            S.wait(t, "dv")
                            for c_ in range(NCH):
                                t.matmul(psC[:, c_:c_ + 1], brow[0:1, c_ * 128:(c_ + 1) * 128],
                                         ones_col[0:1, 0:1], start=True, stop=True)
                            S.inc(t.last_inst(), "pe")

                        @blk.vector
                        def _(v):
                            v = _Chain(v, S, "dvx")
                            S.wait(v, "pe")
                            v.tensor_copy(bkh_col[:, :], psC[:, 0:NCH])
                            S.inc(v.drain(), "dv")
                    else:
                        @blk.tensor
                        def _(t):
                            t = _TTrack(t)
                            S.wait(t, "dv")
                            for (a, b) in n512(0, C):
                                t.matmul(psA[:, a:b], ones_col[:, :], brow[0:1, a:b], start=True, stop=True)
                            S.inc(t.last_inst(), "pe")

                        @blk.vector
                        def _(v):
                            v = _Chain(v, S, "dvx")
                            S.wait(v, "pe")
                            v.tensor_copy(bvh_b[:, :], psA[:, 0:C])
                            S.inc(v.drain(), "dv")

                if upto == "folds0":
                    _fin()
                    return nc
                with nc.psum_tensor("psV", [128, 1024], F32) as psV:
                    for tt in range(NT):
                        @blk.tensor
                        def _(t, tt=tt):
                            t = _TTrack(t)
                            S.wait(t, "dv")
                            for cc in range(NCH):
                                for (a, b) in n512(0, C):
                                    t.matmul(psV[:, a:b], xnT[:, cc, tt * 128:(tt + 1) * 128],
                                             vhat[:, cc, a:b], start=(cc == 0), stop=(cc == NCH - 1))
                            S.inc(t.last_inst(), "pe")

                        @blk.vector
                        def _(v, tt=tt):
                            v = _Chain(v, S, "dvx")
                            S.wait(v, "pe")
                            v.tensor_tensor(
                                vtok[:, tt, :].rearrange("p (h x) -> p h x", h=NH)[:, :, 0:64],
                                psV[:, 0:C].rearrange("p (h x) -> p h x", h=NH),
                                bvh_b[:, 0:C].rearrange("p (h x) -> p h x", h=NH),
                                OP.add)
                            S.inc(v.drain(), "dv")


            if upto == "folds":
                _fin()
                return nc
            with nc.sbuf_tensor("wq_sb", [128, NCH, C], BF16) as wq_sb:
                @blk.gpsimd
                def _(g):
                    g = _Chain(g, S, "gcx")
                    S.wait(g, "pe")
                    S.wait(g, "dv")
                    S.pre(g, "gp")
                    S.dma(g.dma_start(out=wq_sb[:, :, :],
                                      in_=w_attn[:, 0:C].rearrange("(c p) f -> p c f", p=128)), "gp")
                gp_wq = S.n["gp"]

                for hp in range(NH // 2):
                    with ExitStack() as ph:
                        _e = ph.enter_context
                        qp = _e(nc.sbuf_tensor(f"qp{hp}", [128, T], BF16))
                        kp = _e(nc.sbuf_tensor(f"kp{hp}", [128, T], BF16))
                        sqn = _e(nc.sbuf_tensor(f"sqn{hp}", [128, T], F32))
                        nrm = _e(nc.sbuf_tensor(f"nrm{hp}", [2, 8], F32))
                        mneg2 = _e(nc.sbuf_tensor(f"mneg2{hp}", [128, 2], F32))

                        with ExitStack() as ph2:
                            psQ = ph2.enter_context(nc.psum_tensor(f"psQ{hp}", [128, 2048], F32))
                            psN = ph2.enter_context(nc.psum_tensor(f"psN{hp}", [2, 2048], F32))
                            for (which, wtile, outt) in (("q", wq_sb, qp), ("k", khat, kp)):
                                @blk.tensor
                                def _(t, which=which, wtile=wtile, hp=hp):
                                    t = _TTrack(t)
                                    if which == "q":
                                        t.wait_ge(S.h["gp"], gp_wq)
                                    S.wait(t, "dv")
                                    for cc in range(NCH):
                                        for (a, b) in n512(0, T):
                                            t.matmul(psQ[:, a:b], wtile[:, cc, hp * 128:(hp + 1) * 128],
                                                     xnT[:, cc, a:b], start=(cc == 0), stop=(cc == NCH - 1))
                                    S.inc(t.last_inst(), "pe")

                                @blk.vector
                                def _(v, which=which, outt=outt, hp=hp):
                                    v = _Chain(v, S, "dvx")
                                    S.wait(v, "pe")
                                    col = bq_col[:, hp:hp + 1] if which == "q" else bkh_col[:, hp:hp + 1]
                                    v.tensor_scalar(outt[:, :], psQ[:, :], col, None, OP.add)
                                    S.inc(v.drain(), "dv")

                                @blk.scalar
                                def _(sc, outt=outt):
                                    sc = _Chain(sc, S, "acx")
                                    S.wait(sc, "dv")
                                    sc.activation(out=sqn[:, :], in_=outt[:, :], func=AF.Square)
                                    S.inc(sc.drain(), "ac")

                                @blk.tensor
                                def _(t, which=which):
                                    t = _TTrack(t)
                                    S.wait(t, "ac")
                                    S.wait(t, "dv")
                                    for (a, b) in n512(0, T):
                                        t.matmul(psN[0:2, a:b], hsel[:, :], sqn[:, a:b], start=True, stop=True)
                                    S.inc(t.last_inst(), "pe")

                                @blk.vector
                                def _(v, which=which):
                                    v = _Chain(v, S, "dvx")
                                    S.wait(v, "pe")
                                    idx = 0 if which == "q" else 1
                                    v.reduce_max(nrm[0:2, idx:idx + 1], psN[0:2, 0:T], AX)
                                    S.inc(v.drain(), "dv")

                        @blk.scalar
                        def _(sc):
                            sc = _Chain(sc, S, "acx")
                            S.wait(sc, "dv")
                            sc.activation(out=nrm[0:2, 2:4], in_=nrm[0:2, 0:2], func=AF.Sqrt)
                            S.inc(sc.drain(), "ac")

                        @blk.vector
                        def _(v):
                            v = _Chain(v, S, "dvx")
                            S.wait(v, "ac")
                            v.tensor_tensor(nrm[0:2, 4:5], nrm[0:2, 2:3], nrm[0:2, 3:4], OP.mult)
                            v.tensor_scalar(nrm[0:2, 5:6], nrm[0:2, 4:5], -1.0, None, OP.mult)
                            S.inc(v.drain(), "dv")

                        with nc.psum_tensor(f"psM{hp}", [128, 512], F32) as psM:
                            @blk.tensor
                            def _(t):
                                t = _TTrack(t)
                                S.wait(t, "dv")
                                t.matmul(psM[:, 0:1], sel2[:, 0:128], nrm[0:2, 5:6], start=True, stop=True)
                                t.matmul(psM[:, 1:2], sel2[:, 128:256], nrm[0:2, 5:6], start=True, stop=True)
                                S.inc(t.last_inst(), "pe")

                            @blk.vector
                            def _(v):
                                v = _Chain(v, S, "dvx")
                                S.wait(v, "pe")
                                v.tensor_copy(mneg2[:, :], psM[:, 0:2])
                                S.inc(v.drain(), "dv")

                        with ExitStack() as ph3:
                            _e3 = ph3.enter_context
                            pt = _e3(nc.sbuf_tensor(f"pt{hp}", [128, 2, 1024], BF16))
                            attn_un = _e3(nc.sbuf_tensor(f"attn_un{hp}", [64, T], F32))
                            lex = _e3(nc.sbuf_tensor(f"lex{hp}", [128, T], F32))
                            psS0 = _e3(nc.psum_tensor(f"psS0{hp}", [128, 1024], F32))
                            psS1 = _e3(nc.psum_tensor(f"psS1{hp}", [128, 1024], F32))
                            psAV = _e3(nc.psum_tensor(f"psAV{hp}", [65, 2048], F32))
                            psS = (psS0, psS1)
                            for lh in range(2):
                                h = hp * 2 + lh
                                r0 = lh * 64
                                sl = 0
                                for kc in range(NT):
                                    qlo = kc * 128
                                    pos = qlo
                                    while pos < T:
                                        qhi = min(pos + 1024, T)
                                        par = sl % 2

                                        @blk.tensor
                                        def _(t, kc=kc, pos=pos, qhi=qhi, par=par, r0=r0, sl=sl):
                                            t = _TTrack(t)
                                            S.wait(t, "dv")
                                            S.wait(t, "ac")
                                            for (a, b) in n512(pos, qhi):
                                                t.matmul(psS[par][:, a - pos:b - pos],
                                                         kp[r0:r0 + 64, kc * 128:(kc + 1) * 128],
                                                         qp[r0:r0 + 64, a:b], start=True, stop=True)
                                            S.inc(t.last_inst(), "pe")

                                        @blk.scalar
                                        def _(sc, kc=kc, pos=pos, qhi=qhi, par=par, lh=lh):
                                            sc = _Chain(sc, S, "acx")
                                            S.wait(sc, "pe")
                                            sc.activation(out=pt[:, par, 0:qhi - pos],
                                                          in_=psS[par][:, 0:qhi - pos], func=AF.Exp,
                                                          bias=mneg2[:, lh:lh + 1])
                                            S.inc(sc.drain(), "ac")

                                        if pos == qlo:
                                            @blk.vector
                                            def _(v, par=par):
                                                v = _Chain(v, S, "dvx")
                                                S.wait(v, "ac")
                                                v.tensor_tensor(pt[:, par, 0:128], pt[:, par, 0:128],
                                                                triD[:, :], OP.mult)
                                                S.inc(v.drain(), "dv")

                                        @blk.tensor
                                        def _(t, kc=kc, pos=pos, qhi=qhi, par=par, h=h):
                                            t = _TTrack(t)
                                            if pos == kc * 128:
                                                S.wait(t, "dv")
                                            S.wait(t, "ac")
                                            for (a, b) in n512a(pos, qhi):
                                                t.matmul(psAV[:, a:b], vtok[:, kc, h * 65:h * 65 + 65],
                                                         pt[:, par, a - pos:b - pos],
                                                         start=(kc == 0), stop=(kc == NT - 1),
                                                         skip_group_check=True)
                                            S.inc(t.last_inst(), "pe")
                                        pos = qhi
                                        sl += 1

                                @blk.scalar
                                def _(sc):
                                    sc = _Chain(sc, S, "acx")
                                    S.wait(sc, "pe")
                                    sc.activation(out=lex[64:65, :], in_=psAV[64:65, :], func=AF.Ln)
                                    sc.activation(out=lex[64:65, :], in_=lex[64:65, :], func=AF.Exp, scale=-1.0)
                                    S.inc(sc.drain(), "ac")

                                @blk.vector
                                def _(v):
                                    v = _Chain(v, S, "dvx")
                                    S.wait(v, "ac")
                                    v.tensor_copy(attn_un[:, :], psAV[0:64, :])
                                    S.inc(v.drain(), "dv")

                                @blk.tensor
                                def _(t):
                                    t = _TTrack(t)
                                    S.wait(t, "dv")
                                    for j in range(2):
                                        for (a, b) in n512(j * 1024, (j + 1) * 1024):
                                            t.matmul(psS[j][:, a - j * 1024:b - j * 1024],
                                                     ones_r64[64:65, :], lex[64:65, a:b],
                                                     start=True, stop=True)
                                    S.inc(t.last_inst(), "pe")

                                @blk.vector
                                def _(v, h=h):
                                    v = _Chain(v, S, "dvx")
                                    S.wait(v, "pe")
                                    for j in range(2):
                                        v.tensor_tensor(attnTn[(h % 2) * 64:(h % 2) * 64 + 64, h // 2,
                                                                j * 1024:(j + 1) * 1024],
                                                        attn_un[:, j * 1024:(j + 1) * 1024],
                                                        psS[j][0:64, 0:1024], OP.mult)
                                    S.inc(v.drain(), "dv")

            if upto == "attn":
                _fin()
                return nc
            with ExitStack() as ph:
                _e = ph.enter_context
                wpj = _e(nc.sbuf_tensor("wpj", [128, NCH, C], BF16))
                bpj_col = _e(nc.sbuf_tensor("bpj_col", [128, NCH], F32))
                pjf = _e(nc.sbuf_tensor("pjf", [128, 2, T], F32))
                psP = _e(nc.psum_tensor("psP", [128, 2048], F32))
                @blk.gpsimd
                def _(g):
                    g = _Chain(g, S, "gcx")
                    S.wait(g, "pe")
                    S.wait(g, "dv")
                    S.wait(g, "ac")
                    S.pre(g, "gp")
                    S.dma(g.dma_start(out=wpj[:, :, :],
                                      in_=w_proj.rearrange("(c p) f -> p c f", p=128)), "gp")

                @blk.sync
                def _(sy):
                    sy.wait_ge(S.h["pe"], S.n["pe"])
                    sy.wait_ge(S.h["dv"], S.n["dv"])
                    S.pre(sy, "ld")
                    S.dma(sy.dma_start(out=bpj_col[:, :], in_=bproj_col))
                mark_pj = (S.n["gp"], S.n["ld"])

                gp_prev = [None, None]
                for ct in range(NCH):
                    @blk.tensor
                    def _(t, ct=ct):
                        t = _TTrack(t)
                        if ct == 0:
                            t.wait_ge(S.h["gp"], mark_pj[0])
                        S.wait(t, "dv")
                        if ct >= 2:
                            t.wait_ge(S.h[f"ga{ct % 2}"], gp_prev[ct % 2])
                        for cc in range(NCH):
                            for (a, b) in n512(0, T):
                                t.matmul(psP[:, a:b], wpj[:, cc, ct * 128:(ct + 1) * 128],
                                         attnTn[:, cc, a:b], start=(cc == 0), stop=(cc == NCH - 1))
                        S.inc(t.last_inst(), "pe")

                    @blk.vector
                    def _(v, ct=ct):
                        v = _Chain(v, S, "dvx")
                        S.wait(v, "pe")
                        v.wait_ge(S.h["ld"], mark_pj[1])
                        v.tensor_scalar(pjf[:, ct % 2, :], psP[:, :], bpj_col[:, ct:ct + 1], None, OP.add)
                        S.inc(v.drain(), "dv")

                    @blk.gpsimd
                    def _(g, ct=ct):
                        g = _Chain(g, S, "gcx")
                        S.wait(g, "dv")
                        if ct == 0:
                            g.wait_ge(S.h["sp0"], S.n["sp0"])
                            g.wait_ge(S.h["sp1"], S.n["sp1"])
                        S.dma(g.dma_start(out=partial[ct * 128:(ct + 1) * 128, :], in_=pjf[:, ct % 2, :],
                                          accum_op=OP.add), f"ga{ct % 2}")
                        gp_prev[ct % 2] = S.n[f"ga{ct % 2}"]

            if upto == "proj":
                _fin()
                return nc
            with ExitStack() as ph:
                _e = ph.enter_context
                swg_sb = _e(nc.sbuf_tensor("swg_sb", [128, NCH, IDIM], BF16))
                swu_sb = _e(nc.sbuf_tensor("swu_sb", [128, NCH, IDIM], BF16))
                swd_sb = _e(nc.sbuf_tensor("swd_sb", [128, 4, C], BF16))
                gb = _e(nc.sbuf_tensor("gb", [128, T], F32))
                g_row = _e(nc.sbuf_tensor("g_row", [1, T], F32))
                hactT = _e(nc.sbuf_tensor("hactT", [128, 4, T], BF16))
                pgf = _e(nc.sbuf_tensor("pgf", [128, T], F32))
                sgf = _e(nc.sbuf_tensor("sgf", [128, T], F32))
                dnf = _e(nc.sbuf_tensor("dnf", [128, 2, T], F32))
                psG = _e(nc.psum_tensor("psG", [128, 2048], F32))
                psU = _e(nc.psum_tensor("psU", [128, 2048], F32))

                @blk.gpsimd
                def _(g):
                    g = _Chain(g, S, "gcx")
                    S.wait(g, "pe")
                    S.wait(g, "dv")
                    S.wait(g, "ga0")
                    S.wait(g, "ga1")
                    S.pre(g, "gp")
                    S.dma(g.dma_start(out=swg_sb[:, :, :],
                                      in_=sw_gate.rearrange("(c p) i -> p c i", p=128)), "gp")
                    S.pre(g, "gp")
                    S.dma(g.dma_start(out=swu_sb[:, :, :],
                                      in_=sw_up.rearrange("(c p) i -> p c i", p=128)), "gp")
                    S.pre(g, "gp")
                    S.dma(g.dma_start(out=swd_sb[:, :, :],
                                      in_=sw_down.rearrange("(c p) i -> p c i", p=128)), "gp")
                gp_sw = S.n["gp"]

                @blk.vector
                def _(v):
                    v = _Chain(v, S, "dvx")
                    v.wait_ge(S.h["gp"], gp_sw)
                    for c_ in range(NCH):
                        v.tensor_scalar(swg_sb[:, c_, :], swg_sb[:, c_, :], rmsw_col[:, c_:c_ + 1], None, OP.mult)
                        v.tensor_scalar(swu_sb[:, c_, :], swu_sb[:, c_, :], rmsw_col[:, c_:c_ + 1], None, OP.mult)
                    S.inc(v.drain(), "dv")

                @blk.tensor
                def _(t):
                    t = _TTrack(t)
                    S.wait(t, "dv")
                    for tt in range(NT):
                        t.transpose(psG[0:1, tt * 128:(tt + 1) * 128], g_cols[:, tt:tt + 1], iden[:, :])
                    S.inc(t.last_inst(), "pe")

                @blk.vector
                def _(v):
                    v = _Chain(v, S, "dvx")
                    S.wait(v, "pe")
                    v.tensor_copy(g_row[:, :], psG[0:1, 0:T])
                    S.inc(v.drain(), "dv")

                @blk.tensor
                def _(t):
                    t = _TTrack(t)
                    S.wait(t, "dv")
                    for (a, b) in n512(0, T):
                        t.matmul(psG[:, a:b], ones_col[:, :], g_row[:, a:b], start=True, stop=True)
                    S.inc(t.last_inst(), "pe")

                @blk.vector
                def _(v):
                    v = _Chain(v, S, "dvx")
                    S.wait(v, "pe")
                    v.tensor_copy(gb[:, :], psG[:, :])
                    S.inc(v.drain(), "dv")
                dv_gb = S.n["dv"]

                for it in range(4):
                    @blk.tensor
                    def _(t, it=it):
                        t = _TTrack(t)
                        t.wait_ge(S.h["dv"], dv_gb)
                        if it >= 1:
                            S.wait(t, "dv")
                        for cc in range(NCH):
                            for (a, b) in n512(0, T):
                                t.matmul(psG[:, a:b], swg_sb[:, cc, it * 128:(it + 1) * 128],
                                         xnT[:, cc, a:b], start=(cc == 0), stop=(cc == NCH - 1))
                        for cc in range(NCH):
                            for (a, b) in n512(0, T):
                                t.matmul(psU[:, a:b], swu_sb[:, cc, it * 128:(it + 1) * 128],
                                         xnT[:, cc, a:b], start=(cc == 0), stop=(cc == NCH - 1))
                        S.inc(t.last_inst(), "pe")

                    @blk.vector
                    def _(v, it=it):
                        v = _Chain(v, S, "dvx")
                        S.wait(v, "pe")
                        v.tensor_tensor(pgf[:, :], psG[:, :], gb[:, :], OP.mult)
                        S.inc(v.drain(), "dv")

                    @blk.scalar
                    def _(sc, it=it):
                        sc = _Chain(sc, S, "acx")
                        S.wait(sc, "dv")
                        sc.activation(out=sgf[:, :], in_=pgf[:, :], func=AF.Silu)
                        S.inc(sc.drain(), "ac")

                    @blk.vector
                    def _(v, it=it):
                        v = _Chain(v, S, "dvx")
                        S.wait(v, "ac")
                        v.tensor_tensor(pgf[:, :], psU[:, :], gb[:, :], OP.mult)
                        v.tensor_tensor(hactT[:, it, :], sgf[:, :], pgf[:, :], OP.mult)
                        S.inc(v.drain(), "dv")

                gp_dn = [None, None]
                for hc in range(NCH):
                    @blk.tensor
                    def _(t, hc=hc):
                        t = _TTrack(t)
                        S.wait(t, "dv")
                        if hc >= 2:
                            t.wait_ge(S.h[f"ga{hc % 2}"], gp_dn[hc % 2])
                        for ic in range(4):
                            for (a, b) in n512(0, T):
                                t.matmul(psG[:, a:b], swd_sb[:, ic, hc * 128:(hc + 1) * 128],
                                         hactT[:, ic, a:b], start=(ic == 0), stop=(ic == 3))
                        S.inc(t.last_inst(), "pe")

                    @blk.vector
                    def _(v, hc=hc):
                        v = _Chain(v, S, "dvx")
                        S.wait(v, "pe")
                        v.tensor_copy(dnf[:, hc % 2, :], psG[:, :])
                        S.inc(v.drain(), "dv")

                    @blk.gpsimd
                    def _(g, hc=hc):
                        g = _Chain(g, S, "gcx")
                        S.wait(g, "dv")
                        if hc == 0:
                            g.wait_ge(S.h["ga0"], S.n["ga0"])
                            g.wait_ge(S.h["ga1"], S.n["ga1"])
                        S.dma(g.dma_start(out=partial[hc * 128:(hc + 1) * 128, :], in_=dnf[:, hc % 2, :],
                                          accum_op=OP.add), f"ga{hc % 2}")
                        gp_dn[hc % 2] = S.n[f"ga{hc % 2}"]

            @blk.gpsimd
            def _(g):
                g = _Chain(g, S, "gcx")
                for nm in ("gp", "st", "sx0", "sx1", "sp0", "sp1", "ga0", "ga1"):
                    S.wait(g, nm)
                g.nop()



SEM_NAMES = ["ld", "pe", "dv", "ac", "gp", "gc", "st",
             "lt0", "lt1", "lc0", "sx0", "sx1", "sp0", "sp1", "ga0", "ga1",
             "sct0", "sct1", "tr0", "tr1", "wa0", "wa1", "yb0", "yb1",
             "g0", "g1", "lp0", "lp1", "so0", "so1", "tx0", "tx1",
             "cc", "dvx", "acx", "gcx"]


def _stage1_io(nc):
    io = {}
    io["xT"] = nc.dram_tensor("xT", [C, T], F32, kind="ExternalInput").ap()
    io["xtok"] = nc.dram_tensor("xtok", [T, C], F32, kind="ExternalInput").ap()
    io["w_attn"] = nc.dram_tensor("w_attn", [C, 3 * C], F32, kind="ExternalInput").ap()
    io["w_attnT"] = nc.dram_tensor("w_attnT", [3 * C, C], F32, kind="ExternalInput").ap()
    io["b_attn"] = nc.dram_tensor("b_attn", [1, 3 * C], F32, kind="ExternalInput").ap()
    io["w_proj"] = nc.dram_tensor("w_proj", [C, C], F32, kind="ExternalInput").ap()
    io["b_proj"] = nc.dram_tensor("b_proj", [1, C], F32, kind="ExternalInput").ap()
    io["bproj_col"] = nc.dram_tensor("bproj_col", [128, NCH], F32, kind="ExternalInput").ap()
    io["wk_cT"] = nc.dram_tensor("wk_cT", [32, 64], F32, kind="ExternalInput").ap()
    io["wk_e"] = nc.dram_tensor("wk_e", [32, 64], F32, kind="ExternalInput").ap()
    io["wv_cT"] = nc.dram_tensor("wv_cT", [32, 64], F32, kind="ExternalInput").ap()
    io["wv_e"] = nc.dram_tensor("wv_e", [32, 64], F32, kind="ExternalInput").ap()
    io["rms_w"] = nc.dram_tensor("rms_w", [1, C], F32, kind="ExternalInput").ap()
    io["rms_w_col"] = nc.dram_tensor("rms_w_col", [128, NCH], F32, kind="ExternalInput").ap()
    io["battn_col"] = nc.dram_tensor("battn_col", [128, 24], F32, kind="ExternalInput").ap()
    io["gate_wT"] = nc.dram_tensor("gate_wT", [C, E], F32, kind="ExternalInput").ap()
    io["sw_gate"] = nc.dram_tensor("sw_gate", [C, IDIM], F32, kind="ExternalInput").ap()
    io["sw_up"] = nc.dram_tensor("sw_up", [C, IDIM], F32, kind="ExternalInput").ap()
    io["sw_down"] = nc.dram_tensor("sw_down", [IDIM, C], F32, kind="ExternalInput").ap()
    io["partial"] = nc.dram_tensor("partial", [C, T], F32, kind="ExternalOutput").ap()
    io["xm_out"] = nc.dram_tensor("xm_out", [T, C], BF16, kind="ExternalOutput").ap()
    io["meta"] = nc.dram_tensor("meta", [T, 6], F32, kind="ExternalOutput").ap()
    io["counts"] = nc.dram_tensor("counts", [1, E], F32, kind="ExternalOutput").ap()
    return io


def build_stage1(upto="all"):
    nc = bass.Bass("TRN2", target_bir_lowering=False, debug=False)
    io = _stage1_io(nc)
    st0 = ExitStack()
    with st0:
        S = Sems(st0, nc, SEM_NAMES)
        with nc.Block() as blk:
            emit_stage1(_PrefNC(nc, "s1_"), blk, S, io, upto)
    return nc


def _prep_stage1_inputs(inputs):
    f32 = np.float32
    com = {
        "w_attn": np.ascontiguousarray(inputs["L0_attn_c_attn_w"], f32),
        "w_attnT": np.ascontiguousarray(inputs["L0_attn_c_attn_w"].T, f32),
        "b_attn": np.ascontiguousarray(inputs["L0_attn_c_attn_b"].reshape(1, -1), f32),
        "w_proj": np.ascontiguousarray(inputs["L0_attn_c_proj_w"], f32),
        "b_proj": np.ascontiguousarray(inputs["L0_attn_c_proj_b"].reshape(1, -1), f32),
        "wk_cT": np.ascontiguousarray(inputs["L0_attn_wk_c"].T, f32),
        "wk_e": np.ascontiguousarray(inputs["L0_attn_wk_e"], f32),
        "wv_cT": np.ascontiguousarray(inputs["L0_attn_wv_c"].T, f32),
        "wv_e": np.ascontiguousarray(inputs["L0_attn_wv_e"], f32),
        "rms_w": np.ascontiguousarray(inputs["L0_ffn_rms_w"].reshape(1, -1), f32),
        "rms_w_col": np.ascontiguousarray(inputs["L0_ffn_rms_w"].reshape(8, 128).T, f32),
        "battn_col": np.ascontiguousarray(inputs["L0_attn_c_attn_b"].reshape(24, 128).T, f32),
        "bproj_col": np.ascontiguousarray(inputs["L0_attn_c_proj_b"].reshape(8, 128).T, f32),
        "gate_wT": np.ascontiguousarray(inputs["L0_ffn_gate_w"].T, f32),
        "sw_gate": np.ascontiguousarray(inputs["L0_ffn_sw_gate"], f32),
        "sw_up": np.ascontiguousarray(inputs["L0_ffn_sw_up"], f32),
        "sw_down": np.ascontiguousarray(inputs["L0_ffn_sw_down"], f32),
    }
    x = np.asarray(inputs["L0_attn_hidden_states"], f32)
    maps = []
    for b in range(B):
        m = dict(com)
        m["xT"] = np.ascontiguousarray(x[b].T)
        m["xtok"] = np.ascontiguousarray(x[b])
        maps.append(m)
    return maps


def run_stage1(inputs, upto="all", **kw):
    nc = build_stage1(upto)
    maps = _prep_stage1_inputs(inputs)
    res = run_bass_kernel_spmd(nc, maps, core_ids=list(range(B)), **kw)
    return res



def emit_stage2(nc, blk, S, io):
    NTOK = B * T
    NTL = NTOK // 128
    xm_all = io["xm_all"]
    meta_all = io["meta_all"]
    counts_all = io["counts_all"]
    w13 = io["w13"]
    w2w = io["w2w"]
    ebase = io["ebase"]
    yb = io["yb"]
    disp = io["disp"]
    st = ExitStack()
    with st:
        sbt = lambda *a: st.enter_context(nc.sbuf_tensor(*a))

        meta_sb = sbt("meta_sb", [128, NTL, 6], F32)
        cnt8 = sbt("cnt8", [8, E], F32)
        offs = sbt("offs", [8, E], F32)
        offb = sbt("offb", [128, 8, E], F32)
        iota64 = sbt("iota64", [128, E], F32)
        triU8 = sbt("triU8", [8, 8], F32)
        ones_col = sbt("ones_col", [1, 128], F32)
        ebase_c = sbt("ebase_c", [128, 1], F32)
        slots = sbt("slots", [128, NTL, 2], I32)
        oh = sbt("oh", [128, E], F32)
        colA = sbt("colA", [128, 8], F32)
        xmt = sbt("xmt", [128, 2, C], BF16)
        bufT = sbt("bufT", [128, 2, NCH, CAP], BF16)
        w13_sb = sbt("w13_sb", [128, 2, NCH, 2 * IDIM], BF16)
        w2_sb = sbt("w2_sb", [128, 2, 4, C], BF16)
        hactT = sbt("hactT", [128, 4, CAP], BF16)
        sgf = sbt("sgf", [128, CAP], F32)
        ybf = sbt("ybf", [128, 2, C], BF16)

        with _passthrough(blk) as blk:
            def _fin2():
                @blk.gpsimd
                def _(g):
                    for nm in ("ld", "gp", "st", "sct0", "sct1", "lt0", "lt1", "tr0", "tr1",
                               "wa0", "wa1", "yb0", "yb1"):
                        S.wait(g, nm)
                    for nm in ("pe", "dv", "ac", "dvx", "acx", "gcx"):
                        S.wait(g, nm)
                    g.nop()

            @blk.gpsimd
            def _(g):
                g = _Chain(g, S, "gcx")
                g.memset(ones_col[:, :], 1.0)
                g.memset(triU8[:, :], 1.0)
                g.affine_select(triU8[:, :], triU8[:, :], [[1, 8]], OP.is_ge, 0.0,
                                base=-1, channel_multiplier=-1)
                g.iota(iota64[:, :], [[1, E]], base=0, channel_multiplier=0,
                       allow_small_or_imprecise_dtypes=True)
                S.inc(g.drain(), "gc" if False else "gcx")

            @blk.sync
            def _(sy):
                S.pre(sy, "ld")
                S.dma(sy.dma_start(out=meta_sb[:, :, :], in_=meta_all.rearrange("(n p) c -> p n c", p=128)))
                S.pre(sy, "ld")
                S.dma(sy.dma_start(out=cnt8[:, :], in_=counts_all))
                S.pre(sy, "ld")
                S.dma(sy.dma_start(out=ebase_c[:, :], in_=ebase))
            ld_meta = S.n["ld"]

            with nc.psum_tensor("ps2a", [128, 512], F32) as psA:
                @blk.tensor
                def _(t):
                    t = _TTrack(t)
                    t.wait_ge(S.h["ld"], ld_meta)
                    S.wait(t, "gcx")
                    t.matmul(psA[0:8, 0:E], triU8[:, :], cnt8[:, :], start=True, stop=True)
                    S.inc(t.last_inst(), "pe")

                @blk.vector
                def _(v):
                    v = _Chain(v, S, "dvx")
                    S.wait(v, "pe")
                    v.tensor_copy(offs[:, :], psA[0:8, 0:E])
                    S.inc(v.drain(), "dv")

                for c_ in range(8):
                    @blk.gpsimd
                    def _(g, c_=c_):
                        g = _Chain(g, S, "gcx")
                        S.wait(g, "dv")
                        S.pre(g, "gp")
                        S.dma(g.dma_start(out=colA[0:1, 0:E] if False else oh[0:1, :], in_=offs[c_:c_ + 1, :]), "gp")

                    @blk.tensor
                    def _(t, c_=c_):
                        t = _TTrack(t)
                        S.wait(t, "gp")
                        S.wait(t, "dv")
                        t.matmul(psA[:, 0:E], ones_col[:, :], oh[0:1, :], start=True, stop=True)
                        S.inc(t.last_inst(), "pe")

                    @blk.vector
                    def _(v, c_=c_):
                        v = _Chain(v, S, "dvx")
                        S.wait(v, "pe")
                        v.tensor_copy(offb[:, c_, :], psA[:, 0:E])
                        S.inc(v.drain(), "dv")

            @blk.vector
            def _(v):
                v = _Chain(v, S, "dvx")
                S.wait(v, "dv")
                for n in range(NTL):
                    cg = n // 16
                    for k in range(2):
                        v.tensor_scalar(oh[:, :], iota64[:, :], meta_sb[:, n, k:k + 1], None, OP.is_equal)
                        v.tensor_tensor(oh[:, :], oh[:, :], offb[:, cg, :], OP.mult)
                        v.reduce_sum(colA[:, 0:1], oh[:, :], AX)
                        v.tensor_tensor(colA[:, 1:2], colA[:, 0:1], meta_sb[:, n, 4 + k:5 + k], OP.add)
                        v.tensor_scalar(colA[:, 2:3], colA[:, 1:2], 1024.0, 20000.0, OP.is_ge, OP.mult)
                        v.tensor_scalar(colA[:, 3:4], meta_sb[:, n, k:k + 1], ebase_c[:, 0:1], None, OP.subtract)
                        v.tensor_scalar(colA[:, 4:5], colA[:, 3:4], 0.0, None, OP.is_ge)
                        v.tensor_scalar(colA[:, 5:6], colA[:, 3:4], 7.0, None, OP.is_le)
                        v.tensor_tensor(colA[:, 4:5], colA[:, 4:5], colA[:, 5:6], OP.mult)
                        v.tensor_scalar(colA[:, 4:5], colA[:, 4:5], -100000.0, 100000.0, OP.mult, OP.add)
                        v.tensor_scalar(colA[:, 6:7], colA[:, 3:4], 1024.0, None, OP.mult)
                        v.tensor_tensor(colA[:, 6:7], colA[:, 6:7], colA[:, 1:2], OP.add)
                        v.tensor_tensor(colA[:, 6:7], colA[:, 6:7], colA[:, 2:3], OP.add)
                        v.tensor_tensor(colA[:, 6:7], colA[:, 6:7], colA[:, 4:5], OP.add)
                        v.tensor_copy(slots[:, n, k:k + 1], colA[:, 6:7])
                S.inc(v.drain(), "dv")
            dv_slots = S.n["dv"]

            zt = st.enter_context(nc.sbuf_tensor("zt", [128, C], BF16))

            @blk.gpsimd
            def _(g):
                g = _Chain(g, S, "gcx")
                g.memset(zt[:, :], 0.0)
                S.inc(g.drain(), "gc2" if False else "gcx")

            @blk.sync
            def _(sy):
                S.wait(sy, "gcx")
                S.dma(sy.dma_start(out=disp.rearrange("(j p) c -> p j c", p=128),
                                   in_=zt[:, :].rearrange("p (j c) -> p j c", j=1).to_broadcast([128, 64, C])
                                   if hasattr(zt[:, :], "to_broadcast") else zt[:, :]), "st")

            bc_reg = [None]

            @blk.gpsimd
            def _(g):
                r = g.alloc_register()
                g.reg_mov(r, 8 * CAP - 1)
                bc_reg[0] = r

            sct_marks = []
            for n in range(NTL):
                @blk.sync
                def _(sy, n=n):
                    if n >= 2:
                        sy.wait_ge(S.h[f"sct{n % 2}"], sct_marks[n - 2])
                    S.dma(sy.dma_start(out=xmt[:, n % 2, :],
                                       in_=xm_all.rearrange("(n p) c -> p n c", p=128)[:, n, :]),
                          f"lt{n % 2}")

                @blk.gpsimd
                def _(g, n=n):
                    g.wait_ge(S.h[f"lt{n % 2}"], S.n[f"lt{n % 2}"])
                    if n == 0:
                        g.wait_ge(S.h["dv"], dv_slots)
                        S.wait(g, "st")
                    for k in range(2):
                        S.dma(g.indirect_dma_start(
                            out=disp, out_offset=IndirectOffsetOnAxis(slots[:, n, k:k + 1], axis=0),
                            in_=xmt[:, n % 2, :], in_offset=None,
                            bounds_check=bc_reg[0], oob_is_err=False), f"sct{n % 2}")
                    sct_marks.append(S.n[f"sct{n % 2}"])

            disp_r = disp.rearrange("(e r) c -> e r c", e=8)
            for e in range(8):
                par = e % 2

                @blk.sync
                def _(sy, e=e, par=par):
                    sy.wait_ge(S.h["sct0"], S.n["sct0"])
                    sy.wait_ge(S.h["sct1"], S.n["sct1"])
                    if e >= 2:
                        S.wait(sy, "pe")
                    for cc in range(NCH):
                        S.pre(sy, f"tr{par}")
                        S.dma(sy.dma_start(out=bufT[:, par, cc, :],
                                           in_=disp_r[e, :, cc * 128:(cc + 1) * 128],
                                           transpose=True), f"tr{par}")

                @blk.gpsimd
                def _(g, e=e, par=par):
                    if e >= 2:
                        S.wait(g, "pe")
                    S.pre(g, f"wa{par}")
                    S.dma(g.dma_start(out=w13_sb[:, par, :, :],
                                      in_=w13[e, :, :].rearrange("(c p) f -> p c f", p=128)), f"wa{par}")
                    S.pre(g, f"wa{par}")
                    S.dma(g.dma_start(out=w2_sb[:, par, :, :],
                                      in_=w2w[e, :, :].rearrange("(c p) f -> p c f", p=128)), f"wa{par}")

                with nc.psum_tensor(f"psG{e}", [128, 1024], F32) as psG, \
                     nc.psum_tensor(f"psU{e}", [128, 1024], F32) as psU, \
                     nc.psum_tensor(f"psY{e}", [128, 1024], F32) as psY:
                    for it in range(4):
                        @blk.tensor
                        def _(t, e=e, par=par, it=it):
                            t = _TTrack(t)
                            t.wait_ge(S.h[f"tr{par}"], S.n[f"tr{par}"])
                            t.wait_ge(S.h[f"wa{par}"], S.n[f"wa{par}"])
                            S.wait(t, "dv")
                            S.wait(t, "ac")
                            for cc in range(NCH):
                                for (a, b) in n512(0, CAP):
                                    t.matmul(psG[:, a:b], w13_sb[:, par, cc, it * 128:(it + 1) * 128],
                                             bufT[:, par, cc, a:b], start=(cc == 0), stop=(cc == NCH - 1))
                            for cc in range(NCH):
                                for (a, b) in n512(0, CAP):
                                    t.matmul(psU[:, a:b], w13_sb[:, par, cc, IDIM + it * 128:IDIM + (it + 1) * 128],
                                             bufT[:, par, cc, a:b], start=(cc == 0), stop=(cc == NCH - 1))
                            S.inc(t.last_inst(), "pe")

                        @blk.scalar
                        def _(sc, it=it):
                            sc = _Chain(sc, S, "acx")
                            S.wait(sc, "pe")
                            sc.activation(out=sgf[:, :], in_=psG[:, :], func=AF.Silu)
                            S.inc(sc.drain(), "ac")

                        @blk.vector
                        def _(v, it=it):
                            v = _Chain(v, S, "dvx")
                            S.wait(v, "ac")
                            v.tensor_tensor(hactT[:, it, :], sgf[:, :], psU[:, :], OP.mult)
                            S.inc(v.drain(), "dv")

                    for stt in range(8):
                        @blk.tensor
                        def _(t, e=e, par=par, stt=stt):
                            t = _TTrack(t)
                            S.wait(t, "dv")
                            if stt >= 2:
                                S.wait(t, "dv")
                            for ic in range(4):
                                for (a, b) in n512(0, C):
                                    t.matmul(psY[:, a:b], hactT[:, ic, stt * 128:(stt + 1) * 128],
                                             w2_sb[:, par, ic, a:b], start=(ic == 0), stop=(ic == 3))
                            S.inc(t.last_inst(), "pe")

                        @blk.vector
                        def _(v, e=e, stt=stt):
                            v = _Chain(v, S, "dvx")
                            S.wait(v, "pe")
                            v.wait_ge(S.h[f"yb{stt % 2}"], S.n[f"yb{stt % 2}"])
                            v.tensor_copy(ybf[:, stt % 2, :], psY[:, :])
                            S.inc(v.drain(), "dv")

                        @blk.sync
                        def _(sy, e=e, stt=stt):
                            S.wait(sy, "dv")
                            S.dma(sy.dma_start(out=yb[e * CAP + stt * 128:e * CAP + (stt + 1) * 128, :],
                                               in_=ybf[:, stt % 2, :]), f"yb{stt % 2}")

            _fin2()



def _stage2_io(nc):
    io = {}
    io["xm_all"] = nc.dram_tensor("xm_all", [B * T, C], BF16, kind="ExternalInput").ap()
    io["meta_all"] = nc.dram_tensor("meta_all", [B * T, 6], F32, kind="ExternalInput").ap()
    io["counts_all"] = nc.dram_tensor("counts_all", [8, E], F32, kind="ExternalInput").ap()
    io["w13"] = nc.dram_tensor("w13", [8, C, 2 * IDIM], F32, kind="ExternalInput").ap()
    io["w2w"] = nc.dram_tensor("w2w", [8, IDIM, C], F32, kind="ExternalInput").ap()
    io["ebase"] = nc.dram_tensor("ebase", [128, 1], F32, kind="ExternalInput").ap()
    io["yb"] = nc.dram_tensor("yb", [8 * CAP, C], BF16, kind="ExternalOutput").ap()
    io["disp"] = nc.dram_tensor("disp", [8 * CAP, C], BF16, kind="Internal").ap()
    return io


def build_stage2():
    nc = bass.Bass("TRN2", target_bir_lowering=False, debug=False)
    io = _stage2_io(nc)
    st0 = ExitStack()
    with st0:
        S = Sems(st0, nc, SEM_NAMES)
        with nc.Block() as blk:
            emit_stage2(_PrefNC(nc, "s2_"), blk, S, io)
    return nc


def _prep_stage2_inputs(inputs, s1res):
    f32 = np.float32
    xm_all = np.concatenate([r["xm_out"] for r in s1res], axis=0)
    meta_all = np.concatenate([r["meta"] for r in s1res], axis=0).astype(f32)
    counts_all = np.concatenate([r["counts"] for r in s1res], axis=0).astype(f32)
    w13 = np.ascontiguousarray(inputs["L0_ffn_w13"], f32)
    w2w = np.ascontiguousarray(inputs["L0_ffn_w2"], f32)
    maps = []
    for i in range(8):
        maps.append({
            "xm_all": xm_all, "meta_all": meta_all, "counts_all": counts_all,
            "w13": np.ascontiguousarray(w13[8 * i:8 * i + 8]),
            "w2w": np.ascontiguousarray(w2w[8 * i:8 * i + 8]),
            "ebase": np.full((128, 1), 8.0 * i, f32),
        })
    return maps


def run_stage2(inputs, s1res, **kw):
    nc = build_stage2()
    maps = _prep_stage2_inputs(inputs, s1res)
    return run_bass_kernel_spmd(nc, maps, core_ids=list(range(8)), **kw)



def emit_stage3(nc, blk, S, io):
    partialT = io["partialT"]
    yb_all = io["yb_all"]
    meta_in = io["meta_in"]
    counts_all = io["counts_all"]
    bsel = io["bsel"]
    wqT = io["wqT"]
    wkT = io["wkT"]
    wv = io["wv"]
    wo = io["wo"]
    bo = io["bo"]
    wf2 = io["wf2"]
    bfq_col = io["bfq_col"]
    bf_row = io["bf_row"]
    g1r_R = io["g1r_R"]
    g2r_R = io["g2r_R"]
    g1r_I = io["g1r_I"]
    g2r_I = io["g2r_I"]
    bri_row = io["bri_row"]
    out = io["out"]
    a2R = io["a2R"]
    a2I = io["a2I"]
    st = ExitStack()
    with st:
        sbt = lambda *a: st.enter_context(nc.sbuf_tensor(*a))

        x1 = sbt("x1", [128, NT, C], F32)
        xn2T = sbt("xn2T", [128, NCH, T], BF16)
        ones_col = sbt("ones_col", [1, 128], F32)
        ones_r64 = sbt("ones_r64", [128, 128], F32)
        iden = sbt("iden", [128, 128], F32)
        iota64 = sbt("iota64", [128, E], F32)
        triU8 = sbt("triU8", [8, 8], F32)
        offb = sbt("offb", [128, E], F32)
        m2c = sbt("m2c", [128, NT], F32)
        r2c = sbt("r2c", [128, NT], F32)
        meta_sb = sbt("meta_sb", [128, NT, 6], F32)
        slots = sbt("slots", [128, NT, 2], I32)
        colA = sbt("colA", [128, 8], F32)
        oh = sbt("oh", [128, E], F32)
        kv_sb = sbt("kv_sb", [128, NH, 65], BF16)
        bo_b = sbt("bo_b", [128, C], F32)
        bf_b = sbt("bf_b", [128, C], F32)
        bri_b = sbt("bri_b", [128, C], F32)
        lex = sbt("lex", [128, T], F32)
        bfq_sb = sbt("bfq_sb", [128, 8], F32)

        with _passthrough(blk) as blk:
            def _fin3():
                @blk.gpsimd
                def _(g):
                    for nm in ("ld", "gp", "st", "g0", "g1", "lp0", "lp1",
                               "so0", "so1", "tx0", "tx1"):
                        S.wait(g, nm)
                    for nm in ("pe", "dv", "ac", "dvx", "acx", "gcx"):
                        S.wait(g, nm)
                    g.nop()

            @blk.gpsimd
            def _(g):
                g = _Chain(g, S, "gcx")
                g.memset(ones_col[:, :], 1.0)
                g.memset(ones_r64[:, :], 0.0)
                g.memset(ones_r64[64:65, :], 1.0)
                g.memset(iden[:, :], 1.0)
                g.affine_select(iden[:, :], iden[:, :], [[-1, 128]], OP.is_equal, 0.0,
                                base=0, channel_multiplier=1)
                g.iota(iota64[:, :], [[1, E]], base=0, channel_multiplier=0,
                       allow_small_or_imprecise_dtypes=True)
                g.memset(triU8[:, :], 1.0)
                g.affine_select(triU8[:, :], triU8[:, :], [[1, 8]], OP.is_ge, 0.0,
                                base=-1, channel_multiplier=-1)
                S.inc(g.drain(), "gcx")

            @blk.sync
            def _(sy):
                for (dst, srcap) in ((meta_sb[:, :, :], meta_in.rearrange("(n p) c -> p n c", p=128)),
                                     (colA[0:8, 0:1] , bsel), (oh[0:8, :], counts_all),
                                     (bo_b[0:1, :], bo), (bf_b[0:1, :], bf_row),
                                     (bri_b[0:1, :], bri_row),
                                     (bfq_sb[:, :], bfq_col)):
                    S.pre(sy, "ld")
                    S.dma(sy.dma_start(out=dst, in_=srcap), "ld")
            ld0 = S.n["ld"]

            with nc.psum_tensor("ps3o", [128, 1024], F32) as psO:
                @blk.tensor
                def _(t):
                    t = _TTrack(t)
                    t.wait_ge(S.h["ld"], ld0)
                    S.wait(t, "gcx")
                    t.matmul(psO[0:8, 0:E], triU8[:, :], oh[0:8, :], start=True, stop=True)
                    S.inc(t.last_inst(), "pe")

                @blk.vector
                def _(v):
                    v = _Chain(v, S, "dvx")
                    S.wait(v, "pe")
                    v.tensor_copy(oh[0:8, :], psO[0:8, 0:E])
                    S.inc(v.drain(), "dv")

                @blk.tensor
                def _(t):
                    t = _TTrack(t)
                    S.wait(t, "dv")
                    t.matmul(psO[0:1, 0:E], colA[0:8, 0:1], oh[0:8, :], start=True, stop=True)
                    S.inc(t.last_inst(), "pe")

                @blk.vector
                def _(v):
                    v = _Chain(v, S, "dvx")
                    S.wait(v, "pe")
                    v.tensor_copy(oh[0:1, :], psO[0:1, 0:E])
                    S.inc(v.drain(), "dv")

                @blk.tensor
                def _(t):
                    t = _TTrack(t)
                    S.wait(t, "dv")
                    t.matmul(psO[:, 0:E], ones_col[:, :], oh[0:1, :], start=True, stop=True)
                    for (a, b) in n512(0, C):
                        t.matmul(psO[:, 512:1024] if False else psO[:, a:b], ones_col[:, :], bo_b[0:1, a:b],
                                 start=True, stop=True) if False else None
                    S.inc(t.last_inst(), "pe")

                @blk.vector
                def _(v):
                    v = _Chain(v, S, "dvx")
                    S.wait(v, "pe")
                    v.tensor_copy(offb[:, :], psO[:, 0:E])
                    S.inc(v.drain(), "dv")

                for (rowt, bt) in ((bo_b, bo_b), (bf_b, bf_b), (bri_b, bri_b)):
                    @blk.tensor
                    def _(t, rowt=rowt):
                        t = _TTrack(t)
                        S.wait(t, "dv")
                        for (a, b) in n512(0, C):
                            t.matmul(psO[:, a:b], ones_col[:, :], rowt[0:1, a:b], start=True, stop=True)
                        S.inc(t.last_inst(), "pe")

                    @blk.vector
                    def _(v, bt=bt):
                        v = _Chain(v, S, "dvx")
                        S.wait(v, "pe")
                        v.tensor_copy(bt[:, :], psO[:, 0:C])
                        S.inc(v.drain(), "dv")

            @blk.vector
            def _(v):
                v = _Chain(v, S, "dvx")
                S.wait(v, "dv")
                for n in range(NT):
                    for k in range(2):
                        v.tensor_scalar(oh[:, :], iota64[:, :], meta_sb[:, n, k:k + 1], None, OP.is_equal)
                        v.tensor_tensor(oh[:, :], oh[:, :], offb[:, :], OP.mult)
                        v.reduce_sum(colA[:, 0:1], oh[:, :], AX)
                        v.tensor_tensor(colA[:, 1:2], colA[:, 0:1], meta_sb[:, n, 4 + k:5 + k], OP.add)
                        v.tensor_scalar(colA[:, 2:3], colA[:, 1:2], 1024.0, 70000.0, OP.is_ge, OP.mult)
                        v.tensor_scalar(colA[:, 3:4], meta_sb[:, n, k:k + 1], 1024.0, None, OP.mult)
                        v.tensor_tensor(colA[:, 3:4], colA[:, 3:4], colA[:, 1:2], OP.add)
                        v.tensor_tensor(colA[:, 3:4], colA[:, 3:4], colA[:, 2:3], OP.add)
                        v.tensor_copy(slots[:, n, k:k + 1], colA[:, 3:4])
                S.inc(v.drain(), "dv")
            dv_slots3 = S.n["dv"]

            with ExitStack() as ph:
                _e = ph.enter_context
                ptt = _e(nc.sbuf_tensor("ptt", [128, 2, T], F32))
                yg = _e(nc.sbuf_tensor("yg", [128, 2, 2, C], BF16))
                psT3 = _e(nc.psum_tensor("ps3t", [128, 2048], F32))
                bc3 = [None]

                @blk.gpsimd
                def _(g):
                    r = g.alloc_register()
                    g.reg_mov(r, E * CAP - 1)
                    bc3[0] = r

                tmarks = {}
                for cc in range(NCH):
                    par = cc % 2

                    @blk.sync
                    def _(sy, cc=cc, par=par):
                        if cc >= 2:
                            S.wait(sy, "pe", tmarks[cc - 2])
                        S.pre(sy, f"lp{par}")
                        S.dma(sy.dma_start(out=ptt[:, par, :],
                                           in_=partialT[cc * 128:(cc + 1) * 128, :]), f"lp{par}")

                    @blk.tensor
                    def _(t, cc=cc, par=par):
                        t = _TTrack(t)
                        t.wait_ge(S.h[f"lp{par}"], S.n[f"lp{par}"])
                        if cc == 0:
                            S.wait(t, "dv")
                        if cc >= 1:
                            S.wait(t, "dv")
                        for n in range(NT):
                            t.transpose(psT3[:, n * 128:(n + 1) * 128],
                                        ptt[:, par, n * 128:(n + 1) * 128], iden[:, :])
                        S.inc(t.last_inst(), "pe")
                        tmarks[cc] = S.n["pe"]

                    @blk.vector
                    def _(v, cc=cc):
                        v = _Chain(v, S, "dvx")
                        S.wait(v, "pe")
                        for n in range(NT):
                            v.tensor_copy(x1[:, n, cc * 128:(cc + 1) * 128],
                                          psT3[:, n * 128:(n + 1) * 128])
                        S.inc(v.drain(), "dv")

                gmarks = []
                for n in range(NT):
                    par = n % 2

                    @blk.gpsimd
                    def _(g, n=n, par=par):
                        g = _Chain(g, S, "gcx")
                        if n == 0:
                            g.wait_ge(S.h["dv"], dv_slots3)
                            S.wait(g, "dv")
                        if n >= 2:
                            S.wait(g, "dv", gmarks[n - 2])
                        g.memset(yg[:, par, 0, :], 0.0)
                        g.memset(yg[:, par, 1, :], 0.0)
                        gd = g.drain()
                        S.inc(gd, "gcx")
                        g.wait_ge(S.h["gcx"], S.n["gcx"])
                        for k in range(2):
                            S.dma(g.indirect_dma_start(
                                out=yg[:, par, k, :], out_offset=None,
                                in_=yb_all,
                                in_offset=IndirectOffsetOnAxis(slots[:, n, k:k + 1], axis=0),
                                bounds_check=bc3[0], oob_is_err=False), f"g{par}")

                    @blk.vector
                    def _(v, n=n, par=par):
                        v = _Chain(v, S, "dvx")
                        v.wait_ge(S.h[f"g{par}"], S.n[f"g{par}"])
                        v.tensor_scalar(lex[:, 0:C], yg[:, par, 0, :], meta_sb[:, n, 2:3], None, OP.mult)
                        v.scalar_tensor_tensor(lex[:, C:2 * C] if False else lex[:, 0:C],
                                               yg[:, par, 1, :], meta_sb[:, n, 3:4],
                                               lex[:, 0:C], OP.mult, OP.add)
                        v.tensor_tensor(x1[:, n, :], x1[:, n, :], lex[:, 0:C], OP.add)
                        S.inc(v.drain(), "dv")
                        gmarks.append(S.n["dv"])

            with ExitStack() as ph:
                _e = ph.enter_context
                sq3 = _e(nc.sbuf_tensor("sq3", [128, C], F32))
                ssq3 = _e(nc.sbuf_tensor("ssq3", [128, NT], F32))
                xnt3 = _e(nc.sbuf_tensor("xnt3", [128, 2, C], BF16))

                for n in range(NT):
                    par = n % 2

                    @blk.scalar
                    def _(sc, n=n):
                        sc = _Chain(sc, S, "acx")
                        if n == 0:
                            S.wait(sc, "dv")
                        sc.activation(out=sq3[:, :], in_=x1[:, n, :], func=AF.Square,
                                      accum_out=ssq3[:, n:n + 1])
                        S.inc(sc.drain(), "ac")

                    @blk.vector
                    def _(v, n=n, par=par):
                        v = _Chain(v, S, "dvx")
                        S.wait(v, "ac")
                        if n >= 2:
                            v.wait_ge(S.h[f"tx{par}"], S.n[f"tx{par}"])
                        c0 = slice(n, n + 1)
                        v.reduce_sum(m2c[:, c0], x1[:, n, :], AX)
                        v.tensor_scalar(m2c[:, c0], m2c[:, c0], 1.0 / C, None, OP.mult)
                        v.tensor_scalar(ssq3[:, c0], ssq3[:, c0], 1.0 / C, None, OP.mult)
                        v.tensor_tensor(r2c[:, c0], m2c[:, c0], m2c[:, c0], OP.mult)
                        v.tensor_tensor(ssq3[:, c0], ssq3[:, c0], r2c[:, c0], OP.subtract)
                        v.tensor_scalar(r2c[:, c0], ssq3[:, c0], EPS_LN, None, OP.add)
                        S.inc(v.drain(), "dv")

                    @blk.scalar
                    def _(sc, n=n):
                        sc = _Chain(sc, S, "acx")
                        S.wait(sc, "dv")
                        c0 = slice(n, n + 1)
                        sc.activation(out=r2c[:, c0], in_=r2c[:, c0], func=AF.Ln)
                        sc.activation(out=r2c[:, c0], in_=r2c[:, c0], func=AF.Exp, scale=-0.5)
                        S.inc(sc.drain(), "ac")

                    @blk.vector
                    def _(v, n=n, par=par):
                        v = _Chain(v, S, "dvx")
                        S.wait(v, "ac")
                        c0 = slice(n, n + 1)
                        v.tensor_scalar(xnt3[:, par, :], x1[:, n, :], m2c[:, c0], r2c[:, c0],
                                        OP.subtract, OP.mult)
                        S.inc(v.drain(), "dv")

                    @blk.sync
                    def _(sy, n=n, par=par):
                        S.wait(sy, "dv")
                        for cc in range(NCH):
                            S.pre(sy, f"tx{par}")
                            S.dma(sy.dma_start(out=xn2T[:, cc, n * 128:(n + 1) * 128],
                                               in_=xnt3[:, par, cc * 128:(cc + 1) * 128],
                                               transpose=True), f"tx{par}")

            def fold_wf(wsrc, wdst, wfd, psF):
                @blk.sync
                def _(sy):
                    sy.wait_ge(S.h["pe"], S.n["pe"])
                    sy.wait_ge(S.h["dv"], S.n["dv"])
                    S.pre(sy, "ld")
                    S.dma(sy.dma_start(out=wtf[:, :, :],
                                       in_=wsrc.rearrange("(c p) q -> p c q", p=128)), "ld")
                ldw = S.n["ld"]

                for ct in range(NCH):
                    @blk.tensor
                    def _(t, ct=ct, ldw=ldw):
                        t = _TTrack(t)
                        t.wait_ge(S.h["ld"], ldw)
                        S.wait(t, "dv")
                        for h in [x for x in range(NH) if x % 2 == 0] + \
                                 [x for x in range(NH) if x % 2 == 1]:
                            t.matmul(psF[:, h * 64:(h + 1) * 64],
                                     wtf[(h % 2) * 64:(h % 2) * 64 + 64, h // 2, ct * 128:(ct + 1) * 128],
                                     wfd[(h % 2) * 64:(h % 2) * 64 + 64, 0:64], start=True, stop=True)
                        S.inc(t.last_inst(), "pe")

                    @blk.vector
                    def _(v, ct=ct):
                        v = _Chain(v, S, "dvx")
                        S.wait(v, "pe")
                        v.tensor_copy(wdst[:, ct, :], psF[:, 0:C])
                        S.inc(v.drain(), "dv")

            wfd = st.enter_context(nc.sbuf_tensor("wfd", [128, 64], F32))
            wkf_sb = None

            @blk.sync
            def _(sy):
                S.pre(sy, "ld")
                S.dma(sy.dma_start(out=wfd[:, :], in_=wf2), "ld")

            with ExitStack() as phK:
                _eK = phK.enter_context
                kf = _eK(nc.sbuf_tensor("kf", [128, NT, C], BF16))
                psK = _eK(nc.psum_tensor("ps3k", [128, 1024], F32))

                with ExitStack() as phW:
                    wtf = phW.enter_context(nc.sbuf_tensor("wtf", [128, NCH, C], F32))
                    wkf_sb = phW.enter_context(nc.sbuf_tensor("wkf_sb", [128, NCH, C], BF16))
                    fold_wf(wkT, wkf_sb, wfd, psK)

                    for n in range(NT):
                        @blk.tensor
                        def _(t, n=n):
                            t = _TTrack(t)
                            S.wait(t, "dv")
                            for cc in range(NCH):
                                for (a, b) in n512(0, C):
                                    t.matmul(psK[:, a:b], xn2T[:, cc, n * 128:(n + 1) * 128],
                                             wkf_sb[:, cc, a:b], start=(cc == 0), stop=(cc == NCH - 1))
                            S.inc(t.last_inst(), "pe")

                        @blk.vector
                        def _(v, n=n):
                            v = _Chain(v, S, "dvx")
                            S.wait(v, "pe")
                            if n == 0:
                                v.wait_ge(S.h["tx0"], S.n["tx0"])
                                v.wait_ge(S.h["tx1"], S.n["tx1"])
                            v.tensor_tensor(lex[:, 0:C], psK[:, 0:C], bf_b[:, :], OP.add)
                            v.tensor_scalar(kf[:, n, :], lex[:, 0:C], 0.0, None, OP.max)
                            S.inc(v.drain(), "dv")

                with ExitStack() as phV:
                    _eV = phV.enter_context
                    v65 = _eV(nc.sbuf_tensor("v65", [128, NT, NH * 65], BF16))

                    with ExitStack() as phWv:
                        wv_sb = phWv.enter_context(nc.sbuf_tensor("wv_sb", [128, NCH, C], BF16))

                        @blk.gpsimd
                        def _(g):
                            g = _Chain(g, S, "gcx")
                            S.wait(g, "pe")
                            S.wait(g, "dv")
                            for h in range(NH):
                                g.memset(v65[:, :, h * 65 + 64:h * 65 + 65], 1.0)
                            S.pre(g, "gp")
                            S.dma(g.dma_start(out=wv_sb[:, :, :],
                                              in_=wv.rearrange("(c p) q -> p c q", p=128)), "gp")
                        gp_wv = S.n["gp"]

                        for n in range(NT):
                            @blk.tensor
                            def _(t, n=n):
                                t = _TTrack(t)
                                S.wait(t, "dv")
                                if n == 0:
                                    t.wait_ge(S.h["gp"], gp_wv)
                                    S.wait(t, "gcx")
                                for cc in range(NCH):
                                    for (a, b) in n512(0, C):
                                        t.matmul(psK[:, a:b], xn2T[:, cc, n * 128:(n + 1) * 128],
                                                 wv_sb[:, cc, a:b], start=(cc == 0), stop=(cc == NCH - 1))
                                S.inc(t.last_inst(), "pe")

                            @blk.vector
                            def _(v, n=n):
                                v = _Chain(v, S, "dvx")
                                S.wait(v, "pe")
                                v.tensor_copy(
                                    v65[:, n, :].rearrange("p (h x) -> p h x", h=NH)[:, :, 0:64],
                                    psK[:, 0:C].rearrange("p (h x) -> p h x", h=NH))
                                S.inc(v.drain(), "dv")

                    with nc.psum_tensor("ps3kv", [128, 2048], F32) as psKV:
                        @blk.tensor
                        def _(t):
                            t = _TTrack(t)
                            S.wait(t, "dv")
                            for h in range(NH):
                                for n in range(NT):
                                    t.matmul(psKV[0:64, h * 128:h * 128 + 65],
                                             kf[:, n, h * 64:(h + 1) * 64],
                                             v65[:, n, h * 65:h * 65 + 65],
                                             start=(n == 0), stop=(n == NT - 1))
                            for h in range(NH):
                                for n in range(NT):
                                    t.matmul(psKV[64:128, h * 128:h * 128 + 65],
                                             kf[:, n, h * 64:(h + 1) * 64],
                                             v65[:, n, h * 65:h * 65 + 65],
                                             start=(n == 0), stop=(n == NT - 1))
                            S.inc(t.last_inst(), "pe")

                        @blk.vector
                        def _(v):
                            v = _Chain(v, S, "dvx")
                            S.wait(v, "pe")
                            v.tensor_copy(kv_sb[:, :, 0:65],
                                          psKV[:, 0:2048].rearrange("p (h x) -> p h x", h=NH)[:, :, 0:65])
                            S.inc(v.drain(), "dv")

            with ExitStack() as ph2:
                _e2 = ph2.enter_context
                psQ3 = _e2(nc.psum_tensor("ps3q", [128, 1024], F32))
                psL = _e2(nc.psum_tensor("ps3l", [65, 2048], F32))

                qfT = _e2(nc.sbuf_tensor("qfT", [128, NCH, T], BF16))
                with ExitStack() as phW:
                    wqf_sb = phW.enter_context(nc.sbuf_tensor("wqf_sb", [128, NCH, C], BF16))
                    with ExitStack() as phW2:
                        wtf = phW2.enter_context(nc.sbuf_tensor("wtf2", [128, NCH, C], F32))
                        fold_wf(wqT, wqf_sb, wfd, psQ3)

                    for fc in range(NCH):
                        for half in range(2):
                            @blk.tensor
                            def _(t, fc=fc, half=half):
                                t = _TTrack(t)
                                S.wait(t, "dv")
                                S.wait(t, "ac")
                                for cc in range(NCH):
                                    for (a, b) in n512(0, 1024):
                                        t.matmul(psQ3[:, a:b], wqf_sb[:, cc, fc * 128:(fc + 1) * 128],
                                                 xn2T[:, cc, half * 1024 + a:half * 1024 + b],
                                                 start=(cc == 0), stop=(cc == NCH - 1))
                                S.inc(t.last_inst(), "pe")

                            @blk.scalar
                            def _(sc, fc=fc, half=half):
                                sc = _Chain(sc, S, "acx")
                                S.wait(sc, "pe")
                                sc.activation(out=qfT[:, fc, half * 1024:(half + 1) * 1024],
                                              in_=psQ3[:, :], func=AF.Relu,
                                              bias=bfq_sb[:, fc:fc + 1])
                                S.inc(sc.drain(), "ac")

                attn1nT = _e2(nc.sbuf_tensor("attn1nT", [128, NCH, T], BF16))
                aun = lex[0:64, :]
                for h in range(NH):
                    r0 = (h % 2) * 64

                    @blk.tensor
                    def _(t, h=h, r0=r0):
                        t = _TTrack(t)
                        S.wait(t, "ac")
                        S.wait(t, "dv")
                        for (a, b) in n512(0, T):
                            t.matmul(psL[:, a:b], kv_sb[r0:r0 + 64, h, 0:65],
                                     qfT[r0:r0 + 64, h // 2, a:b], start=True, stop=True)
                        S.inc(t.last_inst(), "pe")

                    @blk.vector
                    def _(v, h=h):
                        v = _Chain(v, S, "dvx")
                        S.wait(v, "pe")
                        v.tensor_scalar(lex[64:65, :], psL[64:65, :], 1e-6, None, OP.max)
                        v.tensor_copy(aun[:, :], psL[0:64, :])
                        S.inc(v.drain(), "dv")

                    @blk.scalar
                    def _(sc, h=h):
                        sc = _Chain(sc, S, "acx")
                        S.wait(sc, "dv")
                        sc.activation(out=lex[64:65, :], in_=lex[64:65, :], func=AF.Ln)
                        sc.activation(out=lex[64:65, :], in_=lex[64:65, :], func=AF.Exp, scale=-1.0)
                        S.inc(sc.drain(), "ac")

                    for half in range(2):
                        @blk.tensor
                        def _(t, h=h, half=half):
                            t = _TTrack(t)
                            S.wait(t, "ac")
                            S.wait(t, "dv")
                            for (a, b) in n512(half * 1024, (half + 1) * 1024):
                                t.matmul(psQ3[:, a - half * 1024:b - half * 1024],
                                         ones_r64[64:65, :], lex[64:65, a:b], start=True, stop=True)
                            S.inc(t.last_inst(), "pe")

                        @blk.vector
                        def _(v, h=h, half=half):
                            v = _Chain(v, S, "dvx")
                            S.wait(v, "pe")
                            v.tensor_tensor(
                                attn1nT[(h % 2) * 64:(h % 2) * 64 + 64, h // 2,
                                        half * 1024:(half + 1) * 1024],
                                aun[:, half * 1024:(half + 1) * 1024],
                                psQ3[0:64, 0:1024], OP.mult)
                            S.inc(v.drain(), "dv")

                with nc.sbuf_tensor("wo_sb", [128, NCH, C], BF16) as wo_sb:
                    @blk.gpsimd
                    def _(g):
                        S.wait(g, "pe")
                        S.wait(g, "dv")
                        S.pre(g, "gp")
                        S.dma(g.dma_start(out=wo_sb[:, :, :],
                                          in_=wo.rearrange("(c p) q -> p c q", p=128)), "gp")
                    gp_wo = S.n["gp"]

                    for n in range(NT):
                        @blk.tensor
                        def _(t, n=n):
                            t = _TTrack(t)
                            t.wait_ge(S.h["gp"], gp_wo)
                            S.wait(t, "dv")
                            for cc in range(NCH):
                                for (a, b) in n512(0, C):
                                    t.matmul(psQ3[:, a:b], attn1nT[:, cc, n * 128:(n + 1) * 128],
                                             wo_sb[:, cc, a:b], start=(cc == 0), stop=(cc == NCH - 1))
                            S.inc(t.last_inst(), "pe")

                        @blk.vector
                        def _(v, n=n):
                            v = _Chain(v, S, "dvx")
                            S.wait(v, "pe")
                            v.tensor_tensor(lex[:, 0:C], psQ3[:, 0:C], bo_b[:, :], OP.add)
                            v.tensor_tensor(x1[:, n, :], x1[:, n, :], lex[:, 0:C], OP.add)
                            S.inc(v.drain(), "dv")

            with ExitStack() as ph:
                _e = ph.enter_context
                g1t = _e(nc.sbuf_tensor("g1t", [4, 1024], F32))
                g2t = _e(nc.sbuf_tensor("g2t", [4, 256], F32))
                wsb = _e(nc.sbuf_tensor("wsb", [128, 256], F32))
                wrT = _e(nc.sbuf_tensor("wrT", [128, 4, IDIM], BF16))
                wiT = _e(nc.sbuf_tensor("wiT", [128, 4, IDIM], BF16))
                winT = _e(nc.sbuf_tensor("winT", [128, 4, IDIM], BF16))
                psW = _e(nc.psum_tensor("ps3w", [128, 512], F32))
                psR = _e(nc.psum_tensor("ps3r", [128, 512], F32))
                psI = _e(nc.psum_tensor("ps3i", [128, 512], F32))

                for (g1s, g2s, a2d, wdst) in ((g1r_R, g2r_R, a2R, wrT), (g1r_I, g2r_I, a2I, wiT)):
                    @blk.sync
                    def _(sy, g1s=g1s, g2s=g2s):
                        sy.wait_ge(S.h["pe"], S.n["pe"])
                        sy.wait_ge(S.h["dv"], S.n["dv"])
                        S.pre(sy, "ld")
                        S.dma(sy.dma_start(out=g1t[:, :], in_=g1s), "ld")
                        S.pre(sy, "ld")
                        S.dma(sy.dma_start(out=g2t[:, :], in_=g2s), "ld")
                    ldg = S.n["ld"]

                    for bt in range(8):
                        @blk.tensor
                        def _(t, bt=bt, ldg=ldg):
                            t = _TTrack(t)
                            t.wait_ge(S.h["ld"], ldg)
                            S.wait(t, "st")
                            t.matmul(psW[:, 0:256], g1t[:, bt * 128:(bt + 1) * 128], g2t[:, :],
                                     start=True, stop=True)
                            S.inc(t.last_inst(), "pe")

                        @blk.vector
                        def _(v, bt=bt):
                            v = _Chain(v, S, "dvx")
                            S.wait(v, "pe")
                            v.tensor_copy(wsb[:, :], psW[:, 0:256])
                            S.inc(v.drain(), "dv")

                        @blk.sync
                        def _(sy, bt=bt, a2d=a2d):
                            S.wait(sy, "dv")
                            for bb in range(4):
                                bglob = bt * 4 + bb
                                S.pre(sy, "st")
                                S.dma(sy.dma_start(
                                    out=a2d.rearrange("(d g) (b e) -> d g b e", g=16, b=32)[:, :, bglob, :],
                                    in_=wsb[bb * 32:(bb + 1) * 32, :]), "st")

                @blk.gpsimd
                def _(g):
                    S.wait(g, "st")
                    S.pre(g, "gp")
                    S.dma(g.dma_start(out=wrT[:, :, :], in_=a2R.rearrange("(c p) q -> p c q", p=128)), "gp")
                    S.pre(g, "gp")
                    S.dma(g.dma_start(out=wiT[:, :, :], in_=a2I.rearrange("(c p) q -> p c q", p=128)), "gp")
                gp_w = S.n["gp"]

                @blk.vector
                def _(v):
                    v = _Chain(v, S, "dvx")
                    v.wait_ge(S.h["gp"], gp_w)
                    for ic in range(4):
                        v.tensor_scalar(winT[:, ic, :], wiT[:, ic, :], -1.0, None, OP.mult)
                    S.inc(v.drain(), "dv")
                dv_w = S.n["dv"]

                somarks = {}
                for n in range(NT):
                    par = n % 2

                    @blk.tensor
                    def _(t, n=n):
                        t = _TTrack(t)
                        t.wait_ge(S.h["dv"], dv_w)
                        S.wait(t, "dv")
                        for ic in range(4):
                            t.matmul(psR[:, 0:512], xn2T[:, ic, n * 128:(n + 1) * 128],
                                     wrT[:, ic, :], start=(ic == 0), stop=False)
                        for ic in range(4):
                            t.matmul(psR[:, 0:512], xn2T[:, 4 + ic, n * 128:(n + 1) * 128],
                                     winT[:, ic, :], start=False, stop=(ic == 3))
                        for ic in range(4):
                            t.matmul(psI[:, 0:512], xn2T[:, ic, n * 128:(n + 1) * 128],
                                     wiT[:, ic, :], start=(ic == 0), stop=False)
                        for ic in range(4):
                            t.matmul(psI[:, 0:512], xn2T[:, 4 + ic, n * 128:(n + 1) * 128],
                                     wrT[:, ic, :], start=False, stop=(ic == 3))
                        S.inc(t.last_inst(), "pe")

                    @blk.vector
                    def _(v, n=n, par=par):
                        v = _Chain(v, S, "dvx")
                        S.wait(v, "pe")
                        if n >= 2:
                            v.wait_ge(S.h[f"so{par}"], somarks[n - 2])
                        v.tensor_tensor(lex[:, 0:512], psR[:, 0:512], bri_b[:, 0:512], OP.add)
                        v.tensor_tensor(x1[:, n, 0:512], x1[:, n, 0:512], lex[:, 0:512], OP.add)
                        v.tensor_tensor(lex[:, 512:1024], psI[:, 0:512], bri_b[:, 512:1024], OP.add)
                        v.tensor_tensor(x1[:, n, 512:1024], x1[:, n, 512:1024], lex[:, 512:1024], OP.add)
                        S.inc(v.drain(), "dv")

                    @blk.sync
                    def _(sy, n=n, par=par):
                        S.wait(sy, "dv")
                        S.dma(sy.dma_start(out=out.rearrange("(n p) c -> p n c", p=128)[:, n, :],
                                           in_=x1[:, n, :]), f"so{par}")
                        somarks[n] = S.n[f"so{par}"]

            _fin3()



def _stage3_io(nc):
    io = {}
    io["partialT"] = nc.dram_tensor("partialT", [C, T], F32, kind="ExternalInput").ap()
    io["yb_all"] = nc.dram_tensor("yb_all", [E * CAP, C], BF16, kind="ExternalInput").ap()
    io["meta_in"] = nc.dram_tensor("meta_in", [T, 6], F32, kind="ExternalInput").ap()
    io["counts_all"] = nc.dram_tensor("counts_all", [8, E], F32, kind="ExternalInput").ap()
    io["bsel"] = nc.dram_tensor("bsel", [8, 1], F32, kind="ExternalInput").ap()
    io["wqT"] = nc.dram_tensor("wqT", [C, C], F32, kind="ExternalInput").ap()
    io["wkT"] = nc.dram_tensor("wkT", [C, C], F32, kind="ExternalInput").ap()
    io["wv"] = nc.dram_tensor("wv", [C, C], F32, kind="ExternalInput").ap()
    io["wo"] = nc.dram_tensor("wo", [C, C], F32, kind="ExternalInput").ap()
    io["bo"] = nc.dram_tensor("bo", [1, C], F32, kind="ExternalInput").ap()
    io["wf2"] = nc.dram_tensor("wf2", [128, 64], F32, kind="ExternalInput").ap()
    io["bfq_col"] = nc.dram_tensor("bfq_col", [128, 8], F32, kind="ExternalInput").ap()
    io["bf_row"] = nc.dram_tensor("bf_row", [1, C], F32, kind="ExternalInput").ap()
    io["g1r_R"] = nc.dram_tensor("g1r_R", [4, 1024], F32, kind="ExternalInput").ap()
    io["g2r_R"] = nc.dram_tensor("g2r_R", [4, 256], F32, kind="ExternalInput").ap()
    io["g1r_I"] = nc.dram_tensor("g1r_I", [4, 1024], F32, kind="ExternalInput").ap()
    io["g2r_I"] = nc.dram_tensor("g2r_I", [4, 256], F32, kind="ExternalInput").ap()
    io["bri_row"] = nc.dram_tensor("bri_row", [1, C], F32, kind="ExternalInput").ap()
    io["out"] = nc.dram_tensor("out", [T, C], F32, kind="ExternalOutput").ap()
    io["a2R"] = nc.dram_tensor("a2R", [IDIM, IDIM], F32, kind="Internal").ap()
    io["a2I"] = nc.dram_tensor("a2I", [IDIM, IDIM], F32, kind="Internal").ap()
    return io


def build_stage3():
    nc = bass.Bass("TRN2", target_bir_lowering=False, debug=False)
    io = _stage3_io(nc)
    st0 = ExitStack()
    with st0:
        S = Sems(st0, nc, SEM_NAMES)
        with nc.Block() as blk:
            emit_stage3(_PrefNC(nc, "s3_"), blk, S, io)
    return nc


def _prep_stage3_inputs(inputs, s1res, s2res):
    f32 = np.float32
    yb_all = np.concatenate([r["yb"] for r in s2res], axis=0)
    counts_all = np.concatenate([r["counts"] for r in s1res], axis=0).astype(f32)
    Wf = np.asarray(inputs["L1_attn_Wf"], f32)
    bf = np.asarray(inputs["L1_attn_bf"], f32)
    G1R = np.asarray(inputs["L1_ffn_G1_R"], f32)[0]
    G2R = np.asarray(inputs["L1_ffn_G2_R"], f32)
    G1I = np.asarray(inputs["L1_ffn_G1_I"], f32)[0]
    G2I = np.asarray(inputs["L1_ffn_G2_I"], f32)
    com = {
        "yb_all": yb_all,
        "counts_all": counts_all,
        "wqT": np.ascontiguousarray(np.asarray(inputs["L1_attn_Wq"], f32).T),
        "wkT": np.ascontiguousarray(np.asarray(inputs["L1_attn_Wk"], f32).T),
        "wv": np.ascontiguousarray(inputs["L1_attn_Wv"], f32),
        "wo": np.ascontiguousarray(inputs["L1_attn_Wo"], f32),
        "bo": np.ascontiguousarray(np.asarray(inputs["L1_attn_bo"], f32).reshape(1, -1)),
        "wf2": np.ascontiguousarray(np.concatenate([Wf, Wf], axis=0)),
        "bfq_col": np.ascontiguousarray(np.tile(bf, 16).reshape(8, 128).T),
        "bf_row": np.ascontiguousarray(np.tile(bf, 16).reshape(1, -1)),
        "g1r_R": np.ascontiguousarray(G1R.transpose(1, 0, 2).reshape(4, 1024)),
        "g2r_R": np.ascontiguousarray(G2R[:, :, 0, :].transpose(0, 2, 1).reshape(4, 256)),
        "g1r_I": np.ascontiguousarray(G1I.transpose(1, 0, 2).reshape(4, 1024)),
        "g2r_I": np.ascontiguousarray(G2I[:, :, 0, :].transpose(0, 2, 1).reshape(4, 256)),
        "bri_row": np.ascontiguousarray(np.concatenate(
            [np.asarray(inputs["L1_ffn_bias_real"], f32),
             np.asarray(inputs["L1_ffn_bias_imag"], f32)]).reshape(1, -1)),
    }
    maps = []
    for b in range(8):
        m = dict(com)
        m["partialT"] = np.ascontiguousarray(s1res[b]["partial"])
        m["meta_in"] = np.ascontiguousarray(s1res[b]["meta"].astype(f32))
        sel = np.zeros((8, 1), f32)
        sel[b, 0] = 1.0
        m["bsel"] = sel
        maps.append(m)
    return maps


def run_stage3(inputs, s1res, s2res, **kw):
    nc = build_stage3()
    maps = _prep_stage3_inputs(inputs, s1res, s2res)
    return run_bass_kernel_spmd(nc, maps, core_ids=list(range(8)), **kw)



def build_fused():
    nc = bass.Bass("TRN2", target_bir_lowering=False, debug=False)
    E_IN = lambda n, s, d: nc.dram_tensor(n, s, d, kind="ExternalInput").ap()
    INT = lambda n, s, d: nc.dram_tensor(n, s, d, kind="Internal").ap()

    io1 = {
        "xT": E_IN("xT", [C, T], F32), "xtok": E_IN("xtok", [T, C], F32),
        "w_attn": E_IN("w_attn", [C, 3 * C], F32), "w_attnT": E_IN("w_attnT", [3 * C, C], F32),
        "b_attn": E_IN("b_attn", [1, 3 * C], F32), "w_proj": E_IN("w_proj", [C, C], F32),
        "b_proj": E_IN("b_proj", [1, C], F32), "bproj_col": E_IN("bproj_col", [128, NCH], F32),
        "wk_cT": E_IN("wk_cT", [32, 64], F32), "wk_e": E_IN("wk_e", [32, 64], F32),
        "wv_cT": E_IN("wv_cT", [32, 64], F32), "wv_e": E_IN("wv_e", [32, 64], F32),
        "rms_w": E_IN("rms_w", [1, C], F32), "rms_w_col": E_IN("rms_w_col", [128, NCH], F32),
        "battn_col": E_IN("battn_col", [128, 24], F32), "gate_wT": E_IN("gate_wT", [C, E], F32),
        "sw_gate": E_IN("sw_gate", [C, IDIM], F32), "sw_up": E_IN("sw_up", [C, IDIM], F32),
        "sw_down": E_IN("sw_down", [IDIM, C], F32),
    }
    partial_i = INT("fz_partial", [C, T], F32)
    xm_loc = INT("fz_xm_loc", [T, C], BF16)
    meta_loc = INT("fz_meta_loc", [T, 6], F32)
    counts_loc = INT("fz_counts_loc", [1, E], F32)
    io1["partial"] = partial_i
    io1["xm_out"] = xm_loc
    io1["meta"] = meta_loc
    io1["counts"] = counts_loc

    xm_all = INT("fz_xm_all", [B * T, C], BF16)
    meta_all = INT("fz_meta_all", [B * T, 6], F32)
    counts_all8 = INT("fz_counts_all", [8, E], F32)
    yb_loc = INT("fz_yb_loc", [8 * CAP, C], BF16)
    yb_all = INT("fz_yb_all", [E * CAP, C], BF16)

    io2 = {
        "xm_all": xm_all,
        "meta_all": meta_all,
        "counts_all": counts_all8,
        "w13": E_IN("w13", [8, C, 2 * IDIM], F32),
        "w2w": E_IN("w2w", [8, IDIM, C], F32),
        "ebase": E_IN("ebase", [128, 1], F32),
        "yb": yb_loc,
        "disp": INT("fz_disp", [8 * CAP, C], BF16),
    }
    io3 = {
        "partialT": partial_i,
        "yb_all": yb_all,
        "meta_in": meta_loc,
        "counts_all": counts_all8,
        "bsel": E_IN("bsel", [8, 1], F32),
        "wqT": E_IN("wqT", [C, C], F32), "wkT": E_IN("wkT", [C, C], F32),
        "wv": E_IN("wv", [C, C], F32), "wo": E_IN("wo", [C, C], F32),
        "bo": E_IN("bo", [1, C], F32), "wf2": E_IN("wf2", [128, 64], F32),
        "bfq_col": E_IN("bfq_col", [128, 8], F32), "bf_row": E_IN("bf_row", [1, C], F32),
        "g1r_R": E_IN("g1r_R", [4, 1024], F32), "g2r_R": E_IN("g2r_R", [4, 256], F32),
        "g1r_I": E_IN("g1r_I", [4, 1024], F32), "g2r_I": E_IN("g2r_I", [4, 256], F32),
        "bri_row": E_IN("bri_row", [1, C], F32),
        "out": nc.dram_tensor("out", [T, C], F32, kind="ExternalOutput").ap(),
        "a2R": INT("fz_a2R", [IDIM, IDIM], F32),
        "a2I": INT("fz_a2I", [IDIM, IDIM], F32),
    }

    st0 = ExitStack()
    with st0:
        S = Sems(st0, nc, SEM_NAMES)
        with nc.Block() as blk:
            def _gate(val):
                waits = [("cc", val)]
                for nm in ("pe", "dv", "ac", "gc", "dvx", "acx", "gcx",
                           "ga0", "ga1", "sp0", "sp1", "sx0", "sx1", "st", "gp",
                           "yb0", "yb1", "so0", "so1"):
                    waits.append((nm, S.n[nm]))
                for sec in (blk.sync, blk.gpsimd, blk.tensor, blk.vector, blk.scalar):
                    @sec
                    def _(e):
                        for (nm, v_) in waits:
                            e.wait_ge(S.h[nm], v_)

            def _cc1():
                @blk.gpsimd
                def _(g):
                    for nm in ("sx0", "sx1", "st"):
                        S.wait(g, nm)
                    S.inc(g.collective_compute(
                        "AllGather", OP.bypass, replica_groups=[list(range(8))],
                        ins=[xm_loc], outs=[xm_all]), "cc", 1)
                    S.inc(g.collective_compute(
                        "AllGather", OP.bypass, replica_groups=[list(range(8))],
                        ins=[meta_loc], outs=[meta_all]), "cc", 1)
                    S.inc(g.collective_compute(
                        "AllGather", OP.bypass, replica_groups=[list(range(8))],
                        ins=[counts_loc], outs=[counts_all8]), "cc", 1)

            emit_stage1(_PrefNC(nc, "s1_"), blk, S, io1, hook_after_A=_cc1)

            _gate(3)
            emit_stage2(_PrefNC(nc, "s2_"), blk, S, io2)

            @blk.gpsimd
            def _(g):
                for nm in ("yb0", "yb1"):
                    S.wait(g, nm)
                S.inc(g.collective_compute(
                    "AllGather", OP.bypass, replica_groups=[list(range(8))],
                    ins=[yb_loc], outs=[yb_all]), "cc", 1)

            _gate(4)
            emit_stage3(_PrefNC(nc, "s3_"), blk, S, io3)
    return nc


def _prep_fused_inputs(inputs):
    f32 = np.float32
    m1 = _prep_stage1_inputs(inputs)
    w13 = np.ascontiguousarray(inputs["L0_ffn_w13"], f32)
    w2w = np.ascontiguousarray(inputs["L0_ffn_w2"], f32)
    dummy1 = [{"partial": np.zeros((C, T), f32), "meta": np.zeros((T, 6), f32),
               "counts": np.zeros((1, E), f32), "xm_out": None} for _ in range(8)]
    dummy2 = [{"yb": np.zeros((1, 1), BF)} for _ in range(8)]
    m3 = _prep_stage3_inputs(inputs, dummy1, dummy2)
    drop3 = {"yb_all", "counts_all", "partialT", "meta_in"}
    maps = []
    for i in range(8):
        m = dict(m1[i])
        m["w13"] = np.ascontiguousarray(w13[8 * i:8 * i + 8])
        m["w2w"] = np.ascontiguousarray(w2w[8 * i:8 * i + 8])
        m["ebase"] = np.full((128, 1), 8.0 * i, f32)
        for k, v in m3[i].items():
            if k not in drop3:
                m[k] = v
        maps.append(m)
    return maps


def run_fused(inputs, **kw):
    nc = build_fused()
    maps = _prep_fused_inputs(inputs)
    return run_bass_kernel_spmd(nc, maps, core_ids=list(range(8)), **kw)


def kernel(**inputs):
    inputs = {k: np.asarray(v) for k, v in inputs.items()}
    res = run_fused(inputs).results
    return np.stack([r["out"] for r in res], axis=0).astype(np.float32)



# version 04
# speedup vs baseline: 1.0709x; 1.0135x over previous
import numpy as np
import ml_dtypes
from contextlib import ExitStack, contextmanager

import concourse.bass as bass
import concourse.mybir as mybir
from concourse.bass import IndirectOffsetOnAxis
from concourse.bass_utils import run_bass_kernel_spmd
from concourse.alu_op_type import AluOpType as OP

F32 = mybir.dt.float32
BF16 = mybir.dt.bfloat16
I32 = mybir.dt.int32
U32 = mybir.dt.uint32
AF = mybir.ActivationFunctionType
AX = mybir.AxisListType.X

B = 8
T = 2048
C = 1024
NH = 16
HD = 64
E = 64
CAP = 1024
IDIM = 512
NCH = C // 128
NT = T // 128
EPS_LN = 1e-5
EPS_RMS = 1e-6
BF = ml_dtypes.bfloat16



class _PrefNC:

    def __init__(self, nc, pref):
        object.__setattr__(self, "_nc", nc)
        object.__setattr__(self, "_pref", pref)

    def sbuf_tensor(self, name, *a, **k):
        return self._nc.sbuf_tensor(self._pref + name, *a, **k)

    def psum_tensor(self, name, *a, **k):
        return self._nc.psum_tensor(self._pref + name, *a, **k)

    def __getattr__(self, n):
        return getattr(self._nc, n)


@contextmanager
def _passthrough(x):
    yield x


class Sems:
    def __init__(self, stack, nc, names):
        self.h = {}
        self.n = {}
        for nm in names:
            self.h[nm] = stack.enter_context(nc.semaphore(nm))
            self.n[nm] = 0

    def inc(self, instr, nm, k=1):
        instr.then_inc(self.h[nm], k)
        self.n[nm] += k
        return instr

    def dma(self, instr, nm="ld"):
        return self.inc(instr, nm, 16)

    def pre(self, eng, nm):
        eng.wait_ge(self.h[nm], self.n[nm])

    def wait(self, eng, nm, val=None):
        eng.wait_ge(self.h[nm], self.n[nm] if val is None else val)


_CHAINED_OPS = {
    "tensor_tensor", "tensor_scalar", "tensor_copy", "tensor_add", "tensor_sub",
    "tensor_mul", "tensor_max", "tensor_relu", "tensor_scalar_add", "tensor_scalar_mul",
    "tensor_scalar_max", "tensor_scalar_min", "tensor_scalar_sub", "reduce_sum",
    "reduce_max", "tensor_reduce", "scalar_tensor_tensor", "max", "max_index",
    "select", "copy_predicated", "match_replace", "memset", "iota", "affine_select",
    "activation", "reciprocal",
}



CHAIN_OFF = False


class _TTrack:

    def __init__(self, t):
        object.__setattr__(self, "_t", t)
        object.__setattr__(self, "last", None)

    def matmul(self, *a, **k):
        r = self._t.matmul(*a, **k)
        object.__setattr__(self, "last", r)
        return r

    def transpose(self, *a, **k):
        r = self._t.transpose(*a, **k)
        object.__setattr__(self, "last", r)
        return r

    def last_inst(self):
        return self.last if self.last is not None else self._t.nop()

    def __getattr__(self, n):
        return getattr(self._t, n)


class _Chain:

    def __init__(self, eng, S, nm):
        self._eng = eng
        self._S = S
        self._nm = nm

    def __getattr__(self, name):
        f = getattr(self._eng, name)
        if name in _CHAINED_OPS and not CHAIN_OFF:
            def g(*a, **k):
                self._eng.wait_ge(self._S.h[self._nm], self._S.n[self._nm])
                r = f(*a, **k)
                self._S.inc(r, self._nm)
                return r
            return g
        return f


def n512a(lo, hi):
    out = []
    x = lo
    while x < hi:
        nxt = min(hi, (x // 512 + 1) * 512)
        out.append((x, nxt))
        x = nxt
    return out


def n512(lo, hi):
    out = []
    x = lo
    while x < hi:
        out.append((x, min(x + 512, hi)))
        x = min(x + 512, hi)
    return out


def emit_stage1(nc, blk, S, io, upto="all", hook_after_A=None):
    xT = io["xT"]
    xtok = io["xtok"]
    w_attn = io["w_attn"]
    w_attnT = io["w_attnT"]
    b_attn = io["b_attn"]
    w_proj = io["w_proj"]
    b_proj = io["b_proj"]
    bproj_col = io["bproj_col"]
    wk_cT = io["wk_cT"]
    wk_e = io["wk_e"]
    wv_cT = io["wv_cT"]
    wv_e = io["wv_e"]
    rms_w = io["rms_w"]
    rms_w_col = io["rms_w_col"]
    battn_col = io["battn_col"]
    gate_wT = io["gate_wT"]
    sw_gate = io["sw_gate"]
    sw_up = io["sw_up"]
    sw_down = io["sw_down"]
    partial = io["partial"]
    xm_out = io["xm_out"]
    meta = io["meta"]
    counts = io["counts"]
    st = ExitStack()
    with st:
        sbt = lambda *a: st.enter_context(nc.sbuf_tensor(*a))

        xnT = sbt("xnT", [128, NCH, T], BF16)
        vtok = sbt("vtok", [128, NT, NH * 65], BF16)
        attnTn = sbt("attnTn", [128, NCH, T], BF16)
        khat = sbt("khat", [128, NCH, NH * HD], BF16)
        ones_col = sbt("ones_col", [1, 128], F32)
        ones_r64 = sbt("ones_r64", [128, 128], F32)
        onesc128f = sbt("onesc128f", [128, 1], F32)
        onesc128b = sbt("onesc128b", [128, 1], BF16)
        ones16 = sbt("ones16", [16, 1], F32)
        sel2 = sbt("sel2", [2, 256], F32)
        hsel = sbt("hsel", [128, 2], F32)
        iden = sbt("iden", [128, 128], F32)
        iota64 = sbt("iota64", [128, E], F32)
        triU = sbt("triU", [128, 128], BF16)
        triU16 = sbt("triU16", [16, 16], F32)
        triD = sbt("triD", [128, 128], BF16)
        rmsw_col = sbt("rmsw_col", [128, NCH], F32)
        bq_col = sbt("bq_col", [128, NCH], F32)
        bkv = sbt("bkv", [128, 16], F32)
        bkh_col = sbt("bkh_col", [128, NCH], F32)
        m_cols = sbt("m_cols", [128, NT], F32)
        rstd_cols = sbt("rstd_cols", [128, NT], F32)
        g_cols = sbt("g_cols", [128, NT], F32)
        cg_cols = sbt("cg_cols", [128, NT], F32)
        meta_t = sbt("meta_t", [128, NT, 6], F32)
        small = sbt("small", [128, 64], F32)

        with _passthrough(blk) as blk:
            def _fin():
                @blk.gpsimd
                def _(g):
                    for nm in ("ld", "gp", "st", "lt0", "lt1", "lc0", "sx0", "sx1",
                               "sp0", "sp1", "ga0", "ga1"):
                        S.wait(g, nm)
                    for nm in ("pe", "dvx", "acx", "gcx", "dvx", "acx", "gcx"):
                        S.wait(g, nm)
                    g.nop()
            @blk.gpsimd
            def _(g):
                g = _Chain(g, S, "gcx")
                def ch(instr):
                    return instr
                ch(g.memset(ones_col[:, :], 1.0))
                ch(g.memset(ones_r64[:, :], 0.0))
                ch(g.memset(ones_r64[64:65, :], 1.0))
                ch(g.memset(onesc128f[:, :], 1.0))
                ch(g.memset(onesc128b[:, :], 1.0))
                ch(g.memset(ones16[:, :], 1.0))
                ch(g.memset(sel2[:, :], 1.0))
                ch(g.affine_select(sel2[:, :], sel2[:, :], [[1, 256]], OP.is_ge, 0.0,
                                   base=0, channel_multiplier=-128))
                ch(g.affine_select(sel2[:, :], sel2[:, :], [[-1, 256]], OP.is_ge, 0.0,
                                   base=127, channel_multiplier=128))
                ch(g.memset(hsel[:, :], 0.0))
                ch(g.memset(hsel[0:64, 0:1], 1.0))
                ch(g.memset(hsel[64:128, 1:2], 1.0))
                ch(g.memset(iden[:, :], 1.0))
                ch(g.affine_select(iden[:, :], iden[:, :], [[-1, 128]], OP.is_equal, 0.0,
                                   base=0, channel_multiplier=1))
                ch(g.iota(iota64[:, :], [[1, E]], base=0, channel_multiplier=0,
                          allow_small_or_imprecise_dtypes=True))
                ch(g.memset(triU[:, :], 1.0))
                ch(g.affine_select(triU[:, :], triU[:, :], [[1, 128]], OP.is_ge, 0.0,
                                   base=-1, channel_multiplier=-1))
                ch(g.memset(triU16[:, :], 1.0))
                ch(g.affine_select(triU16[:, :], triU16[:, :], [[1, 16]], OP.is_ge, 0.0,
                                   base=-1, channel_multiplier=-1))
                ch(g.memset(triD[:, :], 1.0))
                ch(g.affine_select(triD[:, :], triD[:, :], [[1, 128]], OP.is_ge, 0.0,
                                   base=0, channel_multiplier=-1))
                for h in range(NH):
                    ch(g.memset(vtok[:, :, h * 65 + 64:h * 65 + 65], 1.0))

            @blk.sync
            def _(sy):
                S.pre(sy, "ld"); S.dma(sy.dma_start(out=rmsw_col[:, :], in_=rms_w_col))
                S.pre(sy, "ld"); S.dma(sy.dma_start(out=bq_col[:, :], in_=battn_col[:, 0:NCH]))
                S.pre(sy, "ld"); S.dma(sy.dma_start(out=bkv[:, :], in_=battn_col[:, NCH:3 * NCH]))
            ld0 = S.n["ld"]

            if upto == "consts":
                _fin()
                return nc
            with ExitStack() as ph:
                _e = ph.enter_context
                xt = _e(nc.sbuf_tensor("xt", [128, 2, T], F32))
                xtk = _e(nc.sbuf_tensor("xtk", [128, 2, C], F32))
                sq = _e(nc.sbuf_tensor("sq", [128, C], F32))
                ssq = _e(nc.sbuf_tensor("ssq", [128, NT], F32))
                xnf = _e(nc.sbuf_tensor("xnf", [128, 1, T], F32))
                xmbf = _e(nc.sbuf_tensor("xmbf", [128, 2, C], BF16))
                m_b = _e(nc.sbuf_tensor("m_b", [128, T], F32))
                rstd_b = _e(nc.sbuf_tensor("rstd_b", [128, T], F32))
                rmsw_b = _e(nc.sbuf_tensor("rmsw_b", [128, C], F32))
                gwT = _e(nc.sbuf_tensor("gwT", [128, NCH, E], F32))
                stile_b = _e(nc.sbuf_tensor("stile_b", [128, E], F32))
                ltile = _e(nc.sbuf_tensor("ltile", [128, E], F32))
                gacc = _e(nc.sbuf_tensor("gacc", [128, NT * E], F32))
                chosen = _e(nc.sbuf_tensor("chosen", [128, NT, E], BF16))
                dcol = _e(nc.sbuf_tensor("dcol", [128, NT], F32))
                mx8 = _e(nc.sbuf_tensor("mx8", [128, 8], F32))
                idx8 = _e(nc.sbuf_tensor("idx8", [128, 8], U32))
                oh = _e(nc.sbuf_tensor("oh", [128, 2, E], F32))
                w64 = _e(nc.sbuf_tensor("w64", [128, E], F32))
                cnt16 = _e(nc.sbuf_tensor("cnt16", [16, E], F32))
                toff16 = _e(nc.sbuf_tensor("toff16", [16, E], F32))
                psA = _e(nc.psum_tensor("psA1", [128, 2048], F32))
                psB = _e(nc.psum_tensor("psB1", [128, 1024], F32))
                psC = _e(nc.psum_tensor("psC1", [128, 512], F32))

                xtok_r = xtok.rearrange("(n p) c -> p n c", p=128)
                xT_r = xT.rearrange("(c p) t -> p c t", p=128)

                @blk.sync
                def _(sy):
                    S.pre(sy, "ld"); S.dma(sy.dma_start(out=gacc[0:1, 0:C], in_=rms_w))
                    S.pre(sy, "ld"); S.dma(sy.dma_start(out=gwT[:, :, :], in_=gate_wT.rearrange("(c p) e -> p c e", p=128)))
                ld0b = S.n["ld"]

                @blk.vector
                def _(v):
                    v = _Chain(v, S, "dvx")
                    v.wait_ge(S.h["ld"], ld0b)
                    S.wait(v, "gcx")
                    for c_ in range(NCH):
                        v.tensor_scalar(gwT[:, c_, :], gwT[:, c_, :], rmsw_col[:, c_:c_ + 1], None, OP.mult)

                @blk.tensor
                def _(t):
                    t = _TTrack(t)
                    S.wait(t, "dvx")
                    for c_ in range(NCH):
                        t.matmul(psB[0:1, 0:E], onesc128f[:, :], gwT[:, c_, :],
                                 start=(c_ == 0), stop=(c_ == NCH - 1))
                    for (a, b) in n512(0, C):
                        t.matmul(psA[:, a:b], ones_col[:, :], gacc[0:1, a:b], start=True, stop=True)
                    S.inc(t.last_inst(), "pe")

                @blk.vector
                def _(v):
                    v = _Chain(v, S, "dvx")
                    S.wait(v, "pe")
                    v.tensor_copy(small[0:1, 0:E], psB[0:1, 0:E])
                    v.tensor_copy(rmsw_b[:, :], psA[:, 0:C])

                @blk.tensor
                def _(t):
                    t = _TTrack(t)
                    S.wait(t, "dvx")
                    t.matmul(psB[:, 0:E], ones_col[:, :], small[0:1, 0:E], start=True, stop=True)
                    S.inc(t.last_inst(), "pe")

                @blk.vector
                def _(v):
                    v = _Chain(v, S, "dvx")
                    S.wait(v, "pe")
                    v.tensor_copy(stile_b[:, :], psB[:, 0:E])

                ld_marks = []
                st_marks = []
                dv_marks = []
                for tt in range(NT):
                    @blk.sync
                    def _(sy, tt=tt):
                        if tt >= 2:
                            S.wait(sy, "dvx", dv_marks[tt - 2])
                        S.dma(sy.dma_start(out=xtk[:, tt % 2, :], in_=xtok_r[:, tt, :]), f"lt{tt % 2}")
                        ld_marks.append(S.n[f"lt{tt % 2}"])

                    @blk.scalar
                    def _(sc, tt=tt):
                        sc = _Chain(sc, S, "acx")
                        sc.wait_ge(S.h[f"lt{tt % 2}"], ld_marks[tt])
                        S.wait(sc, "dvx")
                        sc.activation(out=sq[:, :], in_=xtk[:, tt % 2, :], func=AF.Square,
                                      accum_out=ssq[:, tt:tt + 1])

                    @blk.vector
                    def _(v, tt=tt):
                        v = _Chain(v, S, "dvx")
                        v.wait_ge(S.h[f"lt{tt % 2}"], ld_marks[tt])
                        v.reduce_sum(m_cols[:, tt:tt + 1], xtk[:, tt % 2, :], AX)
                        S.wait(v, "acx")
                        c0 = slice(tt, tt + 1)
                        v.tensor_scalar(m_cols[:, c0], m_cols[:, c0], 1.0 / C, None, OP.mult)
                        v.tensor_scalar(ssq[:, c0], ssq[:, c0], 1.0 / C, None, OP.mult)
                        v.tensor_tensor(g_cols[:, c0], m_cols[:, c0], m_cols[:, c0], OP.mult)
                        v.tensor_tensor(ssq[:, c0], ssq[:, c0], g_cols[:, c0], OP.subtract)
                        v.tensor_scalar(rstd_cols[:, c0], ssq[:, c0], EPS_LN, None, OP.add)

                    @blk.scalar
                    def _(sc, tt=tt):
                        sc = _Chain(sc, S, "acx")
                        S.wait(sc, "dvx")
                        c0 = slice(tt, tt + 1)
                        sc.activation(out=cg_cols[:, c0], in_=rstd_cols[:, c0], func=AF.Ln)
                        sc.activation(out=rstd_cols[:, c0], in_=cg_cols[:, c0], func=AF.Exp, scale=-0.5)
                        sc.activation(out=g_cols[:, c0], in_=cg_cols[:, c0], func=AF.Exp, scale=-1.0)

                    @blk.vector
                    def _(v, tt=tt):
                        v = _Chain(v, S, "dvx")
                        S.wait(v, "acx")
                        c0 = slice(tt, tt + 1)
                        v.tensor_tensor(g_cols[:, c0], g_cols[:, c0], ssq[:, c0], OP.mult)
                        v.tensor_scalar(g_cols[:, c0], g_cols[:, c0], EPS_RMS, None, OP.add)

                    @blk.scalar
                    def _(sc, tt=tt):
                        sc = _Chain(sc, S, "acx")
                        S.wait(sc, "dvx")
                        c0 = slice(tt, tt + 1)
                        sc.activation(out=g_cols[:, c0], in_=g_cols[:, c0], func=AF.Ln)
                        sc.activation(out=g_cols[:, c0], in_=g_cols[:, c0], func=AF.Exp, scale=-0.5)

                    @blk.vector
                    def _(v, tt=tt):
                        v = _Chain(v, S, "dvx")
                        S.wait(v, "acx")
                        c0 = slice(tt, tt + 1)
                        v.tensor_tensor(cg_cols[:, c0], rstd_cols[:, c0], g_cols[:, c0], OP.mult)
                        if tt >= 2:
                            v.wait_ge(S.h[f"sx{tt % 2}"], st_marks[tt - 2])
                        v.tensor_scalar(sq[:, :], xtk[:, tt % 2, :], m_cols[:, c0],
                                        cg_cols[:, c0], OP.subtract, OP.mult)
                        v.tensor_tensor(xmbf[:, tt % 2, :], sq[:, :], rmsw_b[:, :], OP.mult)
                        dv_marks.append(S.n["dvx"])

                    @blk.sync
                    def _(sy, tt=tt):
                        S.wait(sy, "dvx")
                        S.dma(sy.dma_start(out=xm_out.rearrange("(n p) c -> p n c", p=128)[:, tt, :],
                                           in_=xmbf[:, tt % 2, :]), f"sx{tt % 2}")
                        st_marks.append(S.n[f"sx{tt % 2}"])
                dv_stats = S.n["dvx"]

                ph_rows = ExitStack()
                m_row = ph_rows.enter_context(nc.sbuf_tensor("m_row", [1, T], F32))
                for (colt, bt) in ((m_cols, m_b), (rstd_cols, rstd_b)):
                    @blk.tensor
                    def _(t, colt=colt):
                        t = _TTrack(t)
                        if colt is m_cols:
                            t.wait_ge(S.h["dvx"], dv_stats)
                        else:
                            S.wait(t, "dvx")
                        for tt in range(NT):
                            t.transpose(psA[0:1, tt * 128:(tt + 1) * 128], colt[:, tt:tt + 1], iden[:, :])
                        S.inc(t.last_inst(), "pe")

                    @blk.vector
                    def _(v):
                        v = _Chain(v, S, "dvx")
                        S.wait(v, "pe")
                        v.tensor_copy(m_row[:, :], psA[0:1, 0:T])

                    @blk.tensor
                    def _(t):
                        t = _TTrack(t)
                        S.wait(t, "dvx")
                        for (a, b) in n512(0, T):
                            t.matmul(psA[:, a:b], ones_col[:, :], m_row[:, a:b], start=True, stop=True)
                        S.inc(t.last_inst(), "pe")

                    @blk.vector
                    def _(v, bt=bt):
                        v = _Chain(v, S, "dvx")
                        S.wait(v, "pe")
                        v.tensor_copy(bt[:, :], psA[:, :])
                dv_bcast = S.n["dvx"]
                ph_rows.close()

                pe_marks = {}
                stx_marks = {}
                ldc_marks = []
                for c_ in range(NCH):
                    buf = 0

                    @blk.sync
                    def _(sy, c_=c_, buf=buf):
                        if c_ >= 1:
                            sy.wait_ge(S.h[f"sp{(c_ - 1) % 2}"], stx_marks[c_ - 1])
                        S.dma(sy.dma_start(out=xt[:, buf, :], in_=xT_r[:, c_, :]), "lc0")
                        ldc_marks.append(S.n["lc0"])

                    @blk.vector
                    def _(v, c_=c_, buf=buf):
                        v = _Chain(v, S, "dvx")
                        if c_ == 0:
                            v.wait_ge(S.h["dvx"], dv_bcast)
                        v.wait_ge(S.h["lc0"], ldc_marks[c_])
                        if c_ >= 2:
                            v.wait_ge(S.h["pe"], pe_marks[c_ - 2])
                        v.tensor_tensor(xnf[:, buf, :], xt[:, buf, :], m_b[:, :], OP.subtract)
                        v.tensor_tensor(xnf[:, buf, :], xnf[:, buf, :], rstd_b[:, :], OP.mult)
                        v.tensor_copy(xnT[:, c_, :], xnf[:, buf, :])
                        v.tensor_tensor(xt[:, buf, :], xt[:, buf, :], xnf[:, buf, :], OP.add)

                    @blk.sync
                    def _(sy, c_=c_, buf=buf):
                        S.wait(sy, "dvx")
                        S.dma(sy.dma_start(out=partial[c_ * 128:(c_ + 1) * 128, :], in_=xt[:, buf, :]), f"sp{c_ % 2}")
                        stx_marks[c_] = S.n[f"sp{c_ % 2}"]

                    @blk.tensor
                    def _(t, c_=c_, buf=buf):
                        t = _TTrack(t)
                        S.wait(t, "dvx")
                        if c_ >= 1:
                            S.wait(t, "dvx")
                        for tt in range(NT):
                            t.matmul(psB[:, tt * E:(tt + 1) * E],
                                     xnf[:, buf, tt * 128:(tt + 1) * 128], gwT[:, c_, :],
                                     start=True, stop=True)
                        S.inc(t.last_inst(), "pe")
                        pe_marks[c_] = S.n["pe"]

                    @blk.vector
                    def _(v, c_=c_):
                        v = _Chain(v, S, "dvx")
                        S.wait(v, "pe")
                        if c_ == 0:
                            v.tensor_copy(gacc[:, :], psB[:, 0:NT * E])
                        else:
                            v.tensor_tensor(gacc[:, :], gacc[:, :], psB[:, 0:NT * E], OP.add)

                for tt in range(NT):
                    @blk.vector
                    def _(v, tt=tt):
                        v = _Chain(v, S, "dvx")
                        if tt == 0:
                            S.wait(v, "pe")
                        v.tensor_scalar(ltile[:, :], gacc[:, tt * E:(tt + 1) * E],
                                        g_cols[:, tt:tt + 1], None, OP.mult)
                        v.max(out=mx8[:, :], in_=ltile[:, :])
                        v.max_index(out=idx8[:, :], in_max=mx8[:, :], in_values=ltile[:, :])
                        v.tensor_copy(meta_t[:, tt, 0:2], idx8[:, 0:2])
                        v.tensor_tensor(dcol[:, tt:tt + 1], mx8[:, 0:1], mx8[:, 1:2], OP.subtract)
                        v.tensor_scalar(oh[:, 0, :], iota64[:, :], meta_t[:, tt, 0:1], None, OP.is_equal)
                        v.tensor_scalar(oh[:, 1, :], iota64[:, :], meta_t[:, tt, 1:2], None, OP.is_equal)
                        v.tensor_tensor(w64[:, :], oh[:, 0, :], oh[:, 1, :], OP.add)
                        v.tensor_copy(chosen[:, tt, :], w64[:, :])

                    @blk.tensor
                    def _(t, tt=tt):
                        t = _TTrack(t)
                        S.wait(t, "dvx")
                        t.matmul(psC[:, 0:E], triU[:, :], chosen[:, tt, :], start=True, stop=True)
                        S.inc(t.last_inst(), "pe")

                    @blk.vector
                    def _(v, tt=tt):
                        v = _Chain(v, S, "dvx")
                        S.wait(v, "pe")
                        for k in range(2):
                            v.tensor_tensor(w64[:, :], psC[:, 0:E], oh[:, k, :], OP.mult)
                            v.reduce_sum(meta_t[:, tt, 4 + k:5 + k], w64[:, :], AX)

                @blk.scalar
                def _(sc):
                    sc = _Chain(sc, S, "acx")
                    S.wait(sc, "dvx")
                    sc.activation(out=meta_t[:, :, 2:3], in_=dcol[:, :].rearrange("p (n o) -> p n o", o=1),
                                  func=AF.Sigmoid)
                    sc.activation(out=meta_t[:, :, 3:4], in_=dcol[:, :].rearrange("p (n o) -> p n o", o=1),
                                  func=AF.Sigmoid, scale=-1.0)

                @blk.tensor
                def _(t):
                    t = _TTrack(t)
                    S.wait(t, "dvx")
                    ch_flat = chosen.ap().rearrange("p n e -> p (n e)")
                    t.matmul(psC[0:1, 0:512], onesc128b[:, :], ch_flat[:, 0:512], start=True, stop=True)
                    t.matmul(psB[0:1, 0:512], onesc128b[:, :], ch_flat[:, 512:1024], start=True, stop=True)
                    S.inc(t.last_inst(), "pe")

                @blk.vector
                def _(v):
                    v = _Chain(v, S, "dvx")
                    S.wait(v, "pe")
                    v.tensor_copy(gacc[0:1, 0:512], psC[0:1, 0:512])
                    v.tensor_copy(gacc[0:1, 512:1024], psB[0:1, 0:512])

                @blk.gpsimd
                def _(g):
                    g = _Chain(g, S, "gcx")
                    S.wait(g, "dvx")
                    S.pre(g, "gp")
                    S.dma(g.dma_start(out=cnt16[:, :], in_=gacc[0:1, 0:NT * E]), "gp")

                @blk.tensor
                def _(t):
                    t = _TTrack(t)
                    S.wait(t, "gp")
                    t.matmul(psC[0:16, 0:E], triU16[:, :], cnt16[:, :], start=True, stop=True)
                    t.matmul(psB[0:1, 0:E], ones16[:, :], cnt16[:, :], start=True, stop=True)
                    S.inc(t.last_inst(), "pe")

                @blk.vector
                def _(v):
                    v = _Chain(v, S, "dvx")
                    S.wait(v, "pe")
                    v.tensor_copy(toff16[:, :], psC[0:16, 0:E])
                    v.tensor_copy(small[0:1, 0:E], psB[0:1, 0:E])

                @blk.sync
                def _(sy):
                    S.wait(sy, "dvx")
                    S.pre(sy, "st")
                    S.dma(sy.dma_start(out=counts, in_=small[0:1, 0:E]), "st")

                for tt in range(NT):
                    @blk.gpsimd
                    def _(g, tt=tt):
                        g = _Chain(g, S, "gcx")
                        S.wait(g, "dvx")
                        S.pre(g, "gp")
                        if tt == 0:
                            S.wait(g, "st")
                        S.dma(g.dma_start(out=small[0:1, 0:E], in_=toff16[tt:tt + 1, :]), "gp")

                    @blk.tensor
                    def _(t, tt=tt):
                        t = _TTrack(t)
                        S.wait(t, "gp")
                        t.matmul(psB[:, 0:E], ones_col[:, :], small[0:1, 0:E],
                                 start=True, stop=True)
                        S.inc(t.last_inst(), "pe")

                    @blk.vector
                    def _(v, tt=tt):
                        v = _Chain(v, S, "dvx")
                        S.wait(v, "pe")
                        v.tensor_copy(w64[:, :], psB[:, 0:E])
                        for k in range(2):
                            v.tensor_scalar(oh[:, 0, :], iota64[:, :], meta_t[:, tt, k:k + 1], None, OP.is_equal)
                            v.tensor_tensor(oh[:, 0, :], oh[:, 0, :], w64[:, :], OP.mult)
                            v.reduce_sum(oh[:, 1, 0:1], oh[:, 0, :], AX)
                            v.tensor_tensor(meta_t[:, tt, 4 + k:5 + k], meta_t[:, tt, 4 + k:5 + k],
                                            oh[:, 1, 0:1], OP.add)

                @blk.sync
                def _(sy):
                    S.wait(sy, "dvx")
                    S.wait(sy, "acx")
                    S.pre(sy, "st")
                    S.dma(sy.dma_start(out=meta.rearrange("(n p) c -> p n c", p=128), in_=meta_t[:, :, :]), "st")
                    S.wait(sy, "st")
                    sy.wait_ge(S.h["sx0"], S.n["sx0"])
                    sy.wait_ge(S.h["sx1"], S.n["sx1"])
                    sy.wait_ge(S.h["sp0"], S.n["sp0"])
                    sy.wait_ge(S.h["sp1"], S.n["sp1"])

            if upto == "A":
                _fin()
                return nc
            if hook_after_A is not None:
                hook_after_A()
            with ExitStack() as ph:
                _e = ph.enter_context
                w2t = _e(nc.sbuf_tensor("w2t", [64, 128], F32))
                w2d = _e(nc.sbuf_tensor("w2d", [128, 64], F32))
                wfold = _e(nc.sbuf_tensor("wfold", [128, NCH, C], F32))
                brow = _e(nc.sbuf_tensor("brow", [1, C], F32))
                vhat = _e(nc.sbuf_tensor("vhat", [128, NCH, NH * HD], BF16))
                bvh_b = _e(nc.sbuf_tensor("bvh_b", [128, C], F32))
                psA = _e(nc.psum_tensor("psA2", [128, 1024], F32))
                psB = _e(nc.psum_tensor("psB2", [128, 1024], F32))
                psC = _e(nc.psum_tensor("psC2", [128, 512], F32))

                for side in ("k", "v"):
                    @blk.sync
                    def _(sy, side=side):
                        sy.wait_ge(S.h["pe"], S.n["pe"])
                        sy.wait_ge(S.h["dvx"], S.n["dvx"])
                        sy.wait_ge(S.h["acx"], S.n["acx"])
                        S.pre(sy, "ld")
                        S.dma(sy.dma_start(out=w2t[0:32, 0:64], in_=(wk_cT if side == "k" else wv_cT)))
                        S.pre(sy, "ld")
                        S.dma(sy.dma_start(out=w2t[0:32, 64:128], in_=(wk_e if side == "k" else wv_e)))
                        lo = C if side == "k" else 2 * C
                        S.pre(sy, "ld")
                        S.dma(sy.dma_start(out=wfold[:, :, :],
                                           in_=w_attnT[lo:lo + C, :].rearrange("(c p) q -> p c q", p=128)))
                    ldw_ = S.n["ld"]

                    @blk.tensor
                    def _(t, ldw_=ldw_):
                        t = _TTrack(t)
                        t.wait_ge(S.h["ld"], ldw_)
                        t.matmul(psC[0:64, 0:64], w2t[0:32, 0:64], w2t[0:32, 64:128], start=True, stop=True)
                        t.matmul(psB[64:128, 0:64], w2t[0:32, 0:64], w2t[0:32, 64:128], start=True, stop=True)
                        S.inc(t.last_inst(), "pe")

                    @blk.vector
                    def _(v, side=side):
                        v = _Chain(v, S, "dvx")
                        S.wait(v, "pe")
                        v.tensor_scalar(w2d[0:64, 0:64], psC[0:64, 0:64],
                                        0.125 if side == "k" else 1.0, None, OP.mult)
                        v.tensor_scalar(w2d[64:128, 0:64], psB[64:128, 0:64],
                                        0.125 if side == "k" else 1.0, None, OP.mult)

                    if upto == "fk1" and side == "k":
                        _fin()
                        return nc
                    for ct in range(NCH):
                        @blk.tensor
                        def _(t, ct=ct):
                            t = _TTrack(t)
                            S.wait(t, "dvx")
                            for h in [x for x in range(NH) if x % 2 == 0] + \
                                     [x for x in range(NH) if x % 2 == 1]:
                                t.matmul(psB[:, h * 64:(h + 1) * 64],
                                         wfold[(h % 2) * 64:(h % 2) * 64 + 64, h // 2, ct * 128:(ct + 1) * 128],
                                         w2d[(h % 2) * 64:(h % 2) * 64 + 64, 0:64], start=True, stop=True)
                            S.inc(t.last_inst(), "pe")

                        @blk.vector
                        def _(v, ct=ct, side=side):
                            v = _Chain(v, S, "dvx")
                            S.wait(v, "pe")
                            v.tensor_copy((khat if side == "k" else vhat)[:, ct, :], psB[:, 0:C])

                    if upto == "fk2" and side == "k":
                        _fin()
                        return nc
                    @blk.tensor
                    def _(t, side=side):
                        t = _TTrack(t)
                        S.wait(t, "dvx")
                        off = 0 if side == "k" else NCH
                        for h in [x for x in range(NH) if x % 2 == 0] + \
                                 [x for x in range(NH) if x % 2 == 1]:
                            t.matmul(psA[0:1, h * 64:(h + 1) * 64],
                                     bkv[(h % 2) * 64:(h % 2) * 64 + 64, off + h // 2:off + h // 2 + 1],
                                     w2d[(h % 2) * 64:(h % 2) * 64 + 64, 0:64], start=True, stop=True)
                        S.inc(t.last_inst(), "pe")

                    @blk.vector
                    def _(v):
                        v = _Chain(v, S, "dvx")
                        S.wait(v, "pe")
                        v.tensor_copy(brow[0:1, 0:C], psA[0:1, 0:C])

                    if upto == "fk3" and side == "k":
                        _fin()
                        return nc
                    if side == "k":
                        @blk.tensor
                        def _(t):
                            t = _TTrack(t)
                            S.wait(t, "dvx")
                            for c_ in range(NCH):
                                t.matmul(psC[:, c_:c_ + 1], brow[0:1, c_ * 128:(c_ + 1) * 128],
                                         ones_col[0:1, 0:1], start=True, stop=True)
                            S.inc(t.last_inst(), "pe")

                        @blk.vector
                        def _(v):
                            v = _Chain(v, S, "dvx")
                            S.wait(v, "pe")
                            v.tensor_copy(bkh_col[:, :], psC[:, 0:NCH])
                    else:
                        @blk.tensor
                        def _(t):
                            t = _TTrack(t)
                            S.wait(t, "dvx")
                            for (a, b) in n512(0, C):
                                t.matmul(psA[:, a:b], ones_col[:, :], brow[0:1, a:b], start=True, stop=True)
                            S.inc(t.last_inst(), "pe")

                        @blk.vector
                        def _(v):
                            v = _Chain(v, S, "dvx")
                            S.wait(v, "pe")
                            v.tensor_copy(bvh_b[:, :], psA[:, 0:C])

                if upto == "folds0":
                    _fin()
                    return nc
                with nc.psum_tensor("psV", [128, 1024], F32) as psV:
                    for tt in range(NT):
                        @blk.tensor
                        def _(t, tt=tt):
                            t = _TTrack(t)
                            S.wait(t, "dvx")
                            for cc in range(NCH):
                                for (a, b) in n512(0, C):
                                    t.matmul(psV[:, a:b], xnT[:, cc, tt * 128:(tt + 1) * 128],
                                             vhat[:, cc, a:b], start=(cc == 0), stop=(cc == NCH - 1))
                            S.inc(t.last_inst(), "pe")

                        @blk.vector
                        def _(v, tt=tt):
                            v = _Chain(v, S, "dvx")
                            S.wait(v, "pe")
                            v.tensor_tensor(
                                vtok[:, tt, :].rearrange("p (h x) -> p h x", h=NH)[:, :, 0:64],
                                psV[:, 0:C].rearrange("p (h x) -> p h x", h=NH),
                                bvh_b[:, 0:C].rearrange("p (h x) -> p h x", h=NH),
                                OP.add)


            if upto == "folds":
                _fin()
                return nc
            with nc.sbuf_tensor("wq_sb", [128, NCH, C], BF16) as wq_sb:
                @blk.gpsimd
                def _(g):
                    g = _Chain(g, S, "gcx")
                    S.wait(g, "pe")
                    S.wait(g, "dvx")
                    S.pre(g, "gp")
                    S.dma(g.dma_start(out=wq_sb[:, :, :],
                                      in_=w_attn[:, 0:C].rearrange("(c p) f -> p c f", p=128)), "gp")
                gp_wq = S.n["gp"]

                for hp in range(NH // 2):
                    with ExitStack() as ph:
                        _e = ph.enter_context
                        qp = _e(nc.sbuf_tensor(f"qp{hp}", [128, T], BF16))
                        kp = _e(nc.sbuf_tensor(f"kp{hp}", [128, T], BF16))
                        sqn = _e(nc.sbuf_tensor(f"sqn{hp}", [128, T], F32))
                        nrm = _e(nc.sbuf_tensor(f"nrm{hp}", [2, 8], F32))
                        mneg2 = _e(nc.sbuf_tensor(f"mneg2{hp}", [128, 2], F32))

                        with ExitStack() as ph2:
                            psQ = ph2.enter_context(nc.psum_tensor(f"psQ{hp}", [128, 2048], F32))
                            psN = ph2.enter_context(nc.psum_tensor(f"psN{hp}", [2, 2048], F32))
                            for (which, wtile, outt) in (("q", wq_sb, qp), ("k", khat, kp)):
                                @blk.tensor
                                def _(t, which=which, wtile=wtile, hp=hp):
                                    t = _TTrack(t)
                                    if which == "q":
                                        t.wait_ge(S.h["gp"], gp_wq)
                                    S.wait(t, "dvx")
                                    for cc in range(NCH):
                                        for (a, b) in n512(0, T):
                                            t.matmul(psQ[:, a:b], wtile[:, cc, hp * 128:(hp + 1) * 128],
                                                     xnT[:, cc, a:b], start=(cc == 0), stop=(cc == NCH - 1))
                                    S.inc(t.last_inst(), "pe")

                                @blk.vector
                                def _(v, which=which, outt=outt, hp=hp):
                                    v = _Chain(v, S, "dvx")
                                    S.wait(v, "pe")
                                    col = bq_col[:, hp:hp + 1] if which == "q" else bkh_col[:, hp:hp + 1]
                                    v.tensor_scalar(outt[:, :], psQ[:, :], col, None, OP.add)

                                @blk.scalar
                                def _(sc, outt=outt):
                                    sc = _Chain(sc, S, "acx")
                                    S.wait(sc, "dvx")
                                    sc.activation(out=sqn[:, :], in_=outt[:, :], func=AF.Square)

                                @blk.tensor
                                def _(t, which=which):
                                    t = _TTrack(t)
                                    S.wait(t, "acx")
                                    S.wait(t, "dvx")
                                    for (a, b) in n512(0, T):
                                        t.matmul(psN[0:2, a:b], hsel[:, :], sqn[:, a:b], start=True, stop=True)
                                    S.inc(t.last_inst(), "pe")

                                @blk.vector
                                def _(v, which=which):
                                    v = _Chain(v, S, "dvx")
                                    S.wait(v, "pe")
                                    idx = 0 if which == "q" else 1
                                    v.reduce_max(nrm[0:2, idx:idx + 1], psN[0:2, 0:T], AX)

                        @blk.scalar
                        def _(sc):
                            sc = _Chain(sc, S, "acx")
                            S.wait(sc, "dvx")
                            sc.activation(out=nrm[0:2, 2:4], in_=nrm[0:2, 0:2], func=AF.Sqrt)

                        @blk.vector
                        def _(v):
                            v = _Chain(v, S, "dvx")
                            S.wait(v, "acx")
                            v.tensor_tensor(nrm[0:2, 4:5], nrm[0:2, 2:3], nrm[0:2, 3:4], OP.mult)
                            v.tensor_scalar(nrm[0:2, 5:6], nrm[0:2, 4:5], -1.0, None, OP.mult)

                        with nc.psum_tensor(f"psM{hp}", [128, 512], F32) as psM:
                            @blk.tensor
                            def _(t):
                                t = _TTrack(t)
                                S.wait(t, "dvx")
                                t.matmul(psM[:, 0:1], sel2[:, 0:128], nrm[0:2, 5:6], start=True, stop=True)
                                t.matmul(psM[:, 1:2], sel2[:, 128:256], nrm[0:2, 5:6], start=True, stop=True)
                                S.inc(t.last_inst(), "pe")

                            @blk.vector
                            def _(v):
                                v = _Chain(v, S, "dvx")
                                S.wait(v, "pe")
                                v.tensor_copy(mneg2[:, :], psM[:, 0:2])

                        with ExitStack() as ph3:
                            _e3 = ph3.enter_context
                            pt = _e3(nc.sbuf_tensor(f"pt{hp}", [128, 2, 1024], BF16))
                            attn_un = _e3(nc.sbuf_tensor(f"attn_un{hp}", [64, T], F32))
                            lex = _e3(nc.sbuf_tensor(f"lex{hp}", [128, T], F32))
                            psS0 = _e3(nc.psum_tensor(f"psS0{hp}", [128, 1024], F32))
                            psS1 = _e3(nc.psum_tensor(f"psS1{hp}", [128, 1024], F32))
                            psAV = _e3(nc.psum_tensor(f"psAV{hp}", [65, 2048], F32))
                            psS = (psS0, psS1)
                            for lh in range(2):
                                h = hp * 2 + lh
                                r0 = lh * 64
                                sl = 0
                                expm = []
                                maskm = []
                                entry_dv = S.n["dvx"]
                                entry_ac = S.n["acx"]
                                for kc in range(NT):
                                    qlo = kc * 128
                                    pos = qlo
                                    while pos < T:
                                        qhi = min(pos + 1024, T)
                                        par = sl % 2

                                        @blk.tensor
                                        def _(t, kc=kc, pos=pos, qhi=qhi, par=par, r0=r0, sl=sl):
                                            t = _TTrack(t)
                                            if sl == 0:
                                                t.wait_ge(S.h["dvx"], entry_dv)
                                                t.wait_ge(S.h["acx"], entry_ac)
                                            if sl >= 2:
                                                t.wait_ge(S.h["acx"], expm[sl - 2])
                                            for (a, b) in n512(pos, qhi):
                                                t.matmul(psS[par][:, a - pos:b - pos],
                                                         kp[r0:r0 + 64, kc * 128:(kc + 1) * 128],
                                                         qp[r0:r0 + 64, a:b], start=True, stop=True)
                                            S.inc(t.last_inst(), "pe")

                                        @blk.scalar
                                        def _(sc, kc=kc, pos=pos, qhi=qhi, par=par, lh=lh, sl=sl):
                                            sc = _Chain(sc, S, "acx")
                                            S.wait(sc, "pe")
                                            if sl >= 2:
                                                sc.wait_ge(S.h["dvx"], maskm[sl - 2])
                                            sc.activation(out=pt[:, par, 0:qhi - pos],
                                                          in_=psS[par][:, 0:qhi - pos], func=AF.Exp,
                                                          bias=mneg2[:, lh:lh + 1])
                                        expm.append(S.n["acx"])

                                        if pos == qlo:
                                            @blk.vector
                                            def _(v, par=par, sl=sl):
                                                v = _Chain(v, S, "dvx")
                                                v.wait_ge(S.h["acx"], expm[sl])
                                                v.tensor_tensor(pt[:, par, 0:128], pt[:, par, 0:128],
                                                                triD[:, :], OP.mult)
                                        maskm.append(S.n["dvx"])

                                        @blk.tensor
                                        def _(t, kc=kc, pos=pos, qhi=qhi, par=par, h=h, sl=sl):
                                            t = _TTrack(t)
                                            t.wait_ge(S.h["acx"], expm[sl])
                                            if pos == kc * 128:
                                                t.wait_ge(S.h["dvx"], maskm[sl])
                                            for (a, b) in n512a(pos, qhi):
                                                t.matmul(psAV[:, a:b], vtok[:, kc, h * 65:h * 65 + 65],
                                                         pt[:, par, a - pos:b - pos],
                                                         start=(kc == 0), stop=(kc == NT - 1),
                                                         skip_group_check=True)
                                            S.inc(t.last_inst(), "pe")
                                        pos = qhi
                                        sl += 1

                                @blk.scalar
                                def _(sc):
                                    sc = _Chain(sc, S, "acx")
                                    S.wait(sc, "pe")
                                    sc.activation(out=lex[64:65, :], in_=psAV[64:65, :], func=AF.Ln)
                                    sc.activation(out=lex[64:65, :], in_=lex[64:65, :], func=AF.Exp, scale=-1.0)

                                @blk.vector
                                def _(v):
                                    v = _Chain(v, S, "dvx")
                                    S.wait(v, "acx")
                                    v.tensor_copy(attn_un[:, :], psAV[0:64, :])

                                @blk.tensor
                                def _(t):
                                    t = _TTrack(t)
                                    S.wait(t, "dvx")
                                    for j in range(2):
                                        for (a, b) in n512(j * 1024, (j + 1) * 1024):
                                            t.matmul(psS[j][:, a - j * 1024:b - j * 1024],
                                                     ones_r64[64:65, :], lex[64:65, a:b],
                                                     start=True, stop=True)
                                    S.inc(t.last_inst(), "pe")

                                @blk.vector
                                def _(v, h=h):
                                    v = _Chain(v, S, "dvx")
                                    S.wait(v, "pe")
                                    for j in range(2):
                                        v.tensor_tensor(attnTn[(h % 2) * 64:(h % 2) * 64 + 64, h // 2,
                                                                j * 1024:(j + 1) * 1024],
                                                        attn_un[:, j * 1024:(j + 1) * 1024],
                                                        psS[j][0:64, 0:1024], OP.mult)

            if upto == "attn":
                _fin()
                return nc
            with ExitStack() as ph:
                _e = ph.enter_context
                wpj = _e(nc.sbuf_tensor("wpj", [128, NCH, C], BF16))
                bpj_col = _e(nc.sbuf_tensor("bpj_col", [128, NCH], F32))
                pjf = _e(nc.sbuf_tensor("pjf", [128, 2, T], F32))
                psP = _e(nc.psum_tensor("psP", [128, 2048], F32))
                @blk.gpsimd
                def _(g):
                    g = _Chain(g, S, "gcx")
                    S.wait(g, "pe")
                    S.wait(g, "dvx")
                    S.wait(g, "acx")
                    S.pre(g, "gp")
                    S.dma(g.dma_start(out=wpj[:, :, :],
                                      in_=w_proj.rearrange("(c p) f -> p c f", p=128)), "gp")

                @blk.sync
                def _(sy):
                    sy.wait_ge(S.h["pe"], S.n["pe"])
                    sy.wait_ge(S.h["dvx"], S.n["dvx"])
                    S.pre(sy, "ld")
                    S.dma(sy.dma_start(out=bpj_col[:, :], in_=bproj_col))
                mark_pj = (S.n["gp"], S.n["ld"])

                gp_prev = [None, None]
                for ct in range(NCH):
                    @blk.tensor
                    def _(t, ct=ct):
                        t = _TTrack(t)
                        if ct == 0:
                            t.wait_ge(S.h["gp"], mark_pj[0])
                        S.wait(t, "dvx")
                        if ct >= 2:
                            t.wait_ge(S.h[f"ga{ct % 2}"], gp_prev[ct % 2])
                        for cc in range(NCH):
                            for (a, b) in n512(0, T):
                                t.matmul(psP[:, a:b], wpj[:, cc, ct * 128:(ct + 1) * 128],
                                         attnTn[:, cc, a:b], start=(cc == 0), stop=(cc == NCH - 1))
                        S.inc(t.last_inst(), "pe")

                    @blk.vector
                    def _(v, ct=ct):
                        v = _Chain(v, S, "dvx")
                        S.wait(v, "pe")
                        v.wait_ge(S.h["ld"], mark_pj[1])
                        v.tensor_scalar(pjf[:, ct % 2, :], psP[:, :], bpj_col[:, ct:ct + 1], None, OP.add)

                    @blk.gpsimd
                    def _(g, ct=ct):
                        g = _Chain(g, S, "gcx")
                        S.wait(g, "dvx")
                        if ct == 0:
                            g.wait_ge(S.h["sp0"], S.n["sp0"])
                            g.wait_ge(S.h["sp1"], S.n["sp1"])
                        S.dma(g.dma_start(out=partial[ct * 128:(ct + 1) * 128, :], in_=pjf[:, ct % 2, :],
                                          accum_op=OP.add), f"ga{ct % 2}")
                        gp_prev[ct % 2] = S.n[f"ga{ct % 2}"]

            if upto == "proj":
                _fin()
                return nc
            with ExitStack() as ph:
                _e = ph.enter_context
                swg_sb = _e(nc.sbuf_tensor("swg_sb", [128, NCH, IDIM], BF16))
                swu_sb = _e(nc.sbuf_tensor("swu_sb", [128, NCH, IDIM], BF16))
                swd_sb = _e(nc.sbuf_tensor("swd_sb", [128, 4, C], BF16))
                gb = _e(nc.sbuf_tensor("gb", [128, T], F32))
                g_row = _e(nc.sbuf_tensor("g_row", [1, T], F32))
                hactT = _e(nc.sbuf_tensor("hactT", [128, 4, T], BF16))
                pgf = _e(nc.sbuf_tensor("pgf", [128, T], F32))
                sgf = _e(nc.sbuf_tensor("sgf", [128, T], F32))
                dnf = _e(nc.sbuf_tensor("dnf", [128, 2, T], F32))
                psG = _e(nc.psum_tensor("psG", [128, 2048], F32))
                psU = _e(nc.psum_tensor("psU", [128, 2048], F32))

                @blk.gpsimd
                def _(g):
                    g = _Chain(g, S, "gcx")
                    S.wait(g, "pe")
                    S.wait(g, "dvx")
                    S.wait(g, "ga0")
                    S.wait(g, "ga1")
                    S.pre(g, "gp")
                    S.dma(g.dma_start(out=swg_sb[:, :, :],
                                      in_=sw_gate.rearrange("(c p) i -> p c i", p=128)), "gp")
                    S.pre(g, "gp")
                    S.dma(g.dma_start(out=swu_sb[:, :, :],
                                      in_=sw_up.rearrange("(c p) i -> p c i", p=128)), "gp")
                    S.pre(g, "gp")
                    S.dma(g.dma_start(out=swd_sb[:, :, :],
                                      in_=sw_down.rearrange("(c p) i -> p c i", p=128)), "gp")
                gp_sw = S.n["gp"]

                @blk.vector
                def _(v):
                    v = _Chain(v, S, "dvx")
                    v.wait_ge(S.h["gp"], gp_sw)
                    for c_ in range(NCH):
                        v.tensor_scalar(swg_sb[:, c_, :], swg_sb[:, c_, :], rmsw_col[:, c_:c_ + 1], None, OP.mult)
                        v.tensor_scalar(swu_sb[:, c_, :], swu_sb[:, c_, :], rmsw_col[:, c_:c_ + 1], None, OP.mult)

                @blk.tensor
                def _(t):
                    t = _TTrack(t)
                    S.wait(t, "dvx")
                    for tt in range(NT):
                        t.transpose(psG[0:1, tt * 128:(tt + 1) * 128], g_cols[:, tt:tt + 1], iden[:, :])
                    S.inc(t.last_inst(), "pe")

                @blk.vector
                def _(v):
                    v = _Chain(v, S, "dvx")
                    S.wait(v, "pe")
                    v.tensor_copy(g_row[:, :], psG[0:1, 0:T])

                @blk.tensor
                def _(t):
                    t = _TTrack(t)
                    S.wait(t, "dvx")
                    for (a, b) in n512(0, T):
                        t.matmul(psG[:, a:b], ones_col[:, :], g_row[:, a:b], start=True, stop=True)
                    S.inc(t.last_inst(), "pe")

                @blk.vector
                def _(v):
                    v = _Chain(v, S, "dvx")
                    S.wait(v, "pe")
                    v.tensor_copy(gb[:, :], psG[:, :])
                dv_gb = S.n["dvx"]

                for it in range(4):
                    @blk.tensor
                    def _(t, it=it):
                        t = _TTrack(t)
                        t.wait_ge(S.h["dvx"], dv_gb)
                        if it >= 1:
                            S.wait(t, "dvx")
                        for cc in range(NCH):
                            for (a, b) in n512(0, T):
                                t.matmul(psG[:, a:b], swg_sb[:, cc, it * 128:(it + 1) * 128],
                                         xnT[:, cc, a:b], start=(cc == 0), stop=(cc == NCH - 1))
                        for cc in range(NCH):
                            for (a, b) in n512(0, T):
                                t.matmul(psU[:, a:b], swu_sb[:, cc, it * 128:(it + 1) * 128],
                                         xnT[:, cc, a:b], start=(cc == 0), stop=(cc == NCH - 1))
                        S.inc(t.last_inst(), "pe")

                    @blk.vector
                    def _(v, it=it):
                        v = _Chain(v, S, "dvx")
                        S.wait(v, "pe")
                        v.tensor_tensor(pgf[:, :], psG[:, :], gb[:, :], OP.mult)

                    @blk.scalar
                    def _(sc, it=it):
                        sc = _Chain(sc, S, "acx")
                        S.wait(sc, "dvx")
                        sc.activation(out=sgf[:, :], in_=pgf[:, :], func=AF.Silu)

                    @blk.vector
                    def _(v, it=it):
                        v = _Chain(v, S, "dvx")
                        S.wait(v, "acx")
                        v.tensor_tensor(pgf[:, :], psU[:, :], gb[:, :], OP.mult)
                        v.tensor_tensor(hactT[:, it, :], sgf[:, :], pgf[:, :], OP.mult)

                gp_dn = [None, None]
                for hc in range(NCH):
                    @blk.tensor
                    def _(t, hc=hc):
                        t = _TTrack(t)
                        S.wait(t, "dvx")
                        if hc >= 2:
                            t.wait_ge(S.h[f"ga{hc % 2}"], gp_dn[hc % 2])
                        for ic in range(4):
                            for (a, b) in n512(0, T):
                                t.matmul(psG[:, a:b], swd_sb[:, ic, hc * 128:(hc + 1) * 128],
                                         hactT[:, ic, a:b], start=(ic == 0), stop=(ic == 3))
                        S.inc(t.last_inst(), "pe")

                    @blk.vector
                    def _(v, hc=hc):
                        v = _Chain(v, S, "dvx")
                        S.wait(v, "pe")
                        v.tensor_copy(dnf[:, hc % 2, :], psG[:, :])

                    @blk.gpsimd
                    def _(g, hc=hc):
                        g = _Chain(g, S, "gcx")
                        S.wait(g, "dvx")
                        if hc == 0:
                            g.wait_ge(S.h["ga0"], S.n["ga0"])
                            g.wait_ge(S.h["ga1"], S.n["ga1"])
                        S.dma(g.dma_start(out=partial[hc * 128:(hc + 1) * 128, :], in_=dnf[:, hc % 2, :],
                                          accum_op=OP.add), f"ga{hc % 2}")
                        gp_dn[hc % 2] = S.n[f"ga{hc % 2}"]

            @blk.gpsimd
            def _(g):
                g = _Chain(g, S, "gcx")
                for nm in ("gp", "st", "sx0", "sx1", "sp0", "sp1", "ga0", "ga1"):
                    S.wait(g, nm)
                g.nop()



SEM_NAMES = ["ld", "pe", "dvx", "acx", "gp", "gcx", "st",
             "lt0", "lt1", "lc0", "sx0", "sx1", "sp0", "sp1", "ga0", "ga1",
             "sct0", "sct1", "tr0", "tr1", "wa0", "wa1", "yb0", "yb1",
             "g0", "g1", "lp0", "lp1", "so0", "so1", "tx0", "tx1", "cc"]


def _stage1_io(nc):
    io = {}
    io["xT"] = nc.dram_tensor("xT", [C, T], F32, kind="ExternalInput").ap()
    io["xtok"] = nc.dram_tensor("xtok", [T, C], F32, kind="ExternalInput").ap()
    io["w_attn"] = nc.dram_tensor("w_attn", [C, 3 * C], F32, kind="ExternalInput").ap()
    io["w_attnT"] = nc.dram_tensor("w_attnT", [3 * C, C], F32, kind="ExternalInput").ap()
    io["b_attn"] = nc.dram_tensor("b_attn", [1, 3 * C], F32, kind="ExternalInput").ap()
    io["w_proj"] = nc.dram_tensor("w_proj", [C, C], F32, kind="ExternalInput").ap()
    io["b_proj"] = nc.dram_tensor("b_proj", [1, C], F32, kind="ExternalInput").ap()
    io["bproj_col"] = nc.dram_tensor("bproj_col", [128, NCH], F32, kind="ExternalInput").ap()
    io["wk_cT"] = nc.dram_tensor("wk_cT", [32, 64], F32, kind="ExternalInput").ap()
    io["wk_e"] = nc.dram_tensor("wk_e", [32, 64], F32, kind="ExternalInput").ap()
    io["wv_cT"] = nc.dram_tensor("wv_cT", [32, 64], F32, kind="ExternalInput").ap()
    io["wv_e"] = nc.dram_tensor("wv_e", [32, 64], F32, kind="ExternalInput").ap()
    io["rms_w"] = nc.dram_tensor("rms_w", [1, C], F32, kind="ExternalInput").ap()
    io["rms_w_col"] = nc.dram_tensor("rms_w_col", [128, NCH], F32, kind="ExternalInput").ap()
    io["battn_col"] = nc.dram_tensor("battn_col", [128, 24], F32, kind="ExternalInput").ap()
    io["gate_wT"] = nc.dram_tensor("gate_wT", [C, E], F32, kind="ExternalInput").ap()
    io["sw_gate"] = nc.dram_tensor("sw_gate", [C, IDIM], F32, kind="ExternalInput").ap()
    io["sw_up"] = nc.dram_tensor("sw_up", [C, IDIM], F32, kind="ExternalInput").ap()
    io["sw_down"] = nc.dram_tensor("sw_down", [IDIM, C], F32, kind="ExternalInput").ap()
    io["partial"] = nc.dram_tensor("partial", [C, T], F32, kind="ExternalOutput").ap()
    io["xm_out"] = nc.dram_tensor("xm_out", [T, C], BF16, kind="ExternalOutput").ap()
    io["meta"] = nc.dram_tensor("meta", [T, 6], F32, kind="ExternalOutput").ap()
    io["counts"] = nc.dram_tensor("counts", [1, E], F32, kind="ExternalOutput").ap()
    return io


def build_stage1(upto="all"):
    nc = bass.Bass("TRN2", target_bir_lowering=False, debug=False)
    io = _stage1_io(nc)
    st0 = ExitStack()
    with st0:
        S = Sems(st0, nc, SEM_NAMES)
        with nc.Block() as blk:
            emit_stage1(_PrefNC(nc, "s1_"), blk, S, io, upto)
    return nc


def _prep_stage1_inputs(inputs):
    f32 = np.float32
    com = {
        "w_attn": np.ascontiguousarray(inputs["L0_attn_c_attn_w"], f32),
        "w_attnT": np.ascontiguousarray(inputs["L0_attn_c_attn_w"].T, f32),
        "b_attn": np.ascontiguousarray(inputs["L0_attn_c_attn_b"].reshape(1, -1), f32),
        "w_proj": np.ascontiguousarray(inputs["L0_attn_c_proj_w"], f32),
        "b_proj": np.ascontiguousarray(inputs["L0_attn_c_proj_b"].reshape(1, -1), f32),
        "wk_cT": np.ascontiguousarray(inputs["L0_attn_wk_c"].T, f32),
        "wk_e": np.ascontiguousarray(inputs["L0_attn_wk_e"], f32),
        "wv_cT": np.ascontiguousarray(inputs["L0_attn_wv_c"].T, f32),
        "wv_e": np.ascontiguousarray(inputs["L0_attn_wv_e"], f32),
        "rms_w": np.ascontiguousarray(inputs["L0_ffn_rms_w"].reshape(1, -1), f32),
        "rms_w_col": np.ascontiguousarray(inputs["L0_ffn_rms_w"].reshape(8, 128).T, f32),
        "battn_col": np.ascontiguousarray(inputs["L0_attn_c_attn_b"].reshape(24, 128).T, f32),
        "bproj_col": np.ascontiguousarray(inputs["L0_attn_c_proj_b"].reshape(8, 128).T, f32),
        "gate_wT": np.ascontiguousarray(inputs["L0_ffn_gate_w"].T, f32),
        "sw_gate": np.ascontiguousarray(inputs["L0_ffn_sw_gate"], f32),
        "sw_up": np.ascontiguousarray(inputs["L0_ffn_sw_up"], f32),
        "sw_down": np.ascontiguousarray(inputs["L0_ffn_sw_down"], f32),
    }
    x = np.asarray(inputs["L0_attn_hidden_states"], f32)
    maps = []
    for b in range(B):
        m = dict(com)
        m["xT"] = np.ascontiguousarray(x[b].T)
        m["xtok"] = np.ascontiguousarray(x[b])
        maps.append(m)
    return maps


def run_stage1(inputs, upto="all", **kw):
    nc = build_stage1(upto)
    maps = _prep_stage1_inputs(inputs)
    res = run_bass_kernel_spmd(nc, maps, core_ids=list(range(B)), **kw)
    return res



def emit_stage2(nc, blk, S, io):
    NTOK = B * T
    NTL = NTOK // 128
    xm_all = io["xm_all"]
    meta_all = io["meta_all"]
    counts_all = io["counts_all"]
    w13 = io["w13"]
    w2w = io["w2w"]
    ebase = io["ebase"]
    yb = io["yb"]
    disp = io["disp"]
    st = ExitStack()
    with st:
        sbt = lambda *a: st.enter_context(nc.sbuf_tensor(*a))

        meta_sb = sbt("meta_sb", [128, NTL, 6], F32)
        cnt8 = sbt("cnt8", [8, E], F32)
        offs = sbt("offs", [8, E], F32)
        offb = sbt("offb", [128, 8, E], F32)
        iota64 = sbt("iota64", [128, E], F32)
        triU8 = sbt("triU8", [8, 8], F32)
        ones_col = sbt("ones_col", [1, 128], F32)
        ebase_c = sbt("ebase_c", [128, 1], F32)
        slots = sbt("slots", [128, NTL, 2], I32)
        oh = sbt("oh", [128, E], F32)
        colA = sbt("colA", [128, 8], F32)
        xmt = sbt("xmt", [128, 2, C], BF16)
        bufT = sbt("bufT", [128, 2, NCH, CAP], BF16)
        w13_sb = sbt("w13_sb", [128, 2, NCH, 2 * IDIM], BF16)
        w2_sb = sbt("w2_sb", [128, 2, 4, C], BF16)
        hactT = sbt("hactT", [128, 4, CAP], BF16)
        sgf = sbt("sgf", [128, CAP], F32)
        ybf = sbt("ybf", [128, 2, C], BF16)

        with _passthrough(blk) as blk:
            def _fin2():
                @blk.gpsimd
                def _(g):
                    for nm in ("ld", "gp", "st", "sct0", "sct1", "lt0", "lt1", "tr0", "tr1",
                               "wa0", "wa1", "yb0", "yb1"):
                        S.wait(g, nm)
                    for nm in ("pe", "dvx", "acx", "dvx", "acx", "gcx"):
                        S.wait(g, nm)
                    g.nop()

            @blk.gpsimd
            def _(g):
                g = _Chain(g, S, "gcx")
                g.memset(ones_col[:, :], 1.0)
                g.memset(triU8[:, :], 1.0)
                g.affine_select(triU8[:, :], triU8[:, :], [[1, 8]], OP.is_ge, 0.0,
                                base=-1, channel_multiplier=-1)
                g.iota(iota64[:, :], [[1, E]], base=0, channel_multiplier=0,
                       allow_small_or_imprecise_dtypes=True)

            @blk.sync
            def _(sy):
                S.pre(sy, "ld")
                S.dma(sy.dma_start(out=meta_sb[:, :, :], in_=meta_all.rearrange("(n p) c -> p n c", p=128)))
                S.pre(sy, "ld")
                S.dma(sy.dma_start(out=cnt8[:, :], in_=counts_all))
                S.pre(sy, "ld")
                S.dma(sy.dma_start(out=ebase_c[:, :], in_=ebase))
            ld_meta = S.n["ld"]

            with nc.psum_tensor("ps2a", [128, 512], F32) as psA:
                @blk.tensor
                def _(t):
                    t = _TTrack(t)
                    t.wait_ge(S.h["ld"], ld_meta)
                    S.wait(t, "gcx")
                    t.matmul(psA[0:8, 0:E], triU8[:, :], cnt8[:, :], start=True, stop=True)
                    S.inc(t.last_inst(), "pe")

                @blk.vector
                def _(v):
                    v = _Chain(v, S, "dvx")
                    S.wait(v, "pe")
                    v.tensor_copy(offs[:, :], psA[0:8, 0:E])

                for c_ in range(8):
                    @blk.gpsimd
                    def _(g, c_=c_):
                        g = _Chain(g, S, "gcx")
                        S.wait(g, "dvx")
                        S.pre(g, "gp")
                        S.dma(g.dma_start(out=colA[0:1, 0:E] if False else oh[0:1, :], in_=offs[c_:c_ + 1, :]), "gp")

                    @blk.tensor
                    def _(t, c_=c_):
                        t = _TTrack(t)
                        S.wait(t, "gp")
                        S.wait(t, "dvx")
                        t.matmul(psA[:, 0:E], ones_col[:, :], oh[0:1, :], start=True, stop=True)
                        S.inc(t.last_inst(), "pe")

                    @blk.vector
                    def _(v, c_=c_):
                        v = _Chain(v, S, "dvx")
                        S.wait(v, "pe")
                        v.tensor_copy(offb[:, c_, :], psA[:, 0:E])

            @blk.vector
            def _(v):
                v = _Chain(v, S, "dvx")
                S.wait(v, "dvx")
                for n in range(NTL):
                    cg = n // 16
                    for k in range(2):
                        v.tensor_scalar(oh[:, :], iota64[:, :], meta_sb[:, n, k:k + 1], None, OP.is_equal)
                        v.tensor_tensor(oh[:, :], oh[:, :], offb[:, cg, :], OP.mult)
                        v.reduce_sum(colA[:, 0:1], oh[:, :], AX)
                        v.tensor_tensor(colA[:, 1:2], colA[:, 0:1], meta_sb[:, n, 4 + k:5 + k], OP.add)
                        v.tensor_scalar(colA[:, 2:3], colA[:, 1:2], 1024.0, 20000.0, OP.is_ge, OP.mult)
                        v.tensor_scalar(colA[:, 3:4], meta_sb[:, n, k:k + 1], ebase_c[:, 0:1], None, OP.subtract)
                        v.tensor_scalar(colA[:, 4:5], colA[:, 3:4], 0.0, None, OP.is_ge)
                        v.tensor_scalar(colA[:, 5:6], colA[:, 3:4], 7.0, None, OP.is_le)
                        v.tensor_tensor(colA[:, 4:5], colA[:, 4:5], colA[:, 5:6], OP.mult)
                        v.tensor_scalar(colA[:, 4:5], colA[:, 4:5], -100000.0, 100000.0, OP.mult, OP.add)
                        v.tensor_scalar(colA[:, 6:7], colA[:, 3:4], 1024.0, None, OP.mult)
                        v.tensor_tensor(colA[:, 6:7], colA[:, 6:7], colA[:, 1:2], OP.add)
                        v.tensor_tensor(colA[:, 6:7], colA[:, 6:7], colA[:, 2:3], OP.add)
                        v.tensor_tensor(colA[:, 6:7], colA[:, 6:7], colA[:, 4:5], OP.add)
                        v.tensor_copy(slots[:, n, k:k + 1], colA[:, 6:7])
            dv_slots = S.n["dvx"]

            zt = st.enter_context(nc.sbuf_tensor("zt", [128, C], BF16))

            @blk.gpsimd
            def _(g):
                g = _Chain(g, S, "gcx")
                g.memset(zt[:, :], 0.0)
                S.inc(g.drain(), "gc2" if False else "gcx")

            @blk.sync
            def _(sy):
                S.wait(sy, "gcx")
                S.dma(sy.dma_start(out=disp.rearrange("(j p) c -> p j c", p=128),
                                   in_=zt[:, :].rearrange("p (j c) -> p j c", j=1).to_broadcast([128, 64, C])
                                   if hasattr(zt[:, :], "to_broadcast") else zt[:, :]), "st")

            bc_reg = [None]

            @blk.gpsimd
            def _(g):
                r = g.alloc_register()
                g.reg_mov(r, 8 * CAP - 1)
                bc_reg[0] = r

            sct_marks = []
            for n in range(NTL):
                @blk.sync
                def _(sy, n=n):
                    if n >= 2:
                        sy.wait_ge(S.h[f"sct{n % 2}"], sct_marks[n - 2])
                    S.dma(sy.dma_start(out=xmt[:, n % 2, :],
                                       in_=xm_all.rearrange("(n p) c -> p n c", p=128)[:, n, :]),
                          f"lt{n % 2}")

                @blk.gpsimd
                def _(g, n=n):
                    g.wait_ge(S.h[f"lt{n % 2}"], S.n[f"lt{n % 2}"])
                    if n == 0:
                        g.wait_ge(S.h["dvx"], dv_slots)
                        S.wait(g, "st")
                    for k in range(2):
                        S.dma(g.indirect_dma_start(
                            out=disp, out_offset=IndirectOffsetOnAxis(slots[:, n, k:k + 1], axis=0),
                            in_=xmt[:, n % 2, :], in_offset=None,
                            bounds_check=bc_reg[0], oob_is_err=False), f"sct{n % 2}")
                    sct_marks.append(S.n[f"sct{n % 2}"])

            disp_r = disp.rearrange("(e r) c -> e r c", e=8)
            for e in range(8):
                par = e % 2

                @blk.sync
                def _(sy, e=e, par=par):
                    sy.wait_ge(S.h["sct0"], S.n["sct0"])
                    sy.wait_ge(S.h["sct1"], S.n["sct1"])
                    if e >= 2:
                        S.wait(sy, "pe")
                    for cc in range(NCH):
                        S.pre(sy, f"tr{par}")
                        S.dma(sy.dma_start(out=bufT[:, par, cc, :],
                                           in_=disp_r[e, :, cc * 128:(cc + 1) * 128],
                                           transpose=True), f"tr{par}")

                @blk.gpsimd
                def _(g, e=e, par=par):
                    if e >= 2:
                        S.wait(g, "pe")
                    S.pre(g, f"wa{par}")
                    S.dma(g.dma_start(out=w13_sb[:, par, :, :],
                                      in_=w13[e, :, :].rearrange("(c p) f -> p c f", p=128)), f"wa{par}")
                    S.pre(g, f"wa{par}")
                    S.dma(g.dma_start(out=w2_sb[:, par, :, :],
                                      in_=w2w[e, :, :].rearrange("(c p) f -> p c f", p=128)), f"wa{par}")

                with nc.psum_tensor(f"psG{e}", [128, 1024], F32) as psG, \
                     nc.psum_tensor(f"psU{e}", [128, 1024], F32) as psU, \
                     nc.psum_tensor(f"psY{e}", [128, 1024], F32) as psY:
                    for it in range(4):
                        @blk.tensor
                        def _(t, e=e, par=par, it=it):
                            t = _TTrack(t)
                            t.wait_ge(S.h[f"tr{par}"], S.n[f"tr{par}"])
                            t.wait_ge(S.h[f"wa{par}"], S.n[f"wa{par}"])
                            S.wait(t, "dvx")
                            S.wait(t, "acx")
                            for cc in range(NCH):
                                for (a, b) in n512(0, CAP):
                                    t.matmul(psG[:, a:b], w13_sb[:, par, cc, it * 128:(it + 1) * 128],
                                             bufT[:, par, cc, a:b], start=(cc == 0), stop=(cc == NCH - 1))
                            for cc in range(NCH):
                                for (a, b) in n512(0, CAP):
                                    t.matmul(psU[:, a:b], w13_sb[:, par, cc, IDIM + it * 128:IDIM + (it + 1) * 128],
                                             bufT[:, par, cc, a:b], start=(cc == 0), stop=(cc == NCH - 1))
                            S.inc(t.last_inst(), "pe")

                        @blk.scalar
                        def _(sc, it=it):
                            sc = _Chain(sc, S, "acx")
                            S.wait(sc, "pe")
                            sc.activation(out=sgf[:, :], in_=psG[:, :], func=AF.Silu)

                        @blk.vector
                        def _(v, it=it):
                            v = _Chain(v, S, "dvx")
                            S.wait(v, "acx")
                            v.tensor_tensor(hactT[:, it, :], sgf[:, :], psU[:, :], OP.mult)

                    for stt in range(8):
                        @blk.tensor
                        def _(t, e=e, par=par, stt=stt):
                            t = _TTrack(t)
                            S.wait(t, "dvx")
                            if stt >= 2:
                                S.wait(t, "dvx")
                            for ic in range(4):
                                for (a, b) in n512(0, C):
                                    t.matmul(psY[:, a:b], hactT[:, ic, stt * 128:(stt + 1) * 128],
                                             w2_sb[:, par, ic, a:b], start=(ic == 0), stop=(ic == 3))
                            S.inc(t.last_inst(), "pe")

                        @blk.vector
                        def _(v, e=e, stt=stt):
                            v = _Chain(v, S, "dvx")
                            S.wait(v, "pe")
                            v.wait_ge(S.h[f"yb{stt % 2}"], S.n[f"yb{stt % 2}"])
                            v.tensor_copy(ybf[:, stt % 2, :], psY[:, :])

                        @blk.sync
                        def _(sy, e=e, stt=stt):
                            S.wait(sy, "dvx")
                            S.dma(sy.dma_start(out=yb[e * CAP + stt * 128:e * CAP + (stt + 1) * 128, :],
                                               in_=ybf[:, stt % 2, :]), f"yb{stt % 2}")

            _fin2()



def _stage2_io(nc):
    io = {}
    io["xm_all"] = nc.dram_tensor("xm_all", [B * T, C], BF16, kind="ExternalInput").ap()
    io["meta_all"] = nc.dram_tensor("meta_all", [B * T, 6], F32, kind="ExternalInput").ap()
    io["counts_all"] = nc.dram_tensor("counts_all", [8, E], F32, kind="ExternalInput").ap()
    io["w13"] = nc.dram_tensor("w13", [8, C, 2 * IDIM], F32, kind="ExternalInput").ap()
    io["w2w"] = nc.dram_tensor("w2w", [8, IDIM, C], F32, kind="ExternalInput").ap()
    io["ebase"] = nc.dram_tensor("ebase", [128, 1], F32, kind="ExternalInput").ap()
    io["yb"] = nc.dram_tensor("yb", [8 * CAP, C], BF16, kind="ExternalOutput").ap()
    io["disp"] = nc.dram_tensor("disp", [8 * CAP, C], BF16, kind="Internal").ap()
    return io


def build_stage2():
    nc = bass.Bass("TRN2", target_bir_lowering=False, debug=False)
    io = _stage2_io(nc)
    st0 = ExitStack()
    with st0:
        S = Sems(st0, nc, SEM_NAMES)
        with nc.Block() as blk:
            emit_stage2(_PrefNC(nc, "s2_"), blk, S, io)
    return nc


def _prep_stage2_inputs(inputs, s1res):
    f32 = np.float32
    xm_all = np.concatenate([r["xm_out"] for r in s1res], axis=0)
    meta_all = np.concatenate([r["meta"] for r in s1res], axis=0).astype(f32)
    counts_all = np.concatenate([r["counts"] for r in s1res], axis=0).astype(f32)
    w13 = np.ascontiguousarray(inputs["L0_ffn_w13"], f32)
    w2w = np.ascontiguousarray(inputs["L0_ffn_w2"], f32)
    maps = []
    for i in range(8):
        maps.append({
            "xm_all": xm_all, "meta_all": meta_all, "counts_all": counts_all,
            "w13": np.ascontiguousarray(w13[8 * i:8 * i + 8]),
            "w2w": np.ascontiguousarray(w2w[8 * i:8 * i + 8]),
            "ebase": np.full((128, 1), 8.0 * i, f32),
        })
    return maps


def run_stage2(inputs, s1res, **kw):
    nc = build_stage2()
    maps = _prep_stage2_inputs(inputs, s1res)
    return run_bass_kernel_spmd(nc, maps, core_ids=list(range(8)), **kw)



def emit_stage3(nc, blk, S, io):
    partialT = io["partialT"]
    yb_all = io["yb_all"]
    meta_in = io["meta_in"]
    counts_all = io["counts_all"]
    bsel = io["bsel"]
    wqT = io["wqT"]
    wkT = io["wkT"]
    wv = io["wv"]
    wo = io["wo"]
    bo = io["bo"]
    wf2 = io["wf2"]
    bfq_col = io["bfq_col"]
    bf_row = io["bf_row"]
    g1r_R = io["g1r_R"]
    g2r_R = io["g2r_R"]
    g1r_I = io["g1r_I"]
    g2r_I = io["g2r_I"]
    bri_row = io["bri_row"]
    out = io["out"]
    a2R = io["a2R"]
    a2I = io["a2I"]
    st = ExitStack()
    with st:
        sbt = lambda *a: st.enter_context(nc.sbuf_tensor(*a))

        x1 = sbt("x1", [128, NT, C], F32)
        xn2T = sbt("xn2T", [128, NCH, T], BF16)
        ones_col = sbt("ones_col", [1, 128], F32)
        ones_r64 = sbt("ones_r64", [128, 128], F32)
        iden = sbt("iden", [128, 128], F32)
        iota64 = sbt("iota64", [128, E], F32)
        triU8 = sbt("triU8", [8, 8], F32)
        offb = sbt("offb", [128, E], F32)
        m2c = sbt("m2c", [128, NT], F32)
        r2c = sbt("r2c", [128, NT], F32)
        meta_sb = sbt("meta_sb", [128, NT, 6], F32)
        slots = sbt("slots", [128, NT, 2], I32)
        colA = sbt("colA", [128, 8], F32)
        oh = sbt("oh", [128, E], F32)
        kv_sb = sbt("kv_sb", [128, NH, 65], BF16)
        bo_b = sbt("bo_b", [128, C], F32)
        bf_b = sbt("bf_b", [128, C], F32)
        bri_b = sbt("bri_b", [128, C], F32)
        lex = sbt("lex", [128, T], F32)
        bfq_sb = sbt("bfq_sb", [128, 8], F32)

        with _passthrough(blk) as blk:
            def _fin3():
                @blk.gpsimd
                def _(g):
                    for nm in ("ld", "gp", "st", "g0", "g1", "lp0", "lp1",
                               "so0", "so1", "tx0", "tx1"):
                        S.wait(g, nm)
                    for nm in ("pe", "dvx", "acx", "dvx", "acx", "gcx"):
                        S.wait(g, nm)
                    g.nop()

            @blk.gpsimd
            def _(g):
                g = _Chain(g, S, "gcx")
                g.memset(ones_col[:, :], 1.0)
                g.memset(ones_r64[:, :], 0.0)
                g.memset(ones_r64[64:65, :], 1.0)
                g.memset(iden[:, :], 1.0)
                g.affine_select(iden[:, :], iden[:, :], [[-1, 128]], OP.is_equal, 0.0,
                                base=0, channel_multiplier=1)
                g.iota(iota64[:, :], [[1, E]], base=0, channel_multiplier=0,
                       allow_small_or_imprecise_dtypes=True)
                g.memset(triU8[:, :], 1.0)
                g.affine_select(triU8[:, :], triU8[:, :], [[1, 8]], OP.is_ge, 0.0,
                                base=-1, channel_multiplier=-1)

            @blk.sync
            def _(sy):
                for (dst, srcap) in ((meta_sb[:, :, :], meta_in.rearrange("(n p) c -> p n c", p=128)),
                                     (colA[0:8, 0:1] , bsel), (oh[0:8, :], counts_all),
                                     (bo_b[0:1, :], bo), (bf_b[0:1, :], bf_row),
                                     (bri_b[0:1, :], bri_row),
                                     (bfq_sb[:, :], bfq_col)):
                    S.pre(sy, "ld")
                    S.dma(sy.dma_start(out=dst, in_=srcap), "ld")
            ld0 = S.n["ld"]

            with nc.psum_tensor("ps3o", [128, 1024], F32) as psO:
                @blk.tensor
                def _(t):
                    t = _TTrack(t)
                    t.wait_ge(S.h["ld"], ld0)
                    S.wait(t, "gcx")
                    t.matmul(psO[0:8, 0:E], triU8[:, :], oh[0:8, :], start=True, stop=True)
                    S.inc(t.last_inst(), "pe")

                @blk.vector
                def _(v):
                    v = _Chain(v, S, "dvx")
                    S.wait(v, "pe")
                    v.tensor_copy(oh[0:8, :], psO[0:8, 0:E])

                @blk.tensor
                def _(t):
                    t = _TTrack(t)
                    S.wait(t, "dvx")
                    t.matmul(psO[0:1, 0:E], colA[0:8, 0:1], oh[0:8, :], start=True, stop=True)
                    S.inc(t.last_inst(), "pe")

                @blk.vector
                def _(v):
                    v = _Chain(v, S, "dvx")
                    S.wait(v, "pe")
                    v.tensor_copy(oh[0:1, :], psO[0:1, 0:E])

                @blk.tensor
                def _(t):
                    t = _TTrack(t)
                    S.wait(t, "dvx")
                    t.matmul(psO[:, 0:E], ones_col[:, :], oh[0:1, :], start=True, stop=True)
                    for (a, b) in n512(0, C):
                        t.matmul(psO[:, 512:1024] if False else psO[:, a:b], ones_col[:, :], bo_b[0:1, a:b],
                                 start=True, stop=True) if False else None
                    S.inc(t.last_inst(), "pe")

                @blk.vector
                def _(v):
                    v = _Chain(v, S, "dvx")
                    S.wait(v, "pe")
                    v.tensor_copy(offb[:, :], psO[:, 0:E])

                for (rowt, bt) in ((bo_b, bo_b), (bf_b, bf_b), (bri_b, bri_b)):
                    @blk.tensor
                    def _(t, rowt=rowt):
                        t = _TTrack(t)
                        S.wait(t, "dvx")
                        for (a, b) in n512(0, C):
                            t.matmul(psO[:, a:b], ones_col[:, :], rowt[0:1, a:b], start=True, stop=True)
                        S.inc(t.last_inst(), "pe")

                    @blk.vector
                    def _(v, bt=bt):
                        v = _Chain(v, S, "dvx")
                        S.wait(v, "pe")
                        v.tensor_copy(bt[:, :], psO[:, 0:C])

            @blk.vector
            def _(v):
                v = _Chain(v, S, "dvx")
                S.wait(v, "dvx")
                for n in range(NT):
                    for k in range(2):
                        v.tensor_scalar(oh[:, :], iota64[:, :], meta_sb[:, n, k:k + 1], None, OP.is_equal)
                        v.tensor_tensor(oh[:, :], oh[:, :], offb[:, :], OP.mult)
                        v.reduce_sum(colA[:, 0:1], oh[:, :], AX)
                        v.tensor_tensor(colA[:, 1:2], colA[:, 0:1], meta_sb[:, n, 4 + k:5 + k], OP.add)
                        v.tensor_scalar(colA[:, 2:3], colA[:, 1:2], 1024.0, 70000.0, OP.is_ge, OP.mult)
                        v.tensor_scalar(colA[:, 3:4], meta_sb[:, n, k:k + 1], 1024.0, None, OP.mult)
                        v.tensor_tensor(colA[:, 3:4], colA[:, 3:4], colA[:, 1:2], OP.add)
                        v.tensor_tensor(colA[:, 3:4], colA[:, 3:4], colA[:, 2:3], OP.add)
                        v.tensor_copy(slots[:, n, k:k + 1], colA[:, 3:4])
            dv_slots3 = S.n["dvx"]

            with ExitStack() as ph:
                _e = ph.enter_context
                ptt = _e(nc.sbuf_tensor("ptt", [128, 2, T], F32))
                yg = _e(nc.sbuf_tensor("yg", [128, 2, 2, C], BF16))
                psT3 = _e(nc.psum_tensor("ps3t", [128, 2048], F32))
                bc3 = [None]

                @blk.gpsimd
                def _(g):
                    r = g.alloc_register()
                    g.reg_mov(r, E * CAP - 1)
                    bc3[0] = r

                tmarks = {}
                for cc in range(NCH):
                    par = cc % 2

                    @blk.sync
                    def _(sy, cc=cc, par=par):
                        if cc >= 2:
                            S.wait(sy, "pe", tmarks[cc - 2])
                        S.pre(sy, f"lp{par}")
                        S.dma(sy.dma_start(out=ptt[:, par, :],
                                           in_=partialT[cc * 128:(cc + 1) * 128, :]), f"lp{par}")

                    @blk.tensor
                    def _(t, cc=cc, par=par):
                        t = _TTrack(t)
                        t.wait_ge(S.h[f"lp{par}"], S.n[f"lp{par}"])
                        if cc == 0:
                            S.wait(t, "dvx")
                        if cc >= 1:
                            S.wait(t, "dvx")
                        for n in range(NT):
                            t.transpose(psT3[:, n * 128:(n + 1) * 128],
                                        ptt[:, par, n * 128:(n + 1) * 128], iden[:, :])
                        S.inc(t.last_inst(), "pe")
                        tmarks[cc] = S.n["pe"]

                    @blk.vector
                    def _(v, cc=cc):
                        v = _Chain(v, S, "dvx")
                        S.wait(v, "pe")
                        for n in range(NT):
                            v.tensor_copy(x1[:, n, cc * 128:(cc + 1) * 128],
                                          psT3[:, n * 128:(n + 1) * 128])

                gmarks = []
                for n in range(NT):
                    par = n % 2

                    @blk.gpsimd
                    def _(g, n=n, par=par):
                        g = _Chain(g, S, "gcx")
                        if n == 0:
                            g.wait_ge(S.h["dvx"], dv_slots3)
                            S.wait(g, "dvx")
                        if n >= 2:
                            S.wait(g, "dvx", gmarks[n - 2])
                        g.memset(yg[:, par, 0, :], 0.0)
                        g.memset(yg[:, par, 1, :], 0.0)
                        gd = g.drain()
                        S.inc(gd, "gcx")
                        g.wait_ge(S.h["gcx"], S.n["gcx"])
                        for k in range(2):
                            S.dma(g.indirect_dma_start(
                                out=yg[:, par, k, :], out_offset=None,
                                in_=yb_all,
                                in_offset=IndirectOffsetOnAxis(slots[:, n, k:k + 1], axis=0),
                                bounds_check=bc3[0], oob_is_err=False), f"g{par}")

                    @blk.vector
                    def _(v, n=n, par=par):
                        v = _Chain(v, S, "dvx")
                        v.wait_ge(S.h[f"g{par}"], S.n[f"g{par}"])
                        v.tensor_scalar(lex[:, 0:C], yg[:, par, 0, :], meta_sb[:, n, 2:3], None, OP.mult)
                        v.scalar_tensor_tensor(lex[:, C:2 * C] if False else lex[:, 0:C],
                                               yg[:, par, 1, :], meta_sb[:, n, 3:4],
                                               lex[:, 0:C], OP.mult, OP.add)
                        v.tensor_tensor(x1[:, n, :], x1[:, n, :], lex[:, 0:C], OP.add)
                        gmarks.append(S.n["dvx"])

            with ExitStack() as ph:
                _e = ph.enter_context
                sq3 = _e(nc.sbuf_tensor("sq3", [128, C], F32))
                ssq3 = _e(nc.sbuf_tensor("ssq3", [128, NT], F32))
                xnt3 = _e(nc.sbuf_tensor("xnt3", [128, 2, C], BF16))

                for n in range(NT):
                    par = n % 2

                    @blk.scalar
                    def _(sc, n=n):
                        sc = _Chain(sc, S, "acx")
                        if n == 0:
                            S.wait(sc, "dvx")
                        sc.activation(out=sq3[:, :], in_=x1[:, n, :], func=AF.Square,
                                      accum_out=ssq3[:, n:n + 1])

                    @blk.vector
                    def _(v, n=n, par=par):
                        v = _Chain(v, S, "dvx")
                        S.wait(v, "acx")
                        if n >= 2:
                            v.wait_ge(S.h[f"tx{par}"], S.n[f"tx{par}"])
                        c0 = slice(n, n + 1)
                        v.reduce_sum(m2c[:, c0], x1[:, n, :], AX)
                        v.tensor_scalar(m2c[:, c0], m2c[:, c0], 1.0 / C, None, OP.mult)
                        v.tensor_scalar(ssq3[:, c0], ssq3[:, c0], 1.0 / C, None, OP.mult)
                        v.tensor_tensor(r2c[:, c0], m2c[:, c0], m2c[:, c0], OP.mult)
                        v.tensor_tensor(ssq3[:, c0], ssq3[:, c0], r2c[:, c0], OP.subtract)
                        v.tensor_scalar(r2c[:, c0], ssq3[:, c0], EPS_LN, None, OP.add)

                    @blk.scalar
                    def _(sc, n=n):
                        sc = _Chain(sc, S, "acx")
                        S.wait(sc, "dvx")
                        c0 = slice(n, n + 1)
                        sc.activation(out=r2c[:, c0], in_=r2c[:, c0], func=AF.Ln)
                        sc.activation(out=r2c[:, c0], in_=r2c[:, c0], func=AF.Exp, scale=-0.5)

                    @blk.vector
                    def _(v, n=n, par=par):
                        v = _Chain(v, S, "dvx")
                        S.wait(v, "acx")
                        c0 = slice(n, n + 1)
                        v.tensor_scalar(xnt3[:, par, :], x1[:, n, :], m2c[:, c0], r2c[:, c0],
                                        OP.subtract, OP.mult)

                    @blk.sync
                    def _(sy, n=n, par=par):
                        S.wait(sy, "dvx")
                        for cc in range(NCH):
                            S.pre(sy, f"tx{par}")
                            S.dma(sy.dma_start(out=xn2T[:, cc, n * 128:(n + 1) * 128],
                                               in_=xnt3[:, par, cc * 128:(cc + 1) * 128],
                                               transpose=True), f"tx{par}")

            def fold_wf(wsrc, wdst, wfd, psF):
                @blk.sync
                def _(sy):
                    sy.wait_ge(S.h["pe"], S.n["pe"])
                    sy.wait_ge(S.h["dvx"], S.n["dvx"])
                    S.pre(sy, "ld")
                    S.dma(sy.dma_start(out=wtf[:, :, :],
                                       in_=wsrc.rearrange("(c p) q -> p c q", p=128)), "ld")
                ldw = S.n["ld"]

                for ct in range(NCH):
                    @blk.tensor
                    def _(t, ct=ct, ldw=ldw):
                        t = _TTrack(t)
                        t.wait_ge(S.h["ld"], ldw)
                        S.wait(t, "dvx")
                        for h in [x for x in range(NH) if x % 2 == 0] + \
                                 [x for x in range(NH) if x % 2 == 1]:
                            t.matmul(psF[:, h * 64:(h + 1) * 64],
                                     wtf[(h % 2) * 64:(h % 2) * 64 + 64, h // 2, ct * 128:(ct + 1) * 128],
                                     wfd[(h % 2) * 64:(h % 2) * 64 + 64, 0:64], start=True, stop=True)
                        S.inc(t.last_inst(), "pe")

                    @blk.vector
                    def _(v, ct=ct):
                        v = _Chain(v, S, "dvx")
                        S.wait(v, "pe")
                        v.tensor_copy(wdst[:, ct, :], psF[:, 0:C])

            wfd = st.enter_context(nc.sbuf_tensor("wfd", [128, 64], F32))
            wkf_sb = None

            @blk.sync
            def _(sy):
                S.pre(sy, "ld")
                S.dma(sy.dma_start(out=wfd[:, :], in_=wf2), "ld")

            with ExitStack() as phK:
                _eK = phK.enter_context
                kf = _eK(nc.sbuf_tensor("kf", [128, NT, C], BF16))
                psK = _eK(nc.psum_tensor("ps3k", [128, 1024], F32))

                with ExitStack() as phW:
                    wtf = phW.enter_context(nc.sbuf_tensor("wtf", [128, NCH, C], F32))
                    wkf_sb = phW.enter_context(nc.sbuf_tensor("wkf_sb", [128, NCH, C], BF16))
                    fold_wf(wkT, wkf_sb, wfd, psK)

                    for n in range(NT):
                        @blk.tensor
                        def _(t, n=n):
                            t = _TTrack(t)
                            S.wait(t, "dvx")
                            for cc in range(NCH):
                                for (a, b) in n512(0, C):
                                    t.matmul(psK[:, a:b], xn2T[:, cc, n * 128:(n + 1) * 128],
                                             wkf_sb[:, cc, a:b], start=(cc == 0), stop=(cc == NCH - 1))
                            S.inc(t.last_inst(), "pe")

                        @blk.vector
                        def _(v, n=n):
                            v = _Chain(v, S, "dvx")
                            S.wait(v, "pe")
                            if n == 0:
                                v.wait_ge(S.h["tx0"], S.n["tx0"])
                                v.wait_ge(S.h["tx1"], S.n["tx1"])
                            v.tensor_tensor(lex[:, 0:C], psK[:, 0:C], bf_b[:, :], OP.add)
                            v.tensor_scalar(kf[:, n, :], lex[:, 0:C], 0.0, None, OP.max)

                with ExitStack() as phV:
                    _eV = phV.enter_context
                    v65 = _eV(nc.sbuf_tensor("v65", [128, NT, NH * 65], BF16))

                    with ExitStack() as phWv:
                        wv_sb = phWv.enter_context(nc.sbuf_tensor("wv_sb", [128, NCH, C], BF16))

                        @blk.gpsimd
                        def _(g):
                            g = _Chain(g, S, "gcx")
                            S.wait(g, "pe")
                            S.wait(g, "dvx")
                            for h in range(NH):
                                g.memset(v65[:, :, h * 65 + 64:h * 65 + 65], 1.0)
                            S.pre(g, "gp")
                            S.dma(g.dma_start(out=wv_sb[:, :, :],
                                              in_=wv.rearrange("(c p) q -> p c q", p=128)), "gp")
                        gp_wv = S.n["gp"]

                        for n in range(NT):
                            @blk.tensor
                            def _(t, n=n):
                                t = _TTrack(t)
                                S.wait(t, "dvx")
                                if n == 0:
                                    t.wait_ge(S.h["gp"], gp_wv)
                                    S.wait(t, "gcx")
                                for cc in range(NCH):
                                    for (a, b) in n512(0, C):
                                        t.matmul(psK[:, a:b], xn2T[:, cc, n * 128:(n + 1) * 128],
                                                 wv_sb[:, cc, a:b], start=(cc == 0), stop=(cc == NCH - 1))
                                S.inc(t.last_inst(), "pe")

                            @blk.vector
                            def _(v, n=n):
                                v = _Chain(v, S, "dvx")
                                S.wait(v, "pe")
                                v.tensor_copy(
                                    v65[:, n, :].rearrange("p (h x) -> p h x", h=NH)[:, :, 0:64],
                                    psK[:, 0:C].rearrange("p (h x) -> p h x", h=NH))

                    with nc.psum_tensor("ps3kv", [128, 2048], F32) as psKV:
                        @blk.tensor
                        def _(t):
                            t = _TTrack(t)
                            S.wait(t, "dvx")
                            for h in range(NH):
                                for n in range(NT):
                                    t.matmul(psKV[0:64, h * 128:h * 128 + 65],
                                             kf[:, n, h * 64:(h + 1) * 64],
                                             v65[:, n, h * 65:h * 65 + 65],
                                             start=(n == 0), stop=(n == NT - 1))
                            for h in range(NH):
                                for n in range(NT):
                                    t.matmul(psKV[64:128, h * 128:h * 128 + 65],
                                             kf[:, n, h * 64:(h + 1) * 64],
                                             v65[:, n, h * 65:h * 65 + 65],
                                             start=(n == 0), stop=(n == NT - 1))
                            S.inc(t.last_inst(), "pe")

                        @blk.vector
                        def _(v):
                            v = _Chain(v, S, "dvx")
                            S.wait(v, "pe")
                            v.tensor_copy(kv_sb[:, :, 0:65],
                                          psKV[:, 0:2048].rearrange("p (h x) -> p h x", h=NH)[:, :, 0:65])

            with ExitStack() as ph2:
                _e2 = ph2.enter_context
                psQ3 = _e2(nc.psum_tensor("ps3q", [128, 1024], F32))
                psL = _e2(nc.psum_tensor("ps3l", [65, 2048], F32))

                qfT = _e2(nc.sbuf_tensor("qfT", [128, NCH, T], BF16))
                with ExitStack() as phW:
                    wqf_sb = phW.enter_context(nc.sbuf_tensor("wqf_sb", [128, NCH, C], BF16))
                    with ExitStack() as phW2:
                        wtf = phW2.enter_context(nc.sbuf_tensor("wtf2", [128, NCH, C], F32))
                        fold_wf(wqT, wqf_sb, wfd, psQ3)

                    for fc in range(NCH):
                        for half in range(2):
                            @blk.tensor
                            def _(t, fc=fc, half=half):
                                t = _TTrack(t)
                                S.wait(t, "dvx")
                                S.wait(t, "acx")
                                for cc in range(NCH):
                                    for (a, b) in n512(0, 1024):
                                        t.matmul(psQ3[:, a:b], wqf_sb[:, cc, fc * 128:(fc + 1) * 128],
                                                 xn2T[:, cc, half * 1024 + a:half * 1024 + b],
                                                 start=(cc == 0), stop=(cc == NCH - 1))
                                S.inc(t.last_inst(), "pe")

                            @blk.scalar
                            def _(sc, fc=fc, half=half):
                                sc = _Chain(sc, S, "acx")
                                S.wait(sc, "pe")
                                sc.activation(out=qfT[:, fc, half * 1024:(half + 1) * 1024],
                                              in_=psQ3[:, :], func=AF.Relu,
                                              bias=bfq_sb[:, fc:fc + 1])

                attn1nT = _e2(nc.sbuf_tensor("attn1nT", [128, NCH, T], BF16))
                aun = lex[0:64, :]
                for h in range(NH):
                    r0 = (h % 2) * 64

                    @blk.tensor
                    def _(t, h=h, r0=r0):
                        t = _TTrack(t)
                        S.wait(t, "acx")
                        S.wait(t, "dvx")
                        for (a, b) in n512(0, T):
                            t.matmul(psL[:, a:b], kv_sb[r0:r0 + 64, h, 0:65],
                                     qfT[r0:r0 + 64, h // 2, a:b], start=True, stop=True)
                        S.inc(t.last_inst(), "pe")

                    @blk.vector
                    def _(v, h=h):
                        v = _Chain(v, S, "dvx")
                        S.wait(v, "pe")
                        v.tensor_scalar(lex[64:65, :], psL[64:65, :], 1e-6, None, OP.max)
                        v.tensor_copy(aun[:, :], psL[0:64, :])

                    @blk.scalar
                    def _(sc, h=h):
                        sc = _Chain(sc, S, "acx")
                        S.wait(sc, "dvx")
                        sc.activation(out=lex[64:65, :], in_=lex[64:65, :], func=AF.Ln)
                        sc.activation(out=lex[64:65, :], in_=lex[64:65, :], func=AF.Exp, scale=-1.0)

                    for half in range(2):
                        @blk.tensor
                        def _(t, h=h, half=half):
                            t = _TTrack(t)
                            S.wait(t, "acx")
                            S.wait(t, "dvx")
                            for (a, b) in n512(half * 1024, (half + 1) * 1024):
                                t.matmul(psQ3[:, a - half * 1024:b - half * 1024],
                                         ones_r64[64:65, :], lex[64:65, a:b], start=True, stop=True)
                            S.inc(t.last_inst(), "pe")

                        @blk.vector
                        def _(v, h=h, half=half):
                            v = _Chain(v, S, "dvx")
                            S.wait(v, "pe")
                            v.tensor_tensor(
                                attn1nT[(h % 2) * 64:(h % 2) * 64 + 64, h // 2,
                                        half * 1024:(half + 1) * 1024],
                                aun[:, half * 1024:(half + 1) * 1024],
                                psQ3[0:64, 0:1024], OP.mult)

                with nc.sbuf_tensor("wo_sb", [128, NCH, C], BF16) as wo_sb:
                    @blk.gpsimd
                    def _(g):
                        S.wait(g, "pe")
                        S.wait(g, "dvx")
                        S.pre(g, "gp")
                        S.dma(g.dma_start(out=wo_sb[:, :, :],
                                          in_=wo.rearrange("(c p) q -> p c q", p=128)), "gp")
                    gp_wo = S.n["gp"]

                    for n in range(NT):
                        @blk.tensor
                        def _(t, n=n):
                            t = _TTrack(t)
                            t.wait_ge(S.h["gp"], gp_wo)
                            S.wait(t, "dvx")
                            for cc in range(NCH):
                                for (a, b) in n512(0, C):
                                    t.matmul(psQ3[:, a:b], attn1nT[:, cc, n * 128:(n + 1) * 128],
                                             wo_sb[:, cc, a:b], start=(cc == 0), stop=(cc == NCH - 1))
                            S.inc(t.last_inst(), "pe")

                        @blk.vector
                        def _(v, n=n):
                            v = _Chain(v, S, "dvx")
                            S.wait(v, "pe")
                            v.tensor_tensor(lex[:, 0:C], psQ3[:, 0:C], bo_b[:, :], OP.add)
                            v.tensor_tensor(x1[:, n, :], x1[:, n, :], lex[:, 0:C], OP.add)

            with ExitStack() as ph:
                _e = ph.enter_context
                g1t = _e(nc.sbuf_tensor("g1t", [4, 1024], F32))
                g2t = _e(nc.sbuf_tensor("g2t", [4, 256], F32))
                wsb = _e(nc.sbuf_tensor("wsb", [128, 256], F32))
                wrT = _e(nc.sbuf_tensor("wrT", [128, 4, IDIM], BF16))
                wiT = _e(nc.sbuf_tensor("wiT", [128, 4, IDIM], BF16))
                winT = _e(nc.sbuf_tensor("winT", [128, 4, IDIM], BF16))
                psW = _e(nc.psum_tensor("ps3w", [128, 512], F32))
                psR = _e(nc.psum_tensor("ps3r", [128, 512], F32))
                psI = _e(nc.psum_tensor("ps3i", [128, 512], F32))

                for (g1s, g2s, a2d, wdst) in ((g1r_R, g2r_R, a2R, wrT), (g1r_I, g2r_I, a2I, wiT)):
                    @blk.sync
                    def _(sy, g1s=g1s, g2s=g2s):
                        sy.wait_ge(S.h["pe"], S.n["pe"])
                        sy.wait_ge(S.h["dvx"], S.n["dvx"])
                        S.pre(sy, "ld")
                        S.dma(sy.dma_start(out=g1t[:, :], in_=g1s), "ld")
                        S.pre(sy, "ld")
                        S.dma(sy.dma_start(out=g2t[:, :], in_=g2s), "ld")
                    ldg = S.n["ld"]

                    for bt in range(8):
                        @blk.tensor
                        def _(t, bt=bt, ldg=ldg):
                            t = _TTrack(t)
                            t.wait_ge(S.h["ld"], ldg)
                            S.wait(t, "st")
                            t.matmul(psW[:, 0:256], g1t[:, bt * 128:(bt + 1) * 128], g2t[:, :],
                                     start=True, stop=True)
                            S.inc(t.last_inst(), "pe")

                        @blk.vector
                        def _(v, bt=bt):
                            v = _Chain(v, S, "dvx")
                            S.wait(v, "pe")
                            v.tensor_copy(wsb[:, :], psW[:, 0:256])

                        @blk.sync
                        def _(sy, bt=bt, a2d=a2d):
                            S.wait(sy, "dvx")
                            for bb in range(4):
                                bglob = bt * 4 + bb
                                S.pre(sy, "st")
                                S.dma(sy.dma_start(
                                    out=a2d.rearrange("(d g) (b e) -> d g b e", g=16, b=32)[:, :, bglob, :],
                                    in_=wsb[bb * 32:(bb + 1) * 32, :]), "st")

                @blk.gpsimd
                def _(g):
                    S.wait(g, "st")
                    S.pre(g, "gp")
                    S.dma(g.dma_start(out=wrT[:, :, :], in_=a2R.rearrange("(c p) q -> p c q", p=128)), "gp")
                    S.pre(g, "gp")
                    S.dma(g.dma_start(out=wiT[:, :, :], in_=a2I.rearrange("(c p) q -> p c q", p=128)), "gp")
                gp_w = S.n["gp"]

                @blk.vector
                def _(v):
                    v = _Chain(v, S, "dvx")
                    v.wait_ge(S.h["gp"], gp_w)
                    for ic in range(4):
                        v.tensor_scalar(winT[:, ic, :], wiT[:, ic, :], -1.0, None, OP.mult)
                dv_w = S.n["dvx"]

                somarks = {}
                for n in range(NT):
                    par = n % 2

                    @blk.tensor
                    def _(t, n=n):
                        t = _TTrack(t)
                        t.wait_ge(S.h["dvx"], dv_w)
                        S.wait(t, "dvx")
                        for ic in range(4):
                            t.matmul(psR[:, 0:512], xn2T[:, ic, n * 128:(n + 1) * 128],
                                     wrT[:, ic, :], start=(ic == 0), stop=False)
                        for ic in range(4):
                            t.matmul(psR[:, 0:512], xn2T[:, 4 + ic, n * 128:(n + 1) * 128],
                                     winT[:, ic, :], start=False, stop=(ic == 3))
                        for ic in range(4):
                            t.matmul(psI[:, 0:512], xn2T[:, ic, n * 128:(n + 1) * 128],
                                     wiT[:, ic, :], start=(ic == 0), stop=False)
                        for ic in range(4):
                            t.matmul(psI[:, 0:512], xn2T[:, 4 + ic, n * 128:(n + 1) * 128],
                                     wrT[:, ic, :], start=False, stop=(ic == 3))
                        S.inc(t.last_inst(), "pe")

                    @blk.vector
                    def _(v, n=n, par=par):
                        v = _Chain(v, S, "dvx")
                        S.wait(v, "pe")
                        if n >= 2:
                            v.wait_ge(S.h[f"so{par}"], somarks[n - 2])
                        v.tensor_tensor(lex[:, 0:512], psR[:, 0:512], bri_b[:, 0:512], OP.add)
                        v.tensor_tensor(x1[:, n, 0:512], x1[:, n, 0:512], lex[:, 0:512], OP.add)
                        v.tensor_tensor(lex[:, 512:1024], psI[:, 0:512], bri_b[:, 512:1024], OP.add)
                        v.tensor_tensor(x1[:, n, 512:1024], x1[:, n, 512:1024], lex[:, 512:1024], OP.add)

                    @blk.sync
                    def _(sy, n=n, par=par):
                        S.wait(sy, "dvx")
                        S.dma(sy.dma_start(out=out.rearrange("(n p) c -> p n c", p=128)[:, n, :],
                                           in_=x1[:, n, :]), f"so{par}")
                        somarks[n] = S.n[f"so{par}"]

            _fin3()



def _stage3_io(nc):
    io = {}
    io["partialT"] = nc.dram_tensor("partialT", [C, T], F32, kind="ExternalInput").ap()
    io["yb_all"] = nc.dram_tensor("yb_all", [E * CAP, C], BF16, kind="ExternalInput").ap()
    io["meta_in"] = nc.dram_tensor("meta_in", [T, 6], F32, kind="ExternalInput").ap()
    io["counts_all"] = nc.dram_tensor("counts_all", [8, E], F32, kind="ExternalInput").ap()
    io["bsel"] = nc.dram_tensor("bsel", [8, 1], F32, kind="ExternalInput").ap()
    io["wqT"] = nc.dram_tensor("wqT", [C, C], F32, kind="ExternalInput").ap()
    io["wkT"] = nc.dram_tensor("wkT", [C, C], F32, kind="ExternalInput").ap()
    io["wv"] = nc.dram_tensor("wv", [C, C], F32, kind="ExternalInput").ap()
    io["wo"] = nc.dram_tensor("wo", [C, C], F32, kind="ExternalInput").ap()
    io["bo"] = nc.dram_tensor("bo", [1, C], F32, kind="ExternalInput").ap()
    io["wf2"] = nc.dram_tensor("wf2", [128, 64], F32, kind="ExternalInput").ap()
    io["bfq_col"] = nc.dram_tensor("bfq_col", [128, 8], F32, kind="ExternalInput").ap()
    io["bf_row"] = nc.dram_tensor("bf_row", [1, C], F32, kind="ExternalInput").ap()
    io["g1r_R"] = nc.dram_tensor("g1r_R", [4, 1024], F32, kind="ExternalInput").ap()
    io["g2r_R"] = nc.dram_tensor("g2r_R", [4, 256], F32, kind="ExternalInput").ap()
    io["g1r_I"] = nc.dram_tensor("g1r_I", [4, 1024], F32, kind="ExternalInput").ap()
    io["g2r_I"] = nc.dram_tensor("g2r_I", [4, 256], F32, kind="ExternalInput").ap()
    io["bri_row"] = nc.dram_tensor("bri_row", [1, C], F32, kind="ExternalInput").ap()
    io["out"] = nc.dram_tensor("out", [T, C], F32, kind="ExternalOutput").ap()
    io["a2R"] = nc.dram_tensor("a2R", [IDIM, IDIM], F32, kind="Internal").ap()
    io["a2I"] = nc.dram_tensor("a2I", [IDIM, IDIM], F32, kind="Internal").ap()
    return io


def build_stage3():
    nc = bass.Bass("TRN2", target_bir_lowering=False, debug=False)
    io = _stage3_io(nc)
    st0 = ExitStack()
    with st0:
        S = Sems(st0, nc, SEM_NAMES)
        with nc.Block() as blk:
            emit_stage3(_PrefNC(nc, "s3_"), blk, S, io)
    return nc


def _prep_stage3_inputs(inputs, s1res, s2res):
    f32 = np.float32
    yb_all = np.concatenate([r["yb"] for r in s2res], axis=0)
    counts_all = np.concatenate([r["counts"] for r in s1res], axis=0).astype(f32)
    Wf = np.asarray(inputs["L1_attn_Wf"], f32)
    bf = np.asarray(inputs["L1_attn_bf"], f32)
    G1R = np.asarray(inputs["L1_ffn_G1_R"], f32)[0]
    G2R = np.asarray(inputs["L1_ffn_G2_R"], f32)
    G1I = np.asarray(inputs["L1_ffn_G1_I"], f32)[0]
    G2I = np.asarray(inputs["L1_ffn_G2_I"], f32)
    com = {
        "yb_all": yb_all,
        "counts_all": counts_all,
        "wqT": np.ascontiguousarray(np.asarray(inputs["L1_attn_Wq"], f32).T),
        "wkT": np.ascontiguousarray(np.asarray(inputs["L1_attn_Wk"], f32).T),
        "wv": np.ascontiguousarray(inputs["L1_attn_Wv"], f32),
        "wo": np.ascontiguousarray(inputs["L1_attn_Wo"], f32),
        "bo": np.ascontiguousarray(np.asarray(inputs["L1_attn_bo"], f32).reshape(1, -1)),
        "wf2": np.ascontiguousarray(np.concatenate([Wf, Wf], axis=0)),
        "bfq_col": np.ascontiguousarray(np.tile(bf, 16).reshape(8, 128).T),
        "bf_row": np.ascontiguousarray(np.tile(bf, 16).reshape(1, -1)),
        "g1r_R": np.ascontiguousarray(G1R.transpose(1, 0, 2).reshape(4, 1024)),
        "g2r_R": np.ascontiguousarray(G2R[:, :, 0, :].transpose(0, 2, 1).reshape(4, 256)),
        "g1r_I": np.ascontiguousarray(G1I.transpose(1, 0, 2).reshape(4, 1024)),
        "g2r_I": np.ascontiguousarray(G2I[:, :, 0, :].transpose(0, 2, 1).reshape(4, 256)),
        "bri_row": np.ascontiguousarray(np.concatenate(
            [np.asarray(inputs["L1_ffn_bias_real"], f32),
             np.asarray(inputs["L1_ffn_bias_imag"], f32)]).reshape(1, -1)),
    }
    maps = []
    for b in range(8):
        m = dict(com)
        m["partialT"] = np.ascontiguousarray(s1res[b]["partial"])
        m["meta_in"] = np.ascontiguousarray(s1res[b]["meta"].astype(f32))
        sel = np.zeros((8, 1), f32)
        sel[b, 0] = 1.0
        m["bsel"] = sel
        maps.append(m)
    return maps


def run_stage3(inputs, s1res, s2res, **kw):
    nc = build_stage3()
    maps = _prep_stage3_inputs(inputs, s1res, s2res)
    return run_bass_kernel_spmd(nc, maps, core_ids=list(range(8)), **kw)



def build_fused():
    nc = bass.Bass("TRN2", target_bir_lowering=False, debug=False)
    E_IN = lambda n, s, d: nc.dram_tensor(n, s, d, kind="ExternalInput").ap()
    INT = lambda n, s, d: nc.dram_tensor(n, s, d, kind="Internal").ap()

    io1 = {
        "xT": E_IN("xT", [C, T], F32), "xtok": E_IN("xtok", [T, C], F32),
        "w_attn": E_IN("w_attn", [C, 3 * C], F32), "w_attnT": E_IN("w_attnT", [3 * C, C], F32),
        "b_attn": E_IN("b_attn", [1, 3 * C], F32), "w_proj": E_IN("w_proj", [C, C], F32),
        "b_proj": E_IN("b_proj", [1, C], F32), "bproj_col": E_IN("bproj_col", [128, NCH], F32),
        "wk_cT": E_IN("wk_cT", [32, 64], F32), "wk_e": E_IN("wk_e", [32, 64], F32),
        "wv_cT": E_IN("wv_cT", [32, 64], F32), "wv_e": E_IN("wv_e", [32, 64], F32),
        "rms_w": E_IN("rms_w", [1, C], F32), "rms_w_col": E_IN("rms_w_col", [128, NCH], F32),
        "battn_col": E_IN("battn_col", [128, 24], F32), "gate_wT": E_IN("gate_wT", [C, E], F32),
        "sw_gate": E_IN("sw_gate", [C, IDIM], F32), "sw_up": E_IN("sw_up", [C, IDIM], F32),
        "sw_down": E_IN("sw_down", [IDIM, C], F32),
    }
    partial_i = INT("fz_partial", [C, T], F32)
    xm_loc = INT("fz_xm_loc", [T, C], BF16)
    meta_loc = INT("fz_meta_loc", [T, 6], F32)
    counts_loc = INT("fz_counts_loc", [1, E], F32)
    io1["partial"] = partial_i
    io1["xm_out"] = xm_loc
    io1["meta"] = meta_loc
    io1["counts"] = counts_loc

    xm_all = INT("fz_xm_all", [B * T, C], BF16)
    meta_all = INT("fz_meta_all", [B * T, 6], F32)
    counts_all8 = INT("fz_counts_all", [8, E], F32)
    yb_loc = INT("fz_yb_loc", [8 * CAP, C], BF16)
    yb_all = INT("fz_yb_all", [E * CAP, C], BF16)

    io2 = {
        "xm_all": xm_all,
        "meta_all": meta_all,
        "counts_all": counts_all8,
        "w13": E_IN("w13", [8, C, 2 * IDIM], F32),
        "w2w": E_IN("w2w", [8, IDIM, C], F32),
        "ebase": E_IN("ebase", [128, 1], F32),
        "yb": yb_loc,
        "disp": INT("fz_disp", [8 * CAP, C], BF16),
    }
    io3 = {
        "partialT": partial_i,
        "yb_all": yb_all,
        "meta_in": meta_loc,
        "counts_all": counts_all8,
        "bsel": E_IN("bsel", [8, 1], F32),
        "wqT": E_IN("wqT", [C, C], F32), "wkT": E_IN("wkT", [C, C], F32),
        "wv": E_IN("wv", [C, C], F32), "wo": E_IN("wo", [C, C], F32),
        "bo": E_IN("bo", [1, C], F32), "wf2": E_IN("wf2", [128, 64], F32),
        "bfq_col": E_IN("bfq_col", [128, 8], F32), "bf_row": E_IN("bf_row", [1, C], F32),
        "g1r_R": E_IN("g1r_R", [4, 1024], F32), "g2r_R": E_IN("g2r_R", [4, 256], F32),
        "g1r_I": E_IN("g1r_I", [4, 1024], F32), "g2r_I": E_IN("g2r_I", [4, 256], F32),
        "bri_row": E_IN("bri_row", [1, C], F32),
        "out": nc.dram_tensor("out", [T, C], F32, kind="ExternalOutput").ap(),
        "a2R": INT("fz_a2R", [IDIM, IDIM], F32),
        "a2I": INT("fz_a2I", [IDIM, IDIM], F32),
    }

    st0 = ExitStack()
    with st0:
        S = Sems(st0, nc, SEM_NAMES)
        with nc.Block() as blk:
            def _gate(val):
                waits = [("cc", val)]
                for nm in ("pe", "dvx", "acx", "gcx", "dvx", "acx", "gcx",
                           "ga0", "ga1", "sp0", "sp1", "sx0", "sx1", "st", "gp",
                           "yb0", "yb1", "so0", "so1"):
                    waits.append((nm, S.n[nm]))
                for sec in (blk.sync, blk.gpsimd, blk.tensor, blk.vector, blk.scalar):
                    @sec
                    def _(e):
                        for (nm, v_) in waits:
                            e.wait_ge(S.h[nm], v_)

            def _cc1():
                @blk.gpsimd
                def _(g):
                    for nm in ("sx0", "sx1", "st"):
                        S.wait(g, nm)
                    S.inc(g.collective_compute(
                        "AllGather", OP.bypass, replica_groups=[list(range(8))],
                        ins=[xm_loc], outs=[xm_all]), "cc", 1)
                    S.inc(g.collective_compute(
                        "AllGather", OP.bypass, replica_groups=[list(range(8))],
                        ins=[meta_loc], outs=[meta_all]), "cc", 1)
                    S.inc(g.collective_compute(
                        "AllGather", OP.bypass, replica_groups=[list(range(8))],
                        ins=[counts_loc], outs=[counts_all8]), "cc", 1)

            emit_stage1(_PrefNC(nc, "s1_"), blk, S, io1, hook_after_A=_cc1)

            _gate(3)
            emit_stage2(_PrefNC(nc, "s2_"), blk, S, io2)

            @blk.gpsimd
            def _(g):
                for nm in ("yb0", "yb1"):
                    S.wait(g, nm)
                S.inc(g.collective_compute(
                    "AllGather", OP.bypass, replica_groups=[list(range(8))],
                    ins=[yb_loc], outs=[yb_all]), "cc", 1)

            _gate(4)
            emit_stage3(_PrefNC(nc, "s3_"), blk, S, io3)
    return nc


def _prep_fused_inputs(inputs):
    f32 = np.float32
    m1 = _prep_stage1_inputs(inputs)
    w13 = np.ascontiguousarray(inputs["L0_ffn_w13"], f32)
    w2w = np.ascontiguousarray(inputs["L0_ffn_w2"], f32)
    dummy1 = [{"partial": np.zeros((C, T), f32), "meta": np.zeros((T, 6), f32),
               "counts": np.zeros((1, E), f32), "xm_out": None} for _ in range(8)]
    dummy2 = [{"yb": np.zeros((1, 1), BF)} for _ in range(8)]
    m3 = _prep_stage3_inputs(inputs, dummy1, dummy2)
    drop3 = {"yb_all", "counts_all", "partialT", "meta_in"}
    maps = []
    for i in range(8):
        m = dict(m1[i])
        m["w13"] = np.ascontiguousarray(w13[8 * i:8 * i + 8])
        m["w2w"] = np.ascontiguousarray(w2w[8 * i:8 * i + 8])
        m["ebase"] = np.full((128, 1), 8.0 * i, f32)
        for k, v in m3[i].items():
            if k not in drop3:
                m[k] = v
        maps.append(m)
    return maps


def run_fused(inputs, **kw):
    nc = build_fused()
    maps = _prep_fused_inputs(inputs)
    return run_bass_kernel_spmd(nc, maps, core_ids=list(range(8)), **kw)


def kernel(**inputs):
    inputs = {k: np.asarray(v) for k, v in inputs.items()}
    res = run_fused(inputs).results
    return np.stack([r["out"] for r in res], axis=0).astype(np.float32)

